# version 04
# speedup vs baseline: 1.0201x; 1.0201x over previous
import numpy as np
import ml_dtypes
import concourse.bass as bass
import concourse.mybir as mybir
from concourse.bass import IndirectOffsetOnAxis
import bass_rust
from concourse.bass_utils import run_bass_kernel_spmd
from contextlib import ExitStack

F32 = mybir.dt.float32
BF16 = mybir.dt.bfloat16
I32 = mybir.dt.int32
AF = mybir.ActivationFunctionType
ALU = mybir.AluOpType
AX = mybir.AxisListType
RG8 = [[0, 1, 2, 3, 4, 5, 6, 7]]

NCORES = 8
Bv = 2
TLOC = 1024
TOK = Bv * TLOC
D = 1024
NKC = D // 128
RC = 12582912.0

class Prog:
    ENGS = ("sync", "tensor", "vector", "scalar", "gpsimd")

    def __init__(self, nc):
        self.nc = nc
        self.ops = {e: [] for e in self.ENGS}
        self.sems = {}
        self.phys = {}
        self.pool = []
        self.pool_sw = []
        self.swdge_names = set()
        self.ctx = ExitStack()

    SEM_TRACE = []

    def _name(self, name, create=True):
        name = self.ALIAS.get(name, name)
        if name not in self.sems:
            if not create:
                return [None, 0, []]
            self.sems[name] = [None, 0, []]
        return self.sems[name]

    def _phys(self, key, allow_pool=False, sw=False):
        if key not in self.phys:
            pl = self.pool_sw if sw else self.pool
            if allow_pool and pl:
                self.phys[key] = pl.pop()
                return self.phys[key]
            Prog.SEM_TRACE.append(key)
            h = self.ctx.enter_context(self.nc.semaphore(key))
            self.phys[key] = [h, 0]
        return self.phys[key]

    AUTO = ("vector", "scalar", "gpsimd")
    KEEP_SYNC = {"q1_l2t", "q2_l2t", "f3_gt", "f3_ut", "f3_dt",
                 "l2wol0", "l2wol1",
                 "f1_wl0", "f1_wl1", "f2_wl0", "f2_wl1", "f3_wl0", "f3_wl1",
                 "f1_w2l0", "f1_w2l1", "f2_w2l0", "f2_w2l1", "f3_w2l0", "f3_w2l1",
                 "a0_sf", "a3_sf"}
    ALIAS = {}
    MERGE = False

    def rec(self, eng, fn, wait=(), inc=()):
        wait = list(wait)
        inc = list(inc)
        if eng in self.AUTO:
            cn = "c" + eng
            wait.append((cn, self.val(cn)))
            inc.append((cn, 1))
        if self.MERGE:
            dma16 = [nm for nm, n in inc if n == 16 and not nm.startswith("@")]
            if (eng == "sync" and dma16
                    and all(nm not in self.KEEP_SYNC for nm in dma16)):
                eng = "gpsimd"
            for name, n in inc:
                if (n == 16 and not name.startswith("@")
                        and name not in self.ALIAS and name not in self.sems):
                    if eng == "gpsimd":
                        self.ALIAS[name] = "wldg"
        wv = []
        seen = set()
        for w in wait:
            nm = w[0]
            s = self._name(nm)
            idx = w[1] if len(w) > 1 else s[1]
            if idx <= 0:
                continue
            assert idx <= s[1], (nm, idx, s[1])
            tgt = s[2][idx - 1]
            if tgt not in seen:
                seen.add(tgt)
                wv.append(tgt)
        for name, n in inc:
            s = self._name(name)
            if n == 16 or name.startswith("@"):
                if eng == "gpsimd":
                    self.swdge_names.add(self.ALIAS.get(name, name))
            if s[0] is None:
                if n == 16 or name.startswith("@"):
                    s[0] = name
                else:
                    s[0] = "wm_" + eng
            assert (s[0] == "wm_" + eng) or n == 16 or name.startswith("@") or s[0] == name, \
                ("mixed-engine sem name", name, s[0], eng)
        merged = {}
        order = []
        for name, n in inc:
            key = self._name(name)[0]
            if key not in merged:
                merged[key] = 0
                order.append(key)
            merged[key] += n
        finals = {}
        for key in order:
            ph = self._phys(key, allow_pool=(eng in ("sync", "gpsimd")
                                             and not key.startswith("@")
                                             and not key.startswith("wm_")),
                            sw=(eng == "gpsimd"))
            ph[1] += merged[key]
            finals[key] = ph[1]
        for name, n in inc:
            s = self._name(name)
            ph = self.phys[s[0]]
            s[1] += 1
            s[2].append((ph[0], finals[s[0]]))
        iv = [(self.phys[key][0], merged[key]) for key in order]
        self.ops[eng].append((fn, wv, iv))

    def val(self, name):
        return self._name(name, create=False)[1]

    def gate(self, waits):
        for e in self.ENGS:
            self.rec(e, lambda en: None, wait=waits)

    def gate_all(self):
        tg = {}
        for name, s in self.sems.items():
            if s[1] > 0 and s[2]:
                ph, v = s[2][-1]
                if ph is not None:
                    tg[id(ph)] = (ph, max(v, tg.get(id(ph), (ph, 0))[1]))
        wv = list(tg.values())
        for e in self.ENGS:
            def run(en, wv=wv):
                for ph, v in wv:
                    en.wait_ge(ph, v)
                return None
            self.ops[e].append((run, [], []))

    def recycle(self):
        for name, s in list(self.sems.items()):
            if s[0] is not None and s[0] == name and not name.startswith("@"):
                ph = self.phys.get(name)
                if ph is not None:
                    if name in self.swdge_names:
                        self.pool_sw.append(ph)
                    else:
                        self.pool.append(ph)
                    del self.phys[name]
                    s[0] = None
                    self.swdge_names.discard(name)

    def emit(self, block):
        for eng in self.ENGS:
            oplist = self.ops[eng]
            def make(oplist):
                def run(e):
                    for opi, (fn, wv, iv) in enumerate(oplist):
                        for h, v in wv:
                            e.wait_ge(h, v)
                        try:
                            inst = fn(e)
                        except Exception:
                            import traceback as _tb
                            print(f"EMITFAIL eng op#{opi} incs={[(h if isinstance(h,str) else getattr(h,'name','?'), n) for h,n in iv]}")
                            _tb.print_exc()
                            raise
                        if inst is None and iv:
                            inst = e.nop()
                        for k, (h, n) in enumerate(iv):
                            if k == 0:
                                inst.then_inc(h, n)
                            else:
                                e.nop().then_inc(h, n)
                return run
            getattr(block, eng)(make(oplist))

def barrier(P, name, engines=("sync", "tensor", "vector", "scalar", "gpsimd")):
    for e in engines:
        P.rec(e, lambda en: en.engine_nop() if hasattr(en, "engine_nop") else en.nop(),
              inc=((name, 1),))
    tot = P.val(name)
    for e in engines:
        P.rec(e, lambda en: None, wait=((name, tot),))

class Arena:
    def __init__(self, nc, ctx):
        self.nc = nc
        self.stacks = [ctx]
        self.scopes = [{}]

    def push(self):
        st = ExitStack()
        self.stacks.append(st)
        self.scopes.append({})

    def pop(self):
        assert len(self.stacks) > 1
        self.stacks.pop().close()
        self.scopes.pop()

    def _find(self, name):
        for sc in reversed(self.scopes):
            if name in sc:
                return sc[name]
        return None

    _serial = 0

    def sb(self, name, shape, dt=F32):
        t = self._find(name)
        if t is None:
            Arena._serial += 1
            t = self.stacks[-1].enter_context(
                self.nc.sbuf_tensor(f"{name}__{Arena._serial}", list(shape), dt))
            self.scopes[-1][name] = t
        assert list(t.shape) == list(shape), (name, t.shape, shape)
        return t

    def ps(self, name, shape, dt=F32):
        t = self._find(name)
        if t is None:
            t = self.stacks[0].enter_context(self.nc.psum_tensor(name, list(shape), dt))
            self.scopes[0][name] = t
        return t


def ptbanks(A, n=4):
    return [A.ps(f"ptb{i}", [128, 512], F32) for i in range(n)]


def emit_transpose_fm_tm(P, A, src, src_col, dst, dst_col, n_tiles, ident, tag,
                         src_rowspan=128, dt_note=None):
    pts = ptbanks(A)
    base = P.val(tag + "tr")
    baseev = P.val(tag + "ev")
    for i in range(n_tiles):
        slot = i % 4
        w = []
        if i >= 4:
            w.append((tag + "ev", baseev + i - 3))
        P.rec("tensor", lambda e, i=i, slot=slot: e.transpose(
            pts[slot][0:src_rowspan, 0:128] if False else pts[slot][:, 0:128],
            src[:, src_col(i): src_col(i) + 128],
            ident[:, :]), wait=w, inc=((tag + "tr", 1),))
        P.rec("vector", lambda e, i=i, slot=slot: e.tensor_copy(
            dst[:, dst_col(i): dst_col(i) + 128], pts[slot][:, 0:128]),
            wait=((tag + "tr", P.val(tag + "tr")),), inc=((tag + "ev", 1),))


def emit_load_x(P, A, x_dram_tm, x_fm_dram, consts):
    xtm = A.sb("xtm_scr", [128, 16 * 1024], F32)
    xfmb = A.sb("xfm_scr", [128, 2048], F32)
    identf = A.sb("identf_sb", [128, 128], F32)
    xr = x_dram_tm.rearrange("(tc p) f -> tc p f", p=128)
    xo = x_fm_dram.rearrange("(kc p) t -> kc p t", p=128)
    for tc in range(16):
        P.rec("sync", lambda e, tc=tc: e.dma_start(out=xtm[:, tc*1024:(tc+1)*1024], in_=xr[tc]),
              inc=(("ld", 16),))
    ldv = P.val("ld")
    P.rec("tensor", lambda e: None, wait=(("ld", ldv), ("init", P.val("init"))))
    pts = ptbanks(A)
    for kc in range(NKC):
        for tc in range(16):
            i = kc * 16 + tc
            slot = i % 4
            w = []
            if i >= 4:
                w.append(("ldxev", i - 3))
            P.rec("tensor", lambda e, tc=tc, kc=kc, slot=slot: e.transpose(
                pts[slot][:, 0:128], xtm[:, tc*1024 + kc*128: tc*1024 + kc*128 + 128],
                identf[:, :]), wait=w, inc=(("ldxtr", 1),))
            P.rec("vector", lambda e, tc=tc, slot=slot: e.tensor_copy(
                xfmb[:, tc*128:(tc+1)*128], pts[slot][:, 0:128]),
                wait=(("ldxtr", P.val("ldxtr")),), inc=(("ldxev", 1),))
        P.rec("sync", lambda e, kc=kc: e.dma_start(out=xo[kc], in_=xfmb[:, :]),
              wait=(("ldxev", P.val("ldxev")),), inc=(("ldxw", 16),))
        P.rec("vector", lambda e: None, wait=(("ldxw", P.val("ldxw")),))
    return None


def emit_store_x(P, A, x_fm_dram, out_dram, consts, gatev):
    xtm = A.sb("xtm_scr", [128, 16 * 1024], F32)
    xfmb = A.sb("xfm_scr", [128, 2048], F32)
    identf = A.sb("identf_sb", [128, 128], F32)
    xi = x_fm_dram.rearrange("(kc p) t -> kc p t", p=128)
    outr = out_dram.rearrange("(tc p) f -> tc p f", p=128)
    pts = ptbanks(A)
    for kc in range(NKC):
        P.rec("sync", lambda e, kc=kc: e.dma_start(out=xfmb[:, :], in_=xi[kc]),
              wait=([gatev] if kc == 0 else [("stxev", P.val("stxev"))]), inc=(("stl", 16),))
        stl = P.val("stl")
        for tc in range(16):
            i = kc * 16 + tc
            slot = i % 4
            w = [("stl", stl)]
            if i >= 4:
                w.append(("stxev", i - 3))
            P.rec("tensor", lambda e, tc=tc, slot=slot: e.transpose(
                pts[slot][:, 0:128], xfmb[:, tc*128:(tc+1)*128], identf[:, :]),
                wait=w, inc=(("stxtr", 1),))
            P.rec("vector", lambda e, tc=tc, kc=kc, slot=slot: e.tensor_copy(
                xtm[:, tc*1024 + kc*128: tc*1024 + kc*128 + 128], pts[slot][:, 0:128]),
                wait=(("stxtr", P.val("stxtr")),), inc=(("stxev", 1),))
    ev = P.val("stxev")
    for tc in range(16):
        P.rec("sync", lambda e, tc=tc: e.dma_start(out=outr[tc], in_=xtm[:, tc*1024:(tc+1)*1024]),
              wait=(("stxev", ev),), inc=(("st", 16),))
    P.rec("sync", lambda e: None, wait=(("st", P.val("st")),))


def emit_ln(P, A, x_dram, consts, tag):
    sq = A.sb("sq_scr", [128, 4 * 512], F32)
    xs = A.sb("xstream", [128, 4 * 512], F32)
    xkc = A.sb("xn_scr", [128, TOK], F32)
    r_mean = A.sb(f"rmean_{tag}", [1, TOK], F32)
    r_rstd = A.sb(f"rrstd_{tag}", [1, TOK], F32)
    r_scr = A.sb(f"rowA", [1, TOK], F32)
    onesc = A.sb("ones_col", [128, 1], F32)
    onesr = A.sb("ones_row", [1, 128], F32)
    bcast = A.sb("bcast2", [128, 3 * TOK], F32)
    xn = A.sb("xn_bf", [128, NKC * TOK], BF16)
    ps_s = A.ps("ps_a", [128, 512], F32)
    ps_q = A.ps("ps_b", [128, 512], F32)
    xr = x_dram.rearrange("(kc p) t -> kc p t", p=128)
    s_ = "ln_"
    SQ, MMQ, MME, XL = s_+"sq", s_+"mmq", s_+"mme", s_+"xl"

    for tt in range(4):
        for kc in range(NKC):
            i = tt * NKC + kc
            slot = i % 4
            w = []
            if i >= 4:
                w.append((MMQ, P.val(MMQ) - 3))
            P.rec("sync", lambda e, kc=kc, tt=tt, slot=slot: e.dma_start(
                out=xs[:, slot*512:(slot+1)*512], in_=xr[kc][:, tt*512:(tt+1)*512]),
                wait=w, inc=((XL + str(slot), 16),))
            P.rec("scalar", lambda e, slot=slot: e.activation(
                out=sq[:, slot*512:(slot+1)*512], in_=xs[:, slot*512:(slot+1)*512],
                func=AF.Square), wait=((XL + str(slot), P.val(XL + str(slot))),), inc=((SQ, 1),))
            P.rec("tensor", lambda e, kc=kc, slot=slot: e.matmul(
                ps_s[0:1, :], onesc[:, :], xs[:, slot*512:(slot+1)*512],
                start=(kc == 0), stop=(kc == NKC - 1)),
                wait=(((MME, P.val(MME)),) if kc == 0 else ()) + ((SQ, P.val(SQ)),))
            P.rec("tensor", lambda e, kc=kc, slot=slot: e.matmul(
                ps_q[0:1, :], onesc[:, :], sq[:, slot*512:(slot+1)*512],
                start=(kc == 0), stop=(kc == NKC - 1)),
                wait=((SQ, P.val(SQ)),), inc=((MMQ, 1),))
        mmq = P.val(MMQ)
        P.rec("vector", lambda e, tt=tt: e.tensor_copy(r_mean[0:1, tt*512:(tt+1)*512], ps_s[0:1, :]),
              wait=((MMQ, mmq),))
        P.rec("vector", lambda e, tt=tt: e.tensor_copy(r_rstd[0:1, tt*512:(tt+1)*512], ps_q[0:1, :]),
              inc=((MME, 1),))
    P.rec("vector", lambda e: e.tensor_scalar_mul(r_mean[0:1, :], r_mean[0:1, :], 1.0 / D),
          wait=((s_+"mme", P.val(s_+"mme")),))
    P.rec("vector", lambda e: e.tensor_scalar_mul(r_rstd[0:1, :], r_rstd[0:1, :], 1.0 / D))
    P.rec("vector", lambda e: e.tensor_mul(r_scr[0:1, :], r_mean[0:1, :], r_mean[0:1, :]))
    P.rec("vector", lambda e: e.tensor_sub(r_rstd[0:1, :], r_rstd[0:1, :], r_scr[0:1, :]),
          inc=((s_+"v", 1),))
    P.rec("scalar", lambda e: e.activation(out=r_scr[0:1, :], in_=r_rstd[0:1, :],
                                           func=AF.Sqrt, bias=A.sb("epscol_sb", [128, 4], F32)[0:1, 0:1]),
          wait=((s_+"v", P.val(s_+"v")),), inc=((s_+"vs", 1),))
    P.rec("vector", lambda e: e.reciprocal(r_rstd[0:1, :], r_scr[0:1, :]),
          wait=((s_+"vs", P.val(s_+"vs")),), inc=((s_+"v", 1),))
    emit_bcast_rows(P, A, [(r_mean, 0), (r_rstd, TOK)], bcast, onesr, s_,
                    extra_wait=((s_+"v", P.val(s_+"v")),))
    bev = P.val("bcev")
    for kc in range(NKC):
        P.rec("sync", lambda e, kc=kc: e.dma_start(out=xkc[:, :], in_=xr[kc]),
              wait=([(s_+"xn", P.val(s_+"xn"))] if kc else [("bcev", bev)]),
              inc=((s_+"kl", 16),))
        P.rec("vector", lambda e, kc=kc: e.tensor_sub(
            xkc[:, :], xkc[:, :], bcast[:, 0:TOK]),
            wait=((s_+"kl", P.val(s_+"kl")),))
        P.rec("vector", lambda e, kc=kc: e.tensor_mul(
            xn[:, kc*TOK:(kc+1)*TOK], xkc[:, :], bcast[:, TOK:2*TOK]),
            inc=((s_+"xn", 1),))
    return xn, r_mean, r_rstd, bcast


def emit_bcast_rows(P, A, rows_and_dstcols, dst, onesr, tag, extra_wait=(), width=TOK):
    psb = A.ps("ps_bc", [128, 512], F32)
    BM, BE = tag + "bmm", "bcev"
    first = True
    for row, dcol in rows_and_dstcols:
        for tt in range(width // 512):
            w = [(BE, P.val(BE))] if not first else (list(extra_wait) + [(BE, P.val(BE))])
            first = False
            P.rec("tensor", lambda e, row=row, tt=tt: e.matmul(
                psb[:, :], onesr[0:1, :], row[0:1, tt*512:(tt+1)*512],
                start=True, stop=True), wait=w, inc=((BM, 1),))
            P.rec("vector", lambda e, dcol=dcol, tt=tt: e.tensor_copy(
                dst[:, dcol + tt*512: dcol + (tt+1)*512], psb[:, :]),
                wait=((BM, P.val(BM)),), inc=((BE, 1),))


def emit_wT_load(P, A, w_dram, R, C, dst, tag, wq_dram, src_is_bf16=False, prewait=()):
    if not src_is_bf16:
        P.rec("gpsimd", lambda e: e.dma_start(out=wq_dram, in_=w_dram),
              wait=prewait, inc=((tag + "c", 16),))
        cv = [(tag + "c", P.val(tag + "c"))]
    else:
        wq_dram = w_dram
        cv = list(prewait)
    for dc in range(C // 128):
        P.rec("sync", lambda e, dc=dc: e.dma_start_transpose(
            out=dst[:, dc*R:(dc+1)*R], in_=wq_dram[0:R, dc*128:(dc+1)*128]),
            wait=cv if dc == 0 else (), inc=((tag + "t", 16),))


def emit_l2_attn(P, A, x_fm, xn, lntag, consts, W, s_dram):
    identb = A.sb("ident_bf", [128, 128], BF16)
    onesr = A.sb("ones_row", [1, 128], F32)
    h2 = A.sb("act2", [128, NKC * TOK], BF16)
    xn_done = ("ln_xn", P.val("ln_xn"))

    wg_b = A.sb("l2_wg", [64, 1024], BF16)
    wb_b = A.sb("l2_wb", [64, 1024], BF16)
    wc_b = A.sb("l2_wc", [128, NKC * 64], BF16)
    wgT = A.sb("l2_wgT", [128, NKC * 64], BF16)
    wbT = A.sb("l2_wbT", [128, NKC * 64], BF16)
    wcT = A.sb("l2_wcT", [64, NKC * 128], BF16)
    cols = A.sb("l2_cols", [128, 16], F32)
    gcols = A.sb("l2_gcols", [128, 24], F32)
    P.rec("sync", lambda e: e.dma_start(out=cols[0:64, 0:1], in_=W["L2_attn_bg"]), inc=(("l2w", 16),))
    P.rec("sync", lambda e: e.dma_start(out=cols[0:64, 1:2], in_=W["L2_attn_A_log"]), inc=(("l2w", 16),))
    P.rec("sync", lambda e: e.dma_start(out=cols[:, 8:16], in_=W["L2_attn_D1"]), inc=(("l2w", 16),))
    P.rec("sync", lambda e: e.dma_start(out=gcols[:, 0:8], in_=W["L2_attn_gamma"]), inc=(("l2w", 16),))
    P.rec("sync", lambda e: e.dma_start(out=gcols[:, 8:16], in_=W["L2_attn_beta"]), inc=(("l2w", 16),))
    P.rec("sync", lambda e: e.dma_start(out=gcols[:, 16:24], in_=W["L2_attn_bout"]), inc=(("l2w", 16),))
    P.rec("gpsimd", lambda e: e.dma_start(out=wg_b[:, :], in_=W["L2_attn_Wg"]), inc=(("l2wg", 16),))
    P.rec("gpsimd", lambda e: e.dma_start(out=wb_b[:, :], in_=W["L2_attn_WB"]), inc=(("l2wg", 16),))
    WCr = W["L2_attn_WC"].rearrange("(kc p) s -> kc p s", p=128)
    for kc in range(NKC):
        P.rec("gpsimd", lambda e, kc=kc: e.dma_start(out=wc_b[:, kc*64:(kc+1)*64], in_=WCr[kc]),
              inc=(("l2wg", 16),))
    l2wv = P.val("l2w"); l2wgv = P.val("l2wg")

    pts = ptbanks(A)
    tr_jobs = []
    for kc in range(NKC):
        tr_jobs.append((lambda kc=kc: (wg_b[0:64, kc*128:(kc+1)*128],),
                        lambda kc=kc: wgT[:, kc*64:(kc+1)*64], (128, 64)))
    for kc in range(NKC):
        tr_jobs.append((lambda kc=kc: (wb_b[0:64, kc*128:(kc+1)*128],),
                        lambda kc=kc: wbT[:, kc*64:(kc+1)*64], (128, 64)))
    for kc in range(NKC):
        tr_jobs.append((lambda kc=kc: (wc_b[:, kc*64:(kc+1)*64],),
                        lambda kc=kc: wcT[0:64, kc*128:(kc+1)*128], (64, 128)))
    for i, (srcf, dstf, (op, of)) in enumerate(tr_jobs):
        slot = i % 4
        w = [("l2w", l2wv), ("l2wg", l2wgv)]
        if i >= 4:
            w.append(("l2te", i - 3))
        P.rec("tensor", lambda e, srcf=srcf, op=op, of=of, slot=slot: e.transpose(
            pts[slot][0:op, 0:of//2].bitcast(BF16), srcf()[0], identb[0:srcf()[0].shape[0], 0:srcf()[0].shape[0]]),
            wait=w, inc=(("l2tr", 1),))
        P.rec("vector", lambda e, dstf=dstf, op=op, of=of, slot=slot: e.tensor_copy(
            dstf(), pts[slot][0:op, 0:of//2].bitcast(BF16)),
            wait=(("l2tr", P.val("l2tr")),), inc=(("l2te", 1),))
    tev = P.val("l2te")

    u2 = A.sb("l2_u", [64, 2 * TLOC], F32)
    states = A.sb("l2_states", [64, 2 * TLOC], F32)
    Arep = A.sb("l2_arep", [64, TLOC], F32)
    Ageo = A.sb("l2_ageo", [64, TLOC], F32)
    zeros_tl = A.sb("sq_scr", [128, 4 * 512], F32)
    gate_s = A.sb("l2_gate", [64, 512], F32)
    P.rec("scalar", lambda e: e.activation(out=cols[0:64, 2:3], in_=cols[0:64, 1:2], func=AF.Exp),
          wait=(("l2w", l2wv), ("l2wg", l2wgv)), inc=(("l2a", 1),))
    P.rec("vector", lambda e: e.memset(Arep[:, :], 1.0), wait=(("l2a", 1),))
    P.rec("vector", lambda e: e.memset(zeros_tl[0:64, 0:TLOC], 0.0))
    P.rec("vector", lambda e: e.tensor_scalar_mul(Arep[:, :], Arep[:, :], cols[0:64, 2:3]),
          inc=(("l2av", 1),))
    P.rec("vector", lambda e: e.tensor_tensor_scan(
        Ageo[:, :], Arep[:, :], zeros_tl[0:64, 0:TLOC], 1.0, ALU.mult, ALU.add),
        wait=(("l2av", 1),), inc=(("l2av", 1),))

    ps_g = A.ps("ps_a", [128, 512], F32)
    ps_B = A.ps("ps_b", [128, 512], F32)
    for b in range(Bv):
        for t2 in range(2):
            col = b * TLOC + t2 * 512
            for kc in range(NKC):
                P.rec("tensor", lambda e, kc=kc, col=col: e.matmul(
                    ps_g[0:64, :], wgT[:, kc*64:(kc+1)*64],
                    xn[:, kc*TOK + col: kc*TOK + col + 512],
                    start=(kc == 0), stop=(kc == NKC - 1)),
                    wait=(((("l2te", tev),) + (xn_done,) + (("l2uev", P.val("l2uev")),))
                          if kc == 0 else ()))
            for kc in range(NKC):
                P.rec("tensor", lambda e, kc=kc, col=col: e.matmul(
                    ps_B[0:64, :], wbT[:, kc*64:(kc+1)*64],
                    xn[:, kc*TOK + col: kc*TOK + col + 512],
                    start=(kc == 0), stop=(kc == NKC - 1)),
                    inc=((("l2mm", 1),) if kc == NKC - 1 else ()))
            P.rec("scalar", lambda e: e.activation(
                out=gate_s[:, :], in_=ps_g[0:64, :], func=AF.Sigmoid, bias=cols[0:64, 0:1]),
                wait=(("l2mm", P.val("l2mm")),), inc=(("l2gt", 1),))
            P.rec("vector", lambda e, col=col: e.tensor_mul(
                u2[:, col:col + 512], gate_s[:, :], ps_B[0:64, :]),
                wait=(("l2gt", P.val("l2gt")),), inc=(("l2uev", 1),))
    for b in range(Bv):
        P.rec("vector", lambda e, b=b: e.tensor_tensor_scan(
            states[:, b*TLOC:(b+1)*TLOC], Arep[:, :], u2[:, b*TLOC:(b+1)*TLOC],
            0.0, ALU.mult, ALU.add),
            wait=(("l2uev", P.val("l2uev")), ("l2av", 2)), inc=(("l2sc", 1),))

    cc_in = P.nc.dram_tensor("l2_cc_in", [128, 1], F32)
    cc_out = P.nc.dram_tensor("l2_cc_out", [1024, 1], F32)
    cc_loc = A.sb("l2_ccloc", [128, 1], F32)
    carr = A.sb("l2_carr", [128, 8], F32)
    prep8 = A.sb("l2_prep8", [128, 8], F32)
    cpref = A.sb("l2_cpref", [128, 8], F32)
    ccol = A.sb("l2_ccol", [128, 1], F32)
    selb = A.sb("l2_sel", [128, 8], F32)
    P.rec("sync", lambda e: e.dma_start(out=selb[:, :], in_=W["coresel"]), inc=(("l2sel", 16),))
    scv = P.val("l2sc")
    P.rec("sync", lambda e: e.dma_start(out=cc_loc[0:64, :], in_=states[:, TLOC-1:TLOC]),
          wait=(("l2sc", scv),), inc=(("l2cc", 16),))
    P.rec("sync", lambda e: e.dma_start(out=cc_loc[64:128, :], in_=states[:, 2*TLOC-1:2*TLOC]),
          inc=(("l2cc", 16),))
    P.rec("sync", lambda e: e.dma_start(out=cc_in.ap(), in_=cc_loc[:, :]),
          wait=(("l2cc", P.val("l2cc")),), inc=(("l2cc", 16),))
    P.rec("gpsimd", lambda e: e.collective_compute(
        "AllGather", ALU.bypass, replica_groups=RG8,
        ins=[cc_in.ap().opt()], outs=[cc_out.ap().opt()]),
        wait=(("l2cc", P.val("l2cc")),), inc=(("@l2ccc", 1),))
    P.rec("gpsimd", lambda e: e.dma_start(
        out=carr[:, :], in_=cc_out.ap().rearrange("(j p) o -> p (j o)", p=128),
        allow_slow_non_contiguous=True),
        wait=(("@l2ccc", P.val("@l2ccc")),), inc=(("l2ccg", 16),))
    P.rec("sync", lambda e: e.dma_start(out=ccol[0:64, :], in_=Ageo[:, TLOC-1:TLOC]),
          wait=(("l2av", 2),), inc=(("l2pc", 16),))
    P.rec("sync", lambda e: e.dma_start(out=ccol[64:128, :], in_=Ageo[:, TLOC-1:TLOC]),
          inc=(("l2pc", 16),))
    P.rec("vector", lambda e: e.memset(prep8[:, :], 1.0), wait=(("l2pc", P.val("l2pc")),))
    P.rec("vector", lambda e: e.tensor_scalar_mul(prep8[:, :], prep8[:, :], ccol[:, 0:1]),
          inc=(("l2cp", 1),))
    P.rec("vector", lambda e: e.tensor_tensor_scan(
        cpref[:, :], prep8[:, :], carr[:, :], 0.0, ALU.mult, ALU.add),
        wait=(("l2ccg", P.val("l2ccg")), ("l2cp", 1)), inc=(("l2cp", 1),))
    P.rec("vector", lambda e: e.tensor_mul(cpref[:, :], cpref[:, :], selb[:, :]),
          wait=(("l2sel", P.val("l2sel")),))
    P.rec("vector", lambda e: e.reduce_sum(ccol[:, 0:1], cpref[:, :], axis=AX.X),
          inc=(("l2cp", 1),))
    ccol2 = A.sb("l2_ccol2", [64, 2], F32)
    P.rec("sync", lambda e: e.dma_start(out=ccol2[:, 0:1], in_=ccol[0:64, 0:1]),
          wait=(("l2cp", P.val("l2cp")),), inc=(("l2cd", 16),))
    P.rec("sync", lambda e: e.dma_start(out=ccol2[:, 1:2], in_=ccol[64:128, 0:1]),
          inc=(("l2cd", 16),))
    for b in range(Bv):
        P.rec("vector", lambda e, b=b: e.scalar_tensor_tensor(
            states[:, b*TLOC:(b+1)*TLOC], Ageo[:, :], ccol2[:, b:b+1],
            states[:, b*TLOC:(b+1)*TLOC], ALU.mult, ALU.add),
            wait=(("l2cd", P.val("l2cd")),), inc=(("l2st", 1),))

    states_bf = A.sb("l2_states_bf", [64, 2 * TLOC], BF16)
    P.rec("vector", lambda e: e.tensor_copy(states_bf[:, :], states[:, :]),
          wait=(("l2st", P.val("l2st")),), inc=(("l2st", 1),))
    stv = P.val("l2st")
    for kc in range(NKC):
        for b in range(Bv):
            for t2 in range(2):
                col = b * TLOC + t2 * 512
                P.rec("tensor", lambda e, kc=kc, b=b, t2=t2: e.matmul(
                    ps_g[:, :], wcT[0:64, kc*128:(kc+1)*128],
                    states_bf[:, b*TLOC + t2*512: b*TLOC + (t2+1)*512],
                    start=True, stop=True),
                    wait=((("l2st", stv), ("l2h2", P.val("l2h2"))) if True else ()),
                    inc=(("l2ymm", 1),))
                P.rec("vector", lambda e, kc=kc, col=col: e.scalar_tensor_tensor(
                    h2[:, kc*TOK + col: kc*TOK + col + 512],
                    xn[:, kc*TOK + col: kc*TOK + col + 512],
                    cols[:, 8+kc:9+kc], ps_g[:, :], ALU.mult, ALU.add),
                    wait=(("l2ymm", P.val("l2ymm")),), inc=(("l2h2", 1),))
    return dict(h2=h2, gcols=gcols, s_dram=s_dram, h2done=("l2h2", P.val("l2h2")))

def emit_ln2_and_wout(P, A, consts, W, l2a):
    h2 = l2a["h2"]; gcols = l2a["gcols"]; s_dram = l2a["s_dram"]
    hn = A.sb("xn_bf", [128, NKC * TOK], BF16)
    sstg = A.sb("sstg", [128, 512], BF16)
    sstg2 = A.sb("sstg2", [128, 512], BF16)
    onesc = A.sb("ones_col", [128, 1], F32)
    onesb = A.sb("ones_col_bf", [128, 1], BF16)
    onesr = A.sb("ones_row", [1, 128], F32)
    sq = A.sb("sq_scr", [128, 4 * 512], F32)
    r_mean = A.sb("rmean_l2b", [1, TOK], F32)
    r_rstd = A.sb("rrstd_l2b", [1, TOK], F32)
    r_scr = A.sb("rowA", [1, TOK], F32)
    bcast = A.sb("bcast2", [128, 3 * TOK], F32)
    xnscr = A.sb("xn_scr", [128, TOK], F32)
    ps_s = A.ps("ps_a", [128, 512], F32)
    ps_q = A.ps("ps_b", [128, 512], F32)
    s_ = "l2b_"
    SQ, MMQ, MME = s_+"sq", s_+"mmq", s_+"mme"
    h2v = l2a["h2done"]
    for tt in range(4):
        for kc in range(NKC):
            i = tt * NKC + kc
            slot = i % 4
            w = [h2v]
            if i >= 4:
                w.append((MMQ, P.val(MMQ) - 3))
            P.rec("scalar", lambda e, kc=kc, tt=tt, slot=slot: e.activation(
                out=sq[:, slot*512:(slot+1)*512],
                in_=h2[:, kc*TOK + tt*512: kc*TOK + (tt+1)*512],
                func=AF.Square), wait=w, inc=((SQ, 1),))
            P.rec("tensor", lambda e, kc=kc, tt=tt: e.matmul(
                ps_s[0:1, :], onesb[:, :],
                h2[:, kc*TOK + tt*512: kc*TOK + (tt+1)*512],
                start=(kc == 0), stop=(kc == NKC - 1)),
                wait=(((MME, P.val(MME)), h2v) if kc == 0 else ()))
            P.rec("tensor", lambda e, kc=kc, tt=tt, slot=slot: e.matmul(
                ps_q[0:1, :], onesc[:, :], sq[:, slot*512:(slot+1)*512],
                start=(kc == 0), stop=(kc == NKC - 1)),
                wait=((SQ, P.val(SQ)),), inc=((MMQ, 1),))
        mmq = P.val(MMQ)
        P.rec("vector", lambda e, tt=tt: e.tensor_copy(r_mean[0:1, tt*512:(tt+1)*512], ps_s[0:1, :]),
              wait=((MMQ, mmq),))
        P.rec("vector", lambda e, tt=tt: e.tensor_copy(r_rstd[0:1, tt*512:(tt+1)*512], ps_q[0:1, :]),
              inc=((MME, 1),))
    P.rec("vector", lambda e: e.tensor_scalar_mul(r_mean[0:1, :], r_mean[0:1, :], 1.0 / D),
          wait=((s_+"mme", P.val(s_+"mme")),))
    P.rec("vector", lambda e: e.tensor_scalar_mul(r_rstd[0:1, :], r_rstd[0:1, :], 1.0 / D))
    P.rec("vector", lambda e: e.tensor_mul(r_scr[0:1, :], r_mean[0:1, :], r_mean[0:1, :]))
    P.rec("vector", lambda e: e.tensor_sub(r_rstd[0:1, :], r_rstd[0:1, :], r_scr[0:1, :]),
          inc=((s_+"v", 1),))
    P.rec("scalar", lambda e: e.activation(out=r_scr[0:1, :], in_=r_rstd[0:1, :],
                                           func=AF.Sqrt, bias=A.sb("epscol_sb", [128, 4], F32)[0:1, 0:1]),
          wait=((s_+"v", 1),), inc=((s_+"vs", 1),))
    P.rec("vector", lambda e: e.reciprocal(r_rstd[0:1, :], r_scr[0:1, :]),
          wait=((s_+"vs", 1),), inc=((s_+"v", 1),))
    emit_bcast_rows(P, A, [(r_mean, 0), (r_rstd, TOK)], bcast, onesr, s_,
                    extra_wait=((s_+"v", 2),))
    bev = P.val("bcev")
    for kc in range(NKC):
        P.rec("vector", lambda e, kc=kc: e.tensor_sub(
            xnscr[:, :], h2[:, kc*TOK:(kc+1)*TOK], bcast[:, 0:TOK]),
            wait=((("bcev", bev),) if kc == 0 else ()))
        P.rec("vector", lambda e, kc=kc: e.tensor_mul(
            xnscr[:, :], xnscr[:, :], bcast[:, TOK:2*TOK]))
        P.rec("vector", lambda e, kc=kc: e.tensor_scalar(
            hn[:, kc*TOK:(kc+1)*TOK], xnscr[:, :],
            gcols[:, 0+kc:1+kc], gcols[:, 8+kc:9+kc], ALU.mult, ALU.add),
            inc=((s_+"hn", 1),))
    hnv = (s_+"hn", P.val(s_+"hn"))
    wowin = A.sb("l2_wowin", [128, 2 * NKC * 128], BF16)
    wq = P.nc.dram_tensor("l2_wout_bf", [1024, 1024], BF16).ap()
    P.rec("gpsimd", lambda e: e.dma_start(out=wq, in_=W["L2_attn_Wout"]), inc=(("l2woc", 16),))
    wcv = ("l2woc", P.val("l2woc"))
    for oc in range(8):
        wb = 0 if oc % 2 == 0 else NKC * 128
        wv = [wcv]
        if oc >= 2:
            wv.append((s_+"smm", P.val(s_+"smm") - 4))
        for kc in range(NKC):
            P.rec("sync", lambda e, oc=oc, kc=kc, wb=wb: e.dma_start_transpose(
                out=wowin[:, wb + kc*128: wb + (kc+1)*128],
                in_=wq[oc*128:(oc+1)*128, kc*128:(kc+1)*128]),
                wait=wv if kc == 0 else (), inc=(("l2wol" + str(oc % 2), 16),))
        wlv = ("l2wol" + str(oc % 2), P.val("l2wol" + str(oc % 2)))
        for tt in range(4):
            P.rec("tensor", lambda e, oc=oc, tt=tt, wb=wb: e.matmul(
                ps_s[:, :], wowin[:, wb + 0*128: wb + 128],
                hn[:, 0*TOK + tt*512: 0*TOK + (tt+1)*512],
                start=True, stop=False),
                wait=((wlv, hnv, (s_+"sev", P.val(s_+"sev")))))
            for kc in range(1, NKC):
                P.rec("tensor", lambda e, oc=oc, tt=tt, kc=kc, wb=wb: e.matmul(
                    ps_s[:, :], wowin[:, wb + kc*128: wb + kc*128 + 128],
                    hn[:, kc*TOK + tt*512: kc*TOK + (tt+1)*512],
                    start=False, stop=(kc == NKC - 1)),
                    inc=((s_+"smm", 1),) if kc == NKC - 1 else ())
            par = (oc * 4 + tt) % 2
            stg = sstg if par == 0 else sstg2
            swp = s_ + "sw" + str(par)
            P.rec("scalar", lambda e, oc=oc, tt=tt, stg=stg: e.activation(
                out=stg[:, :], in_=ps_s[:, :], func=AF.Identity, bias=gcols[:, 16+oc:17+oc]),
                wait=((s_+"smm", P.val(s_+"smm")), (swp, P.val(swp))),
                inc=((s_+"sev", 1),))
            P.rec("sync", lambda e, oc=oc, tt=tt, stg=stg: e.dma_start(
                out=s_dram[oc*128:(oc+1)*128, tt*512:(tt+1)*512], in_=stg[:, :]),
                wait=((s_+"sev", P.val(s_+"sev")),), inc=((swp, 16),))
    return dict(sdone=(s_+"sw0", P.val(s_+"sw0")), sdone2=(s_+"sw1", P.val(s_+"sw1")))

def emit_absmax_tree(P, A, acc, width, tag):
    sh = A.sb("sq_scr", [128, 4 * 512], F32)
    sz = 64
    while sz >= 1:
        P.rec("gpsimd", lambda e, sz=sz: e.dma_start(out=sh[0:sz, 0:width], in_=acc[sz:2*sz, 0:width]),
              wait=((tag, P.val(tag)),), inc=((tag + "d", 16),))
        P.rec("vector", lambda e, sz=sz: e.tensor_max(acc[0:sz, 0:width], acc[0:sz, 0:width],
                                                      sh[0:sz, 0:width]),
              wait=((tag + "d", P.val(tag + "d")),), inc=((tag, 1),))
        sz //= 2


def emit_wquant_mean(P, A, w_dram, R, C, wq_dram, tag, prewait=()):
    slab = A.sb("wslab", [128, 1024], F32)
    slab2 = A.sb("wslab2", [128, 1024], F32)
    slabb = A.sb("wslabb", [128, 1024], BF16)
    slabb2 = A.sb("wslabb2", [128, 1024], BF16)
    accc = A.sb("wacc", [128, 1], F32)
    redc = A.sb("wred", [128, 1], F32)
    scrow = A.sb("scrow_" + tag, [1, 4], F32)
    wsB = A.sb("wsB_" + tag, [128, 2], F32)
    onesc = A.sb("ones_col", [128, 1], F32)
    onesr = A.sb("ones_row", [1, 128], F32)
    ps = A.ps("ps_bc", [128, 512], F32)
    nslab = R // 128
    wr = w_dram.rearrange("(n p) c -> n p c", p=128)
    wqr = wq_dram.rearrange("(n p) c -> n p c", p=128)
    T = tag
    CW = min(C, 1024)
    ncc = C // CW
    P.rec("vector", lambda e: e.memset(accc[:, :], 0.0), wait=prewait, inc=((T+"v", 1),))
    for n in range(nslab):
        for cc in range(ncc):
            j = n * ncc + cc
            buf = slab if j % 2 == 0 else slab2
            w = [(T+"v", P.val(T+"v") - 1)] if j >= 2 else ()
            lp = T + "l" + str(j % 2)
            P.rec("sync", lambda e, n=n, cc=cc, buf=buf: e.dma_start(
                out=buf[:, 0:CW], in_=wr[n][:, cc*CW:(cc+1)*CW]), wait=w, inc=((lp, 16),))
            P.rec("vector", lambda e, buf=buf: e.reduce_sum(
                redc[:, 0:1], buf[:, 0:CW], axis=AX.X, apply_absolute_value=True),
                wait=((lp, P.val(lp)),))
            P.rec("vector", lambda e: e.tensor_add(accc[:, 0:1], accc[:, 0:1], redc[:, 0:1]),
                inc=((T+"v", 1),))
    P.rec("tensor", lambda e: e.matmul(ps[0:1, 0:1], onesc[:, :], accc[:, 0:1],
                                       start=True, stop=True),
          wait=((T+"v", P.val(T+"v")),), inc=((T+"mm", 1),))
    P.rec("vector", lambda e: e.tensor_copy(scrow[0:1, 0:1], ps[0:1, 0:1]),
          wait=((T+"mm", P.val(T+"mm")),))
    P.rec("vector", lambda e: e.tensor_scalar_mul(scrow[0:1, 1:2], scrow[0:1, 0:1], 1.0 / (R * C)))
    P.rec("vector", lambda e: e.tensor_scalar_max(scrow[0:1, 1:2], scrow[0:1, 1:2], 1e-5))
    P.rec("vector", lambda e: e.reciprocal(scrow[0:1, 2:3], scrow[0:1, 1:2]),
          inc=((T+"v", 1),))
    for j in (1, 2):
        P.rec("tensor", lambda e, j=j: e.matmul(ps[:, 0:1], onesr[0:1, :], scrow[0:1, j:j+1],
                                                start=True, stop=True),
              wait=((T+"v", P.val(T+"v")), ("bcev", P.val("bcev"))), inc=((T+"bmm", 1),))
        P.rec("vector", lambda e, j=j: e.tensor_copy(wsB[:, j-1:j], ps[:, 0:1]),
              wait=((T+"bmm", P.val(T+"bmm")),), inc=(("bcev", 1),))
    bev = ("bcev", P.val("bcev"))
    for n in range(nslab):
        for cc in range(ncc):
            j = n * ncc + cc
            buf = slab if j % 2 == 0 else slab2
            w = [bev] if j < 2 else [(T+"q", P.val(T+"q") - 1)]
            lp = T + "lq" + str(j % 2)
            P.rec("sync", lambda e, n=n, cc=cc, buf=buf: e.dma_start(
                out=buf[:, 0:CW], in_=wr[n][:, cc*CW:(cc+1)*CW]), wait=w, inc=((lp, 16),))
            epsc_ = A.sb("epscol_sb", [128, 4], F32)
            P.rec("scalar", lambda e, buf=buf, epsc_=epsc_: e.activation(
                out=buf[:, 0:CW], in_=buf[:, 0:CW], func=AF.Identity,
                bias=epsc_[:, 1:2], scale=wsB[:, 1:2]),
                wait=((lp, P.val(lp)),), inc=((T+"s", 1),))
            P.rec("vector", lambda e, buf=buf: e.tensor_scalar(
                buf[:, 0:CW], buf[:, 0:CW], -RC, 1.0, ALU.add, ALU.min),
                wait=((T+"s", P.val(T+"s")),))
            sbb = slabb if j % 2 == 0 else slabb2
            wp = T + "w" + str(j % 2)
            P.rec("vector", lambda e, buf=buf, sbb=sbb: e.tensor_scalar(
                sbb[:, 0:CW], buf[:, 0:CW], -1.0, None, ALU.max),
                wait=(((wp, P.val(wp)),) if j >= 2 else ()),
                inc=((T+"q", 1),))
            P.rec("gpsimd", lambda e, n=n, cc=cc, sbb=sbb: e.dma_start(
                out=wqr[n][:, cc*CW:(cc+1)*CW], in_=sbb[:, 0:CW]),
                wait=((T+"q", P.val(T+"q")),), inc=((wp, 16),))
    return ((T+"w0", P.val(T+"w0")), (T+"w1", P.val(T+"w1")))

def emit_l2_ffn(P, A, x_dram, r_mean, r_rstd, consts, W, DR, g_done_sems):
    onesc = A.sb("ones_col", [128, 1], F32)
    onesr = A.sb("ones_row", [1, 128], F32)
    bcast = A.sb("bcast2", [128, 3 * TOK], F32)
    xnscr = A.sb("xn_scr", [128, TOK], F32)
    acc = A.sb("colmax", [128, TOK], F32)
    rowA = A.sb("l2f_rowA", [1, TOK], F32)
    rowB = A.sb("l2f_rowB", [1, TOK], F32)
    rowC = A.sb("l2f_rowC", [1, TOK], F32)
    bcols = A.sb("l2f_bcols", [128, 40], F32)
    P.rec("sync", lambda e: e.dma_start(out=bcols[:, 0:32], in_=W["L2_ffn_b1"]), inc=(("f2w", 16),))
    P.rec("sync", lambda e: e.dma_start(out=bcols[:, 32:40], in_=W["L2_ffn_b2"]), inc=(("f2w", 16),))

    w1q_done = emit_wquant_mean(P, A, W["L2_ffn_w1"], 4096, 1024, DR["w1q"], "q1_l2")
    wsB1 = A.sb("wsB_q1_l2", [128, 2], F32)
    w1win = A.sb("w1win", [128, 2 * 8 * 512], BF16)

    s_ = "f2_"
    lnb = ("l2abv", P.val("l2abv"))
    emit_bcast_rows(P, A, [(r_mean, 0)], bcast, onesr, s_ + "m")
    mb = ("bcev", P.val("bcev"))
    xr = x_dram.rearrange("(kc p) t -> kc p t", p=128)
    P.rec("vector", lambda e: e.memset(acc[:, :], 0.0), inc=((s_ + "am", 1),))
    for kc in range(NKC):
        P.rec("sync", lambda e, kc=kc: e.dma_start(out=xnscr[:, :], in_=xr[kc]),
              wait=([mb] + ([(s_ + "am", P.val(s_ + "am"))] if kc else [])),
              inc=((s_ + "xl", 16),))
        P.rec("vector", lambda e, kc=kc: e.tensor_sub(
            xnscr[:, :], xnscr[:, :], bcast[:, 0:TOK]),
            wait=((s_ + "xl", P.val(s_ + "xl")),))
        P.rec("vector", lambda e: e.scalar_tensor_tensor(
            acc[:, :], xnscr[:, :], 1.0, acc[:, :], ALU.mult, ALU.max))
        P.rec("vector", lambda e: e.scalar_tensor_tensor(
            acc[:, :], xnscr[:, :], -1.0, acc[:, :], ALU.mult, ALU.max),
            inc=((s_ + "am", 1),))
    emit_absmax_tree(P, A, acc, TOK, s_ + "am")
    amv = (s_ + "am", P.val(s_ + "am"))
    P.rec("vector", lambda e: e.tensor_mul(rowA[0:1, :], r_rstd[0:1, :], r_rstd[0:1, :]),
          wait=(amv,))
    P.rec("vector", lambda e: e.tensor_scalar(rowA[0:1, :], rowA[0:1, :], -1e-5, 1.0 + 1e-6,
                                              ALU.mult, ALU.add), inc=((s_ + "r", 1),))
    P.rec("scalar", lambda e: e.activation(out=rowB[0:1, :], in_=rowA[0:1, :], func=AF.Sqrt),
          wait=((s_ + "r", 1),), inc=((s_ + "rs", 1),))
    P.rec("vector", lambda e: e.reciprocal(rowA[0:1, :], rowB[0:1, :]),
          wait=((s_ + "rs", 1),), inc=((s_ + "r", 1),))
    P.rec("vector", lambda e: e.tensor_mul(rowB[0:1, :], r_rstd[0:1, :], rowA[0:1, :]))
    P.rec("vector", lambda e: e.tensor_mul(rowC[0:1, :], rowB[0:1, :], acc[0:1, :]))
    P.rec("vector", lambda e: e.tensor_scalar_max(rowC[0:1, :], rowC[0:1, :], 1e-5),
          inc=((s_ + "r", 1),))
    P.rec("vector", lambda e: e.reciprocal(rowA[0:1, :], rowC[0:1, :]), inc=((s_ + "r", 1),))
    P.rec("vector", lambda e: e.tensor_scalar_mul(rowA[0:1, :], rowA[0:1, :], 127.0))
    P.rec("vector", lambda e: e.tensor_mul(rowB[0:1, :], rowB[0:1, :], rowA[0:1, :]))
    P.rec("vector", lambda e: e.tensor_scalar_mul(rowC[0:1, :], rowC[0:1, :], 1.0 / 127.0),
          inc=((s_ + "r", 1),))
    emit_bcast_rows(P, A, [(rowB, TOK), (rowC, 2 * TOK)], bcast, onesr, s_ + "c",
                    extra_wait=((s_ + "r", P.val(s_ + "r")),))
    cb = ("bcev", P.val("bcev"))
    xq = A.sb("xn_bf", [128, NKC * TOK], BF16)
    for kc in range(NKC):
        P.rec("sync", lambda e, kc=kc: e.dma_start(out=xnscr[:, :], in_=xr[kc]),
              wait=([cb] + ([(s_ + "xq", P.val(s_ + "xq"))] if kc else [])),
              inc=((s_ + "xl", 16),))
        P.rec("vector", lambda e, kc=kc: e.tensor_sub(
            xnscr[:, :], xnscr[:, :], bcast[:, 0:TOK]),
            wait=((s_ + "xl", P.val(s_ + "xl")),))
        P.rec("vector", lambda e: e.tensor_mul(xnscr[:, :], xnscr[:, :], bcast[:, TOK:2*TOK]))
        P.rec("vector", lambda e: e.tensor_scalar(
            xnscr[:, :], xnscr[:, :], RC, RC, ALU.add, ALU.subtract))
        P.rec("vector", lambda e, kc=kc: e.tensor_scalar(
            xq[:, kc*TOK:(kc+1)*TOK], xnscr[:, :], 127.0, -128.0, ALU.min, ALU.max),
            inc=((s_ + "xq", 1),))
    xqv = (s_ + "xq", P.val(s_ + "xq"))

    ps_h = A.ps("ps_a", [128, 512], F32)
    ps_h2 = A.ps("ps_b", [128, 512], F32)
    ps_ss = A.ps("ps_c", [128, 512], F32)
    htile = A.sb("htile", [128, 512], F32)
    htile2 = A.sb("htile2", [128, 512], F32)
    hsq = A.sb("hsq", [128, 512], F32)
    hrow = A.sb("l2f_hrow", [1, TOK], F32)
    P.rec("vector", lambda e: e.memset(acc[:, :], 0.0), wait=(amv,), inc=((s_ + "hm", 1),))
    hdr = DR["h_dram"]
    P.rec("vector", lambda e: e.memset(hrow[0:1, :], 0.0), inc=((s_ + "hre", 1),))
    w1qd0, w1qd1 = w1q_done
    for hs in range(8):
        wb = 0 if hs % 2 == 0 else 8 * 512
        for dc in range(NKC):
            wv = [w1qd0, w1qd1]
            if hs >= 2:
                wv.append((s_ + "hmm", P.val(s_ + "hmm") - 16))
            P.rec("sync", lambda e, hs=hs, dc=dc, wb=wb: e.dma_start_transpose(
                out=w1win[:, wb + dc*512: wb + (dc+1)*512],
                in_=DR["w1q"][hs*512:(hs+1)*512, dc*128:(dc+1)*128]),
                wait=wv if dc == 0 else (), inc=((s_ + "wl" + str(hs % 2), 16),))
        wlv = (s_ + "wl" + str(hs % 2), P.val(s_ + "wl" + str(hs % 2)))
        for tt in range(4):
            for oc2 in range(4):
                oc = hs * 4 + oc2
                i = hs * 16 + tt * 4 + oc2
                psh = ps_h if i % 2 == 0 else ps_h2
                ht = htile if i % 2 == 0 else htile2
                for kc in range(NKC):
                    wv = []
                    if kc == 0:
                        wv = [wlv, xqv, (s_ + "hev", P.val(s_ + "hev") - 1)]
                    P.rec("tensor", lambda e, oc2=oc2, tt=tt, kc=kc, psh=psh, wb=wb: e.matmul(
                        psh[:, :], w1win[:, wb + kc*512 + oc2*128: wb + kc*512 + oc2*128 + 128],
                        xq[:, kc*TOK + tt*512: kc*TOK + (tt+1)*512],
                        start=(kc == 0), stop=(kc == NKC - 1)),
                        wait=wv, inc=((s_ + "hmm", 1),) if kc == NKC - 1 else ())
                hwp = s_ + "hw" + str(i % 2)
                P.rec("vector", lambda e, tt=tt, psh=psh, ht=ht: e.tensor_mul(
                    ht[:, :], psh[:, :], bcast[:, 2*TOK + tt*512: 2*TOK + (tt+1)*512]),
                    wait=((s_ + "hmm", P.val(s_ + "hmm")), (hwp, P.val(hwp))),
                    inc=((s_ + "hev", 1),))
                P.rec("scalar", lambda e, oc=oc, ht=ht: e.activation(
                    out=ht[:, :], in_=ht[:, :], func=AF.Gelu,
                    bias=bcols[:, oc:oc+1], scale=wsB1[:, 0:1]),
                    wait=((s_ + "hev", P.val(s_ + "hev")), ("f2w", P.val("f2w"))),
                    inc=((s_ + "hg", 1),))
                P.rec("vector", lambda e, tt=tt, ht=ht: e.scalar_tensor_tensor(
                    acc[:, tt*512:(tt+1)*512], ht[:, :], 1.0, acc[:, tt*512:(tt+1)*512],
                    ALU.mult, ALU.max), wait=((s_ + "hg", P.val(s_ + "hg")), (s_ + "hm", 1)))
                P.rec("vector", lambda e, tt=tt, ht=ht: e.scalar_tensor_tensor(
                    acc[:, tt*512:(tt+1)*512], ht[:, :], -1.0, acc[:, tt*512:(tt+1)*512],
                    ALU.mult, ALU.max))
                P.rec("scalar", lambda e, ht=ht: e.activation(out=hsq[:, :], in_=ht[:, :],
                                                              func=AF.Square),
                      wait=((s_ + "ssa", P.val(s_ + "ssa")),), inc=((s_ + "hs", 1),))
                P.rec("tensor", lambda e, oc2=oc2: e.matmul(
                    ps_ss[0:1, :], onesc[:, :], hsq[:, :],
                    start=(oc2 == 0), stop=(oc2 == 3)),
                    wait=(((s_ + "hs", P.val(s_ + "hs")),) + (((s_ + "sse", P.val(s_ + "sse")),) if oc2 == 0 else ())),
                    inc=(((s_ + "ssa", 1), (s_ + "ssm", 1)) if oc2 == 3 else ((s_ + "ssa", 1),)))
                P.rec("gpsimd", lambda e, tt=tt, oc=oc, ht=ht: e.dma_start(
                    out=hdr[oc*128:(oc+1)*128, tt*512:(tt+1)*512], in_=ht[:, :]),
                    wait=((s_ + "hg", P.val(s_ + "hg")),),
                    inc=((s_ + "hw" + str(i % 2), 16),))
            P.rec("vector", lambda e, tt=tt: e.scalar_tensor_tensor(
                hrow[0:1, tt*512:(tt+1)*512], ps_ss[0:1, :], 1.0,
                hrow[0:1, tt*512:(tt+1)*512], ALU.mult, ALU.add),
                wait=((s_ + "ssm", P.val(s_ + "ssm")), (s_ + "hre", 1)), inc=((s_ + "sse", 1),))
    P.rec("vector", lambda e: e.engine_nop(), wait=((s_ + "hg", P.val(s_ + "hg")),),
          inc=((s_ + "hm", 1),))
    emit_absmax_tree(P, A, acc, TOK, s_ + "hm")
    hmv = (s_ + "hm", P.val(s_ + "hm"))
    P.rec("vector", lambda e: e.tensor_scalar(rowA[0:1, :], hrow[0:1, :], 1.0 / 4096, 1e-6,
                                              ALU.mult, ALU.add),
          wait=(hmv, (s_ + "sse", 32)), inc=((s_ + "r", 1),))
    P.rec("scalar", lambda e: e.activation(out=rowB[0:1, :], in_=rowA[0:1, :], func=AF.Sqrt),
          wait=((s_ + "r", P.val(s_ + "r")),), inc=((s_ + "rs", 1),))
    P.rec("vector", lambda e: e.reciprocal(rowA[0:1, :], rowB[0:1, :]),
          wait=((s_ + "rs", P.val(s_ + "rs")),), inc=((s_ + "r", 1),))
    P.rec("vector", lambda e: e.tensor_mul(rowB[0:1, :], rowA[0:1, :], acc[0:1, :]))
    P.rec("vector", lambda e: e.tensor_scalar_max(rowB[0:1, :], rowB[0:1, :], 1e-5))
    P.rec("vector", lambda e: e.tensor_scalar_mul(rowC[0:1, :], rowB[0:1, :], 1.0 / 127.0),
          inc=((s_ + "r", 1),))
    P.rec("vector", lambda e: e.reciprocal(rowB[0:1, :], rowB[0:1, :]), inc=((s_ + "r", 1),))
    P.rec("vector", lambda e: e.tensor_scalar_mul(rowB[0:1, :], rowB[0:1, :], 127.0),
          inc=((s_ + "r", 1),))
    P.rec("vector", lambda e: e.tensor_mul(rowB[0:1, :], rowB[0:1, :], rowA[0:1, :]),
          inc=((s_ + "r", 1),))
    emit_bcast_rows(P, A, [(rowB, TOK), (rowC, 2 * TOK)], bcast, onesr, s_ + "d",
                    extra_wait=((s_ + "r", P.val(s_ + "r")),))
    db = ("bcev", P.val("bcev"))

    w2q_done = emit_wquant_mean(P, A, W["L2_ffn_w2"], 1024, 4096, DR["w2q"], "q2_l2",
                                prewait=((s_ + "hmm", P.val(s_ + "hmm")),))
    wsB2 = A.sb("wsB_q2_l2", [128, 2], F32)
    w2win = A.sb("w1win", [128, 2 * 8 * 512], BF16)

    xq2 = A.sb("xn_bf", [128, NKC * TOK], BF16)
    gdr = DR["g_dram"]
    pbanks = [A.ps(t, [128, 512], F32) for t in
              ("ptb0", "ptb1", "ptb2", "ptb3", "ps_a", "ps_b", "ps_bc", "ps_c")]
    for tt in range(4):
        for hc in range(32):
            ht = htile if hc % 2 == 0 else htile2
            wv = [(s_ + "hw0", P.val(s_ + "hw0")), (s_ + "hw1", P.val(s_ + "hw1"))]
            if hc < 2:
                wv.append((s_ + "gmm", P.val(s_ + "gmm")))
                wv.append((s_ + "gs", P.val(s_ + "gs")))
            else:
                wv.append((s_ + "q2", P.val(s_ + "q2") - 1))
            hlp = s_ + "hl" + str(hc % 2)
            P.rec("sync", lambda e, tt=tt, hc=hc, ht=ht: e.dma_start(
                out=ht[:, :], in_=hdr[hc*128:(hc+1)*128, tt*512:(tt+1)*512]),
                wait=wv, inc=((hlp, 16),))
            P.rec("vector", lambda e, tt=tt, ht=ht: e.tensor_mul(
                ht[:, :], ht[:, :], bcast[:, TOK + tt*512: TOK + (tt+1)*512]),
                wait=((hlp, P.val(hlp)), db))
            P.rec("vector", lambda e, ht=ht: e.tensor_scalar(
                ht[:, :], ht[:, :], RC, RC, ALU.add, ALU.subtract))
            P.rec("vector", lambda e, hc=hc, ht=ht: e.tensor_scalar(
                xq2[:, hc*512:(hc+1)*512], ht[:, :], 127.0, -128.0, ALU.min, ALU.max),
                inc=((s_ + "q2", 1),))
        q2v = (s_ + "q2", P.val(s_ + "q2"))
        for hw_ in range(8):
            wb = 0 if hw_ % 2 == 0 else 8 * 512
            wv = list(w2q_done)
            if hw_ >= 2:
                wv.append((s_ + "gwin", P.val(s_ + "gwin") - 1))
            for hh in range(4):
                hc = hw_ * 4 + hh
                P.rec("sync", lambda e, hc=hc, hh=hh, wb=wb: e.dma_start_transpose(
                    out=w2win[:, wb + hh*1024: wb + (hh+1)*1024],
                    in_=DR["w2q"][0:1024, hc*128:(hc+1)*128]),
                    wait=wv if hh == 0 else (), inc=((s_ + "w2l" + str(hw_ % 2), 16),))
            w2lv = (s_ + "w2l" + str(hw_ % 2), P.val(s_ + "w2l" + str(hw_ % 2)))
            for hh in range(4):
                hc = hw_ * 4 + hh
                for oc in range(8):
                    wv2 = []
                    if hh == 0 and oc == 0:
                        wv2 = [w2lv, q2v, (s_ + "pse", P.val(s_ + "pse"))]
                    elif hh == 0:
                        wv2 = [w2lv]
                    inc = []
                    if hc == 31:
                        inc.append((s_ + "gmm", 1))
                    if hh == 3 and oc == 7:
                        inc.append((s_ + "gwin", 1))
                    P.rec("tensor", lambda e, oc=oc, hc=hc, hh=hh, wb=wb: e.matmul(
                        pbanks[oc][:, :], w2win[:, wb + hh*1024 + oc*128: wb + hh*1024 + oc*128 + 128],
                        xq2[:, hc*512:(hc+1)*512],
                        start=(hc == 0), stop=(hc == 31)),
                        wait=wv2, inc=inc)
        gmmv = (s_ + "gmm", P.val(s_ + "gmm"))
        for oc in range(8):
            P.rec("vector", lambda e, tt=tt, oc=oc: e.tensor_mul(
                htile[:, :], pbanks[oc][:, :], bcast[:, 2*TOK + tt*512: 2*TOK + (tt+1)*512]),
                wait=(gmmv, (s_ + "gs", P.val(s_ + "gs"))), inc=((s_ + "pse", 1),))
            P.rec("scalar", lambda e, oc=oc: e.activation(
                out=hsq[:, :], in_=htile[:, :], func=AF.Silu,
                bias=bcols[:, 32+oc:33+oc], scale=wsB2[:, 0:1]),
                wait=((s_ + "gw", P.val(s_ + "gw")), ("f2w", P.val("f2w")),
                      (s_ + "pse", P.val(s_ + "pse"))),
                inc=((s_ + "gs", 1),))
            P.rec("gpsimd", lambda e, oc=oc, tt=tt: e.dma_start(
                out=gdr[oc*128:(oc+1)*128, tt*512:(tt+1)*512], in_=hsq[:, :]),
                wait=((s_ + "gs", P.val(s_ + "gs")),),
                inc=((s_ + "gw", 16),))
    return (s_ + "gw", P.val(s_ + "gw"))

def emit_combine(P, A, x_dram, s_dram, g_dram, tag, swait, gwait, extra=()):
    sbuf1 = A.sb("cmb_s", [128, TOK], BF16)
    sbuf2 = A.sb("cmb_g", [128, TOK], BF16)
    scr = A.sb("xn_scr", [128, TOK], F32)
    xb = A.sb("xstream", [128, 4 * 512], F32)
    T = "cb" + tag
    xr = x_dram.rearrange("(kc p) t -> kc p t", p=128)
    sr = s_dram.rearrange("(kc p) t -> kc p t", p=128)
    gr = g_dram.rearrange("(kc p) t -> kc p t", p=128)
    for kc in range(NKC):
        w = ([swait, gwait] + list(extra)) if kc == 0 else [(T + "w", P.val(T + "w"))]
        P.rec("sync", lambda e, kc=kc: e.dma_start(out=sbuf1[:, :], in_=sr[kc]),
              wait=w, inc=((T + "l", 16),))
        P.rec("sync", lambda e, kc=kc: e.dma_start(out=sbuf2[:, :], in_=gr[kc]),
              inc=((T + "l", 16),))
        P.rec("sync", lambda e, kc=kc: e.dma_start(out=xb[:, :], in_=xr[kc]),
              inc=((T + "l", 16),))
        P.rec("vector", lambda e: e.tensor_mul(scr[:, :], sbuf1[:, :], sbuf2[:, :]),
              wait=((T + "l", P.val(T + "l")),))
        P.rec("vector", lambda e, kc=kc: e.tensor_add(xb[:, :], xb[:, :], scr[:, :]),
              inc=((T + "u", 1),))
        P.rec("sync", lambda e, kc=kc: e.dma_start(out=xr[kc], in_=xb[:, :]),
              wait=((T + "u", P.val(T + "u")),), inc=((T + "w", 16),))
    P.rec("sync", lambda e: None, wait=((T + "w", P.val(T + "w")),))
    return (T + "w", P.val(T + "w"))



def build_program(layers=(2,), n_final_out=True):
    nc = bass.Bass("TRN2", target_bir_lowering=False)
    W = {}

    def din(name, shape):
        W[name] = nc.dram_tensor(name, list(shape), F32, kind="ExternalInput").ap()

    din("x_in", [TOK, D])
    din("ident_f32", [128, 128])
    din("ident_bf_src", [128, 128])
    din("ones_fill", [128, 1])
    din("ones_rowsrc", [1, 128])
    din("epscol", [128, 4])
    din("coresel", [128, 8])
    x_out = nc.dram_tensor("x_out", [TOK, D], F32, kind="ExternalOutput").ap()

    if "1f" in layers or "L1" in layers:
        din("L1_ffn_w1", [4096, 1024])
        din("L1_ffn_conv_w", [128, 96])
        din("L1_ffn_conv_b", [128, 32])
        din("L1_ffn_w2", [1024, 4096])
        W["prevrows"] = nc.dram_tensor("prevrows", [128, 1], I32, kind="ExternalInput").ap()
        W["nextrows"] = nc.dram_tensor("nextrows", [128, 1], I32, kind="ExternalInput").ap()
    if "0a" in layers or "L0" in layers:
        din("L0_attn_W_qkv", [1024, 3072])
        din("L0_attn_W_out", [1024, 1024])
        din("L0_bq_cols", [64, 48])
        din("L0_bo_cols", [128, 8])
        din("L0_masks", [128, 4 * 512])
    if "1a" in layers or "L1" in layers:
        din("L1_Win", [1024, 2048])
        din("L1_cwx", [128, 32]); din("L1_cwz", [128, 32])
        din("L1_cbx", [128, 8]); din("L1_cbz", [128, 8])
        W["l1_hidx"] = nc.dram_tensor("l1_hidx", [128, 48], I32, kind="ExternalInput").ap()
        din("L1_Wqk_my", [1024, 512])
        din("L1_Wvg_my", [1024, 768])
        din("L1_Wbgk_my", [1024, 2])
        din("L1_bias2", [128, 2])
        din("L1_alog", [128, 1])
        din("L1_gnw", [1, 384])
        din("L1_WsoT_my", [1024, 384])
        din("L1_Wout1", [1024, 1024])
        din("L1_Wout2", [1024, 1024])
        din("L1_cumtri", [128, 128])
        din("L1_negtril", [128, 128])
        din("L1_negtriu", [128, 128])
        din("L1_triuinc", [128, 128])
        W["l1_gidx"] = nc.dram_tensor("l1_gidx", [128, 64], I32, kind="ExternalInput").ap()
        W["l1_yidx"] = nc.dram_tensor("l1_yidx", [128, 64], I32, kind="ExternalInput").ap()
        W["l1_sidx"] = nc.dram_tensor("l1_sidx", [128, 16], I32, kind="ExternalInput").ap()
    if "0f" in layers or "L0" in layers:
        din("L0_rw_cols", [128, 64])
        din("L0_w1_my", [1024, 4096])
        din("L0_w2_my", [4096, 1024])
        din("tri8", [8, 8])
        din("step8", [8, 1])
        W["moewin2"] = nc.dram_tensor("moewin2", [128, 1], I32, kind="ExternalInput").ap()
        din("moerowc", [128, 16])
    if "3a" in layers or "L3" in layers:
        din("L3_attn_w_qkv", [1024, 3072])
        din("L3_attn_w_out", [1024, 1024])
        din("L3_bq_cols", [128, 24])
        din("L3_bo_cols", [128, 8])
        din("L3_relrow", [1, 9216])
        din("L3_w1rpe", [1, 32])
        din("L3_b1rpe", [32, 1])
        din("L3_w2rpe", [32, 16])
        din("L3_b2rpe", [16, 1])
        din("flip_f32", [128, 128])
    if "3f" in layers or "L3" in layers:
        din("L3_ffn_clips", [1, 2])
        din("L3_ffn_w_gate", [2816, 1024])
        din("L3_ffn_w_up", [2816, 1024])
        din("L3_ffn_w_down", [1024, 2816])
    if 2 in layers:
        din("L2_attn_A_log", [64, 1])
        din("L2_attn_WB", [64, 1024])
        din("L2_attn_WC", [1024, 64])
        din("L2_attn_D1", [128, 8])
        din("L2_attn_Wg", [64, 1024])
        din("L2_attn_bg", [64, 1])
        din("L2_attn_Wout", [1024, 1024])
        din("L2_attn_bout", [128, 8])
        din("L2_attn_gamma", [128, 8])
        din("L2_attn_beta", [128, 8])
        din("L2_ffn_w1", [4096, 1024])
        din("L2_ffn_b1", [128, 32])
        din("L2_ffn_w2", [1024, 4096])
        din("L2_ffn_b2", [128, 8])

    DR = {
        "w1q": nc.dram_tensor("dr_w1q", [4096, 1024], BF16).ap(),
        "w2q": nc.dram_tensor("dr_w2q", [1024, 4096], BF16).ap(),
        "h_dram": nc.dram_tensor("dr_h", [4096, TOK], F32).ap(),
        "g_dram": nc.dram_tensor("dr_g", [1024, TOK], BF16).ap(),
        "s_dram": nc.dram_tensor("dr_s", [1024, TOK], BF16).ap(),
        "wgq": nc.dram_tensor("dr_wgq", [2816, 1024], BF16).ap(),
        "wuq": nc.dram_tensor("dr_wuq", [2816, 1024], BF16).ap(),
        "wdq": nc.dram_tensor("dr_wdq", [1024, 2816], BF16).ap(),
        "h3_dram": nc.dram_tensor("dr_h3", [2816, TOK], F32).ap(),
        "o0_dram": nc.dram_tensor("dr_o0", [4 * 8192, 64], BF16).ap(),
        "qloc3": nc.dram_tensor("dr_qloc3", [1024, 2048], BF16).ap(),
        "kloc3": nc.dram_tensor("dr_kloc3", [1024, 2048], BF16).ap(),
        "vloc3": nc.dram_tensor("dr_vloc3", [2048, 1024], BF16).ap(),
        "agk3": nc.dram_tensor("dr_agk3", [8192, 2048], BF16).ap(),
        "agv3": nc.dram_tensor("dr_agv3", [16384, 1024], BF16).ap(),
        "cv3": nc.dram_tensor("dr_cv3", [16, 9216], F32).ap(),
        "ecv3": nc.dram_tensor("dr_ecv3", [16, 9216], BF16).ap(),
        "sa3": nc.dram_tensor("dr_sa3", [1024, 2048], BF16).ap(),
        "cnt_in": nc.dram_tensor("dr_cntin", [8, 1], F32).ap(),
        "cnt_out": nc.dram_tensor("dr_cntout", [64, 1], F32).ap(),
        "srd": nc.dram_tensor("dr_srd", [4096, 1], F32).ap(),
        "rvtw0": nc.dram_tensor("dr_rvtw0", [32768, 2], F32).ap(),
        "rvtwA": nc.dram_tensor("dr_rvtwA", [32768, 2], F32).ap(),
        "xtm0": nc.dram_tensor("dr_xtm0", [2048, 1024], BF16).ap(),
        "agxn": nc.dram_tensor("dr_agxn", [16384, 1024], BF16).ap(),
        "ysl": nc.dram_tensor("dr_ysl", [4096, 1024], BF16).ap(),
        "agy": nc.dram_tensor("dr_agy", [32768, 1024], BF16).ap(),
        "rvall_d": nc.dram_tensor("dr_rvall", [128, 64], F32).ap(),
        "rstd_d": nc.dram_tensor("dr_rstd", [TOK, 1], F32).ap(),
        "mrs_d": nc.dram_tensor("dr_mrs", [TOK, 1], F32).ap(),
        "bnd_loc": nc.dram_tensor("dr_bndl", [1152, 8], F32).ap(),
        "bnd_all": nc.dram_tensor("dr_bnda", [9216, 8], F32).ap(),
        "xs_tm": nc.dram_tensor("dr_xstm", [2048, 1024], BF16).ap(),
        "agxs": nc.dram_tensor("dr_agxs", [16384, 1024], BF16).ap(),
        "zs_fm": nc.dram_tensor("dr_zsfm", [1024, 2048], BF16).ap(),
        "bg_d": nc.dram_tensor("dr_bgd", [8192, 2], F32).ap(),
        "vb_tm": nc.dram_tensor("dr_vbtm", [8192, 384], BF16).ap(),
        "gb_tm": nc.dram_tensor("dr_gbtm", [8192, 384], BF16).ap(),
        "ypart": nc.dram_tensor("dr_ypart", [16384, 1024], F32).ap(),
        "yall": nc.dram_tensor("dr_yall", [16384, 1024], F32).ap(),
    }
    g_dbg = None
    if any(x in layers for x in ("1f", "3f", "0a", "3a", "0f", "1a")):
        g_dbg = nc.dram_tensor("g_dbg", [1024, TOK], BF16, kind="ExternalOutput").ap()

    with ExitStack() as outer:
        P = Prog(nc)
        A = Arena(nc, outer)
        onesc = A.sb("ones_col", [128, 1], F32)
        onesb = A.sb("ones_col_bf", [128, 1], BF16)
        onesr = A.sb("ones_row", [1, 128], F32)
        identb = A.sb("ident_bf", [128, 128], BF16)
        for t in ("ptb0", "ptb1", "ptb2", "ptb3", "ps_a", "ps_b", "ps_bc", "ps_c"):
            A.ps(t, [128, 512], F32)
        P.rec("sync", lambda e: e.dma_start(out=onesc[:, :], in_=W["ones_fill"]),
              inc=(("init", 16),))
        P.rec("gpsimd", lambda e: e.dma_start(out=onesb[:, :], in_=W["ones_fill"]),
              inc=(("initg", 16),))
        P.rec("gpsimd", lambda e: e.dma_start(out=identb[:, :], in_=W["ident_bf_src"]),
              inc=(("initg", 16),))
        P.rec("sync", lambda e: e.dma_start(out=onesr[0:1, :], in_=W["ones_rowsrc"]),
              inc=(("init", 16),))
        epsc = A.sb("epscol_sb", [128, 4], F32)
        P.rec("sync", lambda e: e.dma_start(out=epsc[:, :], in_=W["epscol"]),
              inc=(("init", 16),))
        identf0 = A.sb("identf_sb", [128, 128], F32)
        P.rec("sync", lambda e: e.dma_start(out=identf0[:, :], in_=W["ident_f32"]),
              inc=(("init", 16),))
        initv = P.val("init"); initgv = P.val("initg")
        P.rec("tensor", lambda e: None, wait=(("init", initv), ("initg", initgv)))
        P.rec("vector", lambda e: None, wait=(("init", initv), ("initg", initgv)))
        P.rec("scalar", lambda e: None, wait=(("init", initv), ("initg", initgv)))

        x_dram = nc.dram_tensor("x_res", [D, TOK], F32).ap()
        A.push()
        emit_load_x(P, A, W["x_in"], x_dram, W)
        A.pop()
        P.gate((("ldxw", P.val("ldxw")),))

        for L in layers:
            print("LAYER", L, "sbuf_base", nc.sbuf_base)
            A.push()
            if L == "L0":
                A.sb("xn_bf", [128, NKC * TOK], BF16)
                A.push()
                xn, r_mean, r_rstd, bc = emit_ln(P, A, x_dram, W, "l0")
                ctx0 = emit_l0_ffn_route(P, A, xn, ("ln_xn", P.val("ln_xn")), W, DR,
                                         x_dram, r_mean, r_rstd)
                A.pop()
                P.gate((("ln_xn", P.val("ln_xn")), ("ln_mmq", P.val("ln_mmq")))
                       + tuple(ctx0["gate"]))
                A.push()
                sdone0 = emit_l0_attn(P, A, xn, ("ln_xn", P.val("ln_xn")), W, W, DR)
                A.pop()
                P.gate((sdone0,))
                gdone0 = emit_l0_ffn_main(P, A, W, DR, ctx0)
                emit_combine(P, A, x_dram, DR["s_dram"], DR["g_dram"], "l0",
                             sdone0, gdone0[0], extra=(gdone0[1],))
            if L == "L1":
                A.sb("xn_bf", [128, NKC * TOK], BF16)
                A.sb("rmean_l1", [1, TOK], F32)
                A.sb("rrstd_l1", [1, TOK], F32)
                A.push()
                xn, r_mean, r_rstd, bc = emit_ln(P, A, x_dram, W, "l1")
                A.pop()
                P.gate((("ln_xn", P.val("ln_xn")), ("ln_mmq", P.val("ln_mmq"))))
                agxv = emit_l1_attn(P, A, xn, ("ln_xn", P.val("ln_xn")), W, DR, x_dram)
                emit_l1_attn_core(P, A, W, DR, agxv)
                yv = emit_l1_delta(P, A, W, DR)
                A.pop()
                P.gate((yv, ("a1_de", P.val("a1_de")), ("a1_ds", P.val("a1_ds"))))
                sdone1 = emit_l1_fin(P, A, W, DR, yv)
                P.gate(tuple(sdone1))
                gdone1 = emit_l1_ffn(P, A, x_dram, r_mean, r_rstd, W, W, DR)
                emit_combine(P, A, x_dram, DR["s_dram"], DR["g_dram"], "l1",
                             sdone1[0], gdone1, extra=(sdone1[1],))
            if L == "L3":
                A.sb("xn_bf", [128, NKC * TOK], BF16)
                A.sb("rmean_l3", [1, TOK], F32)
                A.sb("rrstd_l3", [1, TOK], F32)
                A.push()
                xn, r_mean, r_rstd, bc = emit_ln(P, A, x_dram, W, "l3")
                A.pop()
                P.gate((("ln_xn", P.val("ln_xn")), ("ln_mmq", P.val("ln_mmq"))))
                A.push()
                sdone3 = emit_l3_attn(P, A, xn, ("ln_xn", P.val("ln_xn")), W, DR)
                A.pop()
                P.gate((sdone3,))
                gdone3 = emit_l3_ffn(P, A, x_dram, r_mean, r_rstd, W, W, DR)
                emit_combine(P, A, x_dram, DR["s_dram"], DR["g_dram"], "l3",
                             sdone3, gdone3)
            if L == "1f":
                xn, r_mean, r_rstd, bc = emit_ln(P, A, x_dram, W, "l1")
                gdone = emit_l1_ffn(P, A, x_dram, r_mean, r_rstd, W, W, DR)
                P.rec("sync", lambda e: e.dma_start(out=g_dbg, in_=DR["g_dram"]),
                      wait=(gdone,), inc=(("gdb", 16),))
                P.rec("sync", lambda e: None, wait=(("gdb", P.val("gdb")),))
            if L == "0a":
                A.sb("xn_bf", [128, NKC * TOK], BF16)
                A.push()
                xn, r_mean, r_rstd, bc = emit_ln(P, A, x_dram, W, "l0")
                A.pop()
                P.gate((("ln_xn", P.val("ln_xn")), ("ln_mmq", P.val("ln_mmq"))))
                sdone = emit_l0_attn(P, A, xn, ("ln_xn", P.val("ln_xn")), W, W, DR)
                P.rec("sync", lambda e: e.dma_start(out=g_dbg, in_=DR["s_dram"]),
                      wait=(sdone,), inc=(("gdb", 16),))
                P.rec("sync", lambda e: None, wait=(("gdb", P.val("gdb")),))
            if L == "1a":
                A.push()
                xn, r_mean, r_rstd, bc = emit_ln(P, A, x_dram, W, "l1")
                agxv = emit_l1_attn(P, A, xn, ("ln_xn", P.val("ln_xn")), W, DR, x_dram)
                A.pop()
                emit_l1_attn_core(P, A, W, DR, agxv)
                yv = emit_l1_delta(P, A, W, DR)
                A.pop()
                P.gate((yv, ("a1_de", P.val("a1_de")), ("a1_ds", P.val("a1_ds"))))
                sdone = emit_l1_fin(P, A, W, DR, yv)
                P.rec("sync", lambda e: e.dma_start(out=g_dbg, in_=DR["s_dram"]),
                      wait=tuple(sdone), inc=(("gdb", 16),))
                P.rec("sync", lambda e: None, wait=(("gdb", P.val("gdb")),))
            if L == "0f":
                A.push()
                xn, r_mean, r_rstd, bc = emit_ln(P, A, x_dram, W, "l0")
                ctx0 = emit_l0_ffn_route(P, A, xn, ("ln_xn", P.val("ln_xn")), W, DR,
                                         x_dram, r_mean, r_rstd)
                A.pop()
                gdone = emit_l0_ffn_main(P, A, W, DR, ctx0)
                P.rec("sync", lambda e: e.dma_start(out=g_dbg, in_=DR["g_dram"]),
                      wait=tuple(gdone), inc=(("gdb", 16),))
                P.rec("sync", lambda e: None, wait=(("gdb", P.val("gdb")),))
            if L == "3a":
                A.sb("xn_bf", [128, NKC * TOK], BF16)
                A.push()
                xn, r_mean, r_rstd, bc = emit_ln(P, A, x_dram, W, "l3")
                A.pop()
                P.gate((("ln_xn", P.val("ln_xn")), ("ln_mmq", P.val("ln_mmq"))))
                sdone = emit_l3_attn(P, A, xn, ("ln_xn", P.val("ln_xn")), W, DR)
                P.rec("sync", lambda e: e.dma_start(out=g_dbg, in_=DR["s_dram"]),
                      wait=(sdone,), inc=(("gdb", 16),))
                P.rec("sync", lambda e: None, wait=(("gdb", P.val("gdb")),))
            if L == "3f":
                xn, r_mean, r_rstd, bc = emit_ln(P, A, x_dram, W, "l3")
                gdone = emit_l3_ffn(P, A, x_dram, r_mean, r_rstd, W, W, DR)
                P.rec("sync", lambda e: e.dma_start(out=g_dbg, in_=DR["g_dram"]),
                      wait=(gdone,), inc=(("gdb", 16),))
                P.rec("sync", lambda e: None, wait=(("gdb", P.val("gdb")),))
            if L == 2:
                xn, r_mean, r_rstd, bc = emit_ln(P, A, x_dram, W, "l2")
                A.push()
                l2a = emit_l2_attn(P, A, None, xn, "l2", W, W, DR["s_dram"])
                r2 = emit_ln2_and_wout(P, A, W, W, l2a)
                A.pop()
                P.gate((r2["sdone"], r2["sdone2"]))
                gdone = emit_l2_ffn(P, A, x_dram, r_mean, r_rstd, W, W, DR, None)
                emit_combine(P, A, x_dram, DR["s_dram"], DR["g_dram"], "l2", r2["sdone"], gdone,
                             extra=(r2["sdone2"],))
            A.pop()
            if len(layers) > 1:
                P.gate_all()
                P.recycle()

        A.push()
        if "L3" in layers:
            fin_w = ("cbl3w", P.val("cbl3w"))
        elif 2 in layers:
            fin_w = ("cbl2w", P.val("cbl2w"))
        else:
            fin_w = ("ldxw", P.val("ldxw"))
        emit_store_x(P, A, x_dram, x_out, W, fin_w)
        A.pop()

        with nc.Block() as block:
            P.emit(block)
        P.ctx.close()
    return nc


def make_consts():
    o = {}
    o["ident_f32"] = np.eye(128, dtype=np.float32)
    o["ident_bf_src"] = np.eye(128, dtype=np.float32)
    o["ones_fill"] = np.ones((128, 1), np.float32)
    o["ones_rowsrc"] = np.ones((1, 128), np.float32)
    ep = np.zeros((128, 4), np.float32); ep[:, 0] = 1e-5; ep[:, 1] = 12582912.0; ep[:, 2] = 1e-6
    o["epscol"] = ep
    return o

def colsify(v, nk=8):
    F = v.shape[0]
    return np.ascontiguousarray(v.reshape(F // 128, 128).T).astype(np.float32)

def make_inputs_l2(I, core, x_cur):
    o = make_consts()
    sel = np.zeros((128, 8), np.float32)
    if core >= 1:
        sel[:, core - 1] = 1.0
    o["coresel"] = sel
    xs = x_cur[:, 1024*core:1024*(core+1), :].reshape(TOK, D)
    o["x_in"] = np.ascontiguousarray(xs)
    o["L2_attn_A_log"] = I["L2_attn_A_log"].reshape(64, 1)
    o["L2_attn_WB"] = I["L2_attn_WB"]
    o["L2_attn_WC"] = I["L2_attn_WC"]
    o["L2_attn_D1"] = colsify(I["L2_attn_D"] + 1.0)
    o["L2_attn_Wg"] = I["L2_attn_Wg"]
    o["L2_attn_bg"] = I["L2_attn_bg"].reshape(64, 1)
    o["L2_attn_Wout"] = I["L2_attn_Wout"]
    o["L2_attn_bout"] = colsify(I["L2_attn_bout"])
    o["L2_attn_gamma"] = colsify(I["L2_attn_gamma"])
    o["L2_attn_beta"] = colsify(I["L2_attn_beta"])
    o["L2_ffn_w1"] = I["L2_ffn_w1"]
    o["L2_ffn_b1"] = colsify(I["L2_ffn_b1"])
    o["L2_ffn_w2"] = I["L2_ffn_w2"]
    o["L2_ffn_b2"] = colsify(I["L2_ffn_b2"])
    return {k: np.ascontiguousarray(v, dtype=np.float32) for k, v in o.items()}

def emit_l1_ffn(P, A, x_dram, r_mean, r_rstd, consts, W, DR):
    onesc = A.sb("ones_col", [128, 1], F32)
    onesr = A.sb("ones_row", [1, 128], F32)
    bcast = A.sb("bcast2", [128, 3 * TOK], F32)
    xnscr = A.sb("xn_scr", [128, TOK], F32)
    acc = A.sb("colmax", [128, TOK], F32)
    rowA = A.sb("f1_rowA", [1, TOK], F32)
    rowB = A.sb("f1_rowB", [1, TOK], F32)
    rowC = A.sb("f1_rowC", [1, TOK], F32)
    convw = A.sb("f1_convw", [128, 3 * 32], F32)
    convb = A.sb("f1_convb", [128, 32], F32)
    s_ = "f1_"
    P.rec("sync", lambda e: e.dma_start(out=convw[:, :], in_=W["L1_ffn_conv_w"]), inc=(("f1w", 16),))
    P.rec("sync", lambda e: e.dma_start(out=convb[:, :], in_=W["L1_ffn_conv_b"]), inc=(("f1w", 16),))

    w1q_done = emit_wquant_mean(P, A, W["L1_ffn_w1"], 4096, 1024, DR["w1q"], "q1_l2")
    wsB1 = A.sb("wsB_q1_l2", [128, 2], F32)
    w1win = A.sb("w1win", [128, 2 * 8 * 512], BF16)
    xr = x_dram.rearrange("(kc p) t -> kc p t", p=128)

    emit_bcast_rows(P, A, [(r_mean, 0)], bcast, onesr, s_ + "m")
    mb = ("bcev", P.val("bcev"))
    P.rec("vector", lambda e: e.memset(acc[:, :], 0.0), inc=((s_ + "am", 1),))
    for kc in range(NKC):
        P.rec("sync", lambda e, kc=kc: e.dma_start(out=xnscr[:, :], in_=xr[kc]),
              wait=([mb] + ([(s_ + "am", P.val(s_ + "am"))] if kc else [])),
              inc=((s_ + "xl", 16),))
        P.rec("vector", lambda e, kc=kc: e.tensor_sub(
            xnscr[:, :], xnscr[:, :], bcast[:, 0:TOK]),
            wait=((s_ + "xl", P.val(s_ + "xl")),))
        P.rec("vector", lambda e: e.scalar_tensor_tensor(
            acc[:, :], xnscr[:, :], 1.0, acc[:, :], ALU.mult, ALU.max))
        P.rec("vector", lambda e: e.scalar_tensor_tensor(
            acc[:, :], xnscr[:, :], -1.0, acc[:, :], ALU.mult, ALU.max),
            inc=((s_ + "am", 1),))
    emit_absmax_tree(P, A, acc, TOK, s_ + "am")
    amv = (s_ + "am", P.val(s_ + "am"))
    P.rec("vector", lambda e: e.tensor_mul(rowC[0:1, :], r_rstd[0:1, :], acc[0:1, :]),
          wait=(amv,))
    P.rec("vector", lambda e: e.tensor_scalar_max(rowC[0:1, :], rowC[0:1, :], 1e-5))
    P.rec("vector", lambda e: e.reciprocal(rowA[0:1, :], rowC[0:1, :]))
    P.rec("vector", lambda e: e.tensor_scalar_mul(rowA[0:1, :], rowA[0:1, :], 127.0))
    P.rec("vector", lambda e: e.tensor_mul(rowB[0:1, :], r_rstd[0:1, :], rowA[0:1, :]))
    P.rec("vector", lambda e: e.tensor_scalar_mul(rowC[0:1, :], rowC[0:1, :], 1.0 / 127.0),
          inc=((s_ + "r", 1),))
    emit_bcast_rows(P, A, [(rowB, TOK), (rowC, 2 * TOK)], bcast, onesr, s_ + "c",
                    extra_wait=((s_ + "r", P.val(s_ + "r")),))
    cb = ("bcev", P.val("bcev"))
    xq = A.sb("xn_bf", [128, NKC * TOK], BF16)
    for kc in range(NKC):
        P.rec("sync", lambda e, kc=kc: e.dma_start(out=xnscr[:, :], in_=xr[kc]),
              wait=([cb] + ([(s_ + "xq", P.val(s_ + "xq"))] if kc else [])),
              inc=((s_ + "xl", 16),))
        P.rec("vector", lambda e, kc=kc: e.tensor_sub(
            xnscr[:, :], xnscr[:, :], bcast[:, 0:TOK]),
            wait=((s_ + "xl", P.val(s_ + "xl")),))
        P.rec("vector", lambda e: e.tensor_mul(xnscr[:, :], xnscr[:, :], bcast[:, TOK:2*TOK]))
        P.rec("vector", lambda e: e.tensor_scalar(
            xnscr[:, :], xnscr[:, :], RC, RC, ALU.add, ALU.subtract))
        P.rec("vector", lambda e, kc=kc: e.tensor_scalar(
            xq[:, kc*TOK:(kc+1)*TOK], xnscr[:, :], 127.0, -128.0, ALU.min, ALU.max),
            inc=((s_ + "xq", 1),))
    xqv = (s_ + "xq", P.val(s_ + "xq"))

    ps_h = A.ps("ps_a", [128, 512], F32)
    ps_h2 = A.ps("ps_b", [128, 512], F32)
    htile = A.sb("htile", [128, 512], F32)
    htile2 = A.sb("htile2", [128, 512], F32)
    hdr = DR["h_dram"]
    w1qd0, w1qd1 = w1q_done
    for hs in range(8):
        wb = 0 if hs % 2 == 0 else 8 * 512
        wv = [w1qd0, w1qd1]
        if hs >= 2:
            wv.append((s_ + "hmm", P.val(s_ + "hmm") - 16))
        for dc in range(NKC):
            P.rec("sync", lambda e, hs=hs, dc=dc, wb=wb: e.dma_start_transpose(
                out=w1win[:, wb + dc*512: wb + (dc+1)*512],
                in_=DR["w1q"][hs*512:(hs+1)*512, dc*128:(dc+1)*128]),
                wait=wv if dc == 0 else (), inc=((s_ + "wl" + str(hs % 2), 16),))
        wlv = (s_ + "wl" + str(hs % 2), P.val(s_ + "wl" + str(hs % 2)))
        for tt in range(4):
            for oc2 in range(4):
                oc = hs * 4 + oc2
                i = hs * 16 + tt * 4 + oc2
                psh = ps_h if i % 2 == 0 else ps_h2
                ht = htile if i % 2 == 0 else htile2
                for kc in range(NKC):
                    wv2 = []
                    if kc == 0:
                        wv2 = [wlv, xqv, (s_ + "hev", P.val(s_ + "hev") - 1)]
                    P.rec("tensor", lambda e, oc2=oc2, tt=tt, kc=kc, psh=psh, wb=wb: e.matmul(
                        psh[:, :], w1win[:, wb + kc*512 + oc2*128: wb + kc*512 + oc2*128 + 128],
                        xq[:, kc*TOK + tt*512: kc*TOK + (tt+1)*512],
                        start=(kc == 0), stop=(kc == NKC - 1)),
                        wait=wv2, inc=((s_ + "hmm", 1),) if kc == NKC - 1 else ())
                hwp = s_ + "hw" + str(i % 2)
                P.rec("vector", lambda e, tt=tt, psh=psh, ht=ht: e.tensor_mul(
                    ht[:, :], psh[:, :], bcast[:, 2*TOK + tt*512: 2*TOK + (tt+1)*512]),
                    wait=((s_ + "hmm", P.val(s_ + "hmm")), (hwp, P.val(hwp))),
                    inc=((s_ + "hev", 1),))
                P.rec("scalar", lambda e, ht=ht: e.activation(
                    out=ht[:, :], in_=ht[:, :], func=AF.Identity, scale=wsB1[:, 0:1]),
                    wait=((s_ + "hev", P.val(s_ + "hev")),), inc=((s_ + "hg", 1),))
                bnd = {0: (0, 0), 1: (511, 1), 2: (0, 2), 3: (511, 3)}
                col, j = bnd[tt]
                P.rec("vector", lambda e, ht=ht, col=col, j=j, oc=oc: e.tensor_copy(
                    halo[:, j*32 + oc: j*32 + oc + 1], ht[:, col:col+1]),
                    wait=((s_ + "hg", P.val(s_ + "hg")),), inc=((s_ + "hcp", 1),))
                P.rec("gpsimd", lambda e, tt=tt, oc=oc, ht=ht: e.dma_start(
                    out=hdr[oc*128:(oc+1)*128, tt*512:(tt+1)*512], in_=ht[:, :]),
                    wait=((s_ + "hg", P.val(s_ + "hg")),),
                    inc=((s_ + "hw" + str(i % 2), 16),))
    hwv = [(s_ + "hw0", P.val(s_ + "hw0")), (s_ + "hw1", P.val(s_ + "hw1"))]

    cc_in = P.nc.dram_tensor("f1_cc_in", [128, 128], F32).ap()
    cc_out = P.nc.dram_tensor("f1_cc_out", [1024, 128], F32).ap()
    halo = A.sb("f1_halo", [128, 128], F32)
    haloP = A.sb("f1_haloP", [128, 128], F32)
    haloN = A.sb("f1_haloN", [128, 128], F32)
    idxP = A.sb("f1_idxP", [128, 1], I32)
    idxN = A.sb("f1_idxN", [128, 1], I32)
    P.rec("sync", lambda e: e.dma_start(out=idxP[:, :], in_=W["prevrows"]), inc=(("f1w", 16),))
    P.rec("sync", lambda e: e.dma_start(out=idxN[:, :], in_=W["nextrows"]), inc=(("f1w", 16),))
    P.rec("sync", lambda e: e.dma_start(out=cc_in, in_=halo[:, :]),
          wait=((s_ + "hcp", P.val(s_ + "hcp")),), inc=((s_ + "bl", 16),))
    P.rec("gpsimd", lambda e: e.collective_compute(
        "AllGather", ALU.bypass, replica_groups=RG8,
        ins=[cc_in.tensor.ap().opt()], outs=[cc_out.tensor.ap().opt()]),
        wait=((s_ + "bl", P.val(s_ + "bl")),), inc=(("@f1cc", 1),))
    P.rec("vector", lambda e: e.memset(haloP[:, :], 0.0), inc=((s_ + "hz", 1),))
    P.rec("vector", lambda e: e.memset(haloN[:, :], 0.0), inc=((s_ + "hz", 1),))
    P.rec("gpsimd", lambda e: e.indirect_dma_start(
        out=haloP[:, :], out_offset=None, in_=cc_out,
        in_offset=IndirectOffsetOnAxis(idxP[:, :], 0),
        bounds_check=1023, oob_is_err=False),
        wait=(("@f1cc", P.val("@f1cc")), ("f1w", P.val("f1w")), (s_ + "hz", 2)),
        inc=((s_ + "hpg", 16),))
    P.rec("gpsimd", lambda e: e.indirect_dma_start(
        out=haloN[:, :], out_offset=None, in_=cc_out,
        in_offset=IndirectOffsetOnAxis(idxN[:, :], 0),
        bounds_check=1023, oob_is_err=False),
        inc=((s_ + "hpg", 16),))
    hpv = (s_ + "hpg", P.val(s_ + "hpg"))

    hrowb = A.sb("f1_hrow", [128, TOK], F32)
    hacc = A.sb("f1_hacc", [128, TOK], F32)
    P.rec("vector", lambda e: e.memset(acc[:, :], 0.0), wait=(amv,), inc=((s_ + "cm", 1),))
    for oc in range(32):
        P.rec("sync", lambda e, oc=oc: e.dma_start(
            out=hrowb[:, :], in_=hdr[oc*128:(oc+1)*128, :]),
            wait=hwv + [(s_ + "cw", P.val(s_ + "cw")), (s_ + "cmu", P.val(s_ + "cmu"))],
            inc=((s_ + "cl", 16),))
        clv = (s_ + "cl", P.val(s_ + "cl"))
        P.rec("vector", lambda e, oc=oc: e.tensor_scalar(
            hacc[:, :], hrowb[:, :], convw[:, 32 + oc:33 + oc], convb[:, oc:oc+1],
            ALU.mult, ALU.add),
            wait=[clv, ("f1w", P.val("f1w")), (s_ + "cw", P.val(s_ + "cw"))])
        for b in range(Bv):
            s0 = b * TLOC
            P.rec("vector", lambda e, oc=oc, s0=s0: e.scalar_tensor_tensor(
                hacc[:, s0+1:s0+TLOC], hrowb[:, s0:s0+TLOC-1], convw[:, oc:oc+1],
                hacc[:, s0+1:s0+TLOC], ALU.mult, ALU.add))
            P.rec("vector", lambda e, oc=oc, s0=s0: e.scalar_tensor_tensor(
                hacc[:, s0:s0+TLOC-1], hrowb[:, s0+1:s0+TLOC], convw[:, 64+oc:65+oc],
                hacc[:, s0:s0+TLOC-1], ALU.mult, ALU.add))
            P.rec("vector", lambda e, oc=oc, s0=s0, b=b: e.scalar_tensor_tensor(
                hacc[:, s0:s0+1], haloP[:, (2*b+1)*32 + oc:(2*b+1)*32 + oc + 1],
                convw[:, oc:oc+1], hacc[:, s0:s0+1], ALU.mult, ALU.add),
                wait=(hpv,))
            P.rec("vector", lambda e, oc=oc, s0=s0, b=b: e.scalar_tensor_tensor(
                hacc[:, s0+TLOC-1:s0+TLOC], haloN[:, (2*b)*32 + oc:(2*b)*32 + oc + 1],
                convw[:, 64+oc:65+oc], hacc[:, s0+TLOC-1:s0+TLOC], ALU.mult, ALU.add),
                inc=(((s_ + "cvv", 1),) if b == Bv - 1 else ()))
        P.rec("scalar", lambda e: e.activation(out=hacc[:, :], in_=hacc[:, :], func=AF.Silu),
              wait=((s_ + "cvv", P.val(s_ + "cvv")),), inc=((s_ + "cs", 1),))
        P.rec("vector", lambda e: e.scalar_tensor_tensor(
            acc[:, :], hacc[:, :], 1.0, acc[:, :], ALU.mult, ALU.max),
            wait=((s_ + "cs", P.val(s_ + "cs")), (s_ + "cm", 1)))
        P.rec("vector", lambda e: e.scalar_tensor_tensor(
            acc[:, :], hacc[:, :], -1.0, acc[:, :], ALU.mult, ALU.max),
            inc=((s_ + "cmu", 1),))
        P.rec("gpsimd", lambda e, oc=oc: e.dma_start(
            out=hdr[oc*128:(oc+1)*128, :], in_=hacc[:, :]),
            wait=((s_ + "cmu", P.val(s_ + "cmu")),), inc=((s_ + "cw", 16),))
    cwv = (s_ + "cw", P.val(s_ + "cw"))
    P.rec("vector", lambda e: e.engine_nop(), wait=((s_ + "cmu", P.val(s_ + "cmu")),),
          inc=((s_ + "cm", 1),))
    emit_absmax_tree(P, A, acc, TOK, s_ + "cm")
    cmv = (s_ + "cm", P.val(s_ + "cm"))
    P.rec("vector", lambda e: e.tensor_scalar_max(rowC[0:1, :], acc[0:1, :], 1e-5), wait=(cmv,))
    P.rec("vector", lambda e: e.reciprocal(rowB[0:1, :], rowC[0:1, :]))
    P.rec("vector", lambda e: e.tensor_scalar_mul(rowB[0:1, :], rowB[0:1, :], 127.0))
    P.rec("vector", lambda e: e.tensor_scalar_mul(rowC[0:1, :], rowC[0:1, :], 1.0 / 127.0),
          inc=((s_ + "r", 1),))
    emit_bcast_rows(P, A, [(rowB, TOK), (rowC, 2 * TOK)], bcast, onesr, s_ + "d",
                    extra_wait=((s_ + "r", P.val(s_ + "r")),))
    db = ("bcev", P.val("bcev"))

    w2q_done = emit_wquant_mean(P, A, W["L1_ffn_w2"], 1024, 4096, DR["w2q"], "q2_l2",
                                prewait=((s_ + "hmm", P.val(s_ + "hmm")),))
    wsB2 = A.sb("wsB_q2_l2", [128, 2], F32)
    w2win = A.sb("w1win", [128, 2 * 8 * 512], BF16)
    xq2 = A.sb("xn_bf", [128, NKC * TOK], BF16)
    gdr = DR["g_dram"]
    hsq = A.sb("hsq", [128, 512], F32)
    pbanks = [A.ps(t, [128, 512], F32) for t in
              ("ptb0", "ptb1", "ptb2", "ptb3", "ps_a", "ps_b", "ps_bc", "ps_c")]
    for tt in range(4):
        for hc in range(32):
            ht = htile if hc % 2 == 0 else htile2
            wv = [cwv]
            if hc < 2:
                wv.append((s_ + "gmm", P.val(s_ + "gmm")))
                wv.append((s_ + "gs", P.val(s_ + "gs")))
            else:
                wv.append((s_ + "q2", P.val(s_ + "q2") - 1))
            hlp = s_ + "hl" + str(hc % 2)
            P.rec("sync", lambda e, tt=tt, hc=hc, ht=ht: e.dma_start(
                out=ht[:, :], in_=hdr[hc*128:(hc+1)*128, tt*512:(tt+1)*512]),
                wait=wv, inc=((hlp, 16),))
            P.rec("vector", lambda e, tt=tt, ht=ht: e.tensor_mul(
                ht[:, :], ht[:, :], bcast[:, TOK + tt*512: TOK + (tt+1)*512]),
                wait=((hlp, P.val(hlp)), db))
            P.rec("vector", lambda e, ht=ht: e.tensor_scalar(
                ht[:, :], ht[:, :], RC, RC, ALU.add, ALU.subtract))
            P.rec("vector", lambda e, hc=hc, ht=ht: e.tensor_scalar(
                xq2[:, hc*512:(hc+1)*512], ht[:, :], 127.0, -128.0, ALU.min, ALU.max),
                inc=((s_ + "q2", 1),))
        q2v = (s_ + "q2", P.val(s_ + "q2"))
        for hw_ in range(8):
            wb = 0 if hw_ % 2 == 0 else 8 * 512
            wv = list(w2q_done)
            if hw_ >= 2:
                wv.append((s_ + "gwin", P.val(s_ + "gwin") - 1))
            for hh in range(4):
                hc = hw_ * 4 + hh
                P.rec("sync", lambda e, hc=hc, hh=hh, wb=wb: e.dma_start_transpose(
                    out=w2win[:, wb + hh*1024: wb + (hh+1)*1024],
                    in_=DR["w2q"][0:1024, hc*128:(hc+1)*128]),
                    wait=wv if hh == 0 else (), inc=((s_ + "w2l" + str(hw_ % 2), 16),))
            w2lv = (s_ + "w2l" + str(hw_ % 2), P.val(s_ + "w2l" + str(hw_ % 2)))
            for hh in range(4):
                hc = hw_ * 4 + hh
                for oc in range(8):
                    wv2 = []
                    if hh == 0 and oc == 0:
                        wv2 = [w2lv, q2v, (s_ + "pse", P.val(s_ + "pse"))]
                    elif hh == 0:
                        wv2 = [w2lv]
                    inc = []
                    if hc == 31:
                        inc.append((s_ + "gmm", 1))
                    if hh == 3 and oc == 7:
                        inc.append((s_ + "gwin", 1))
                    P.rec("tensor", lambda e, oc=oc, hc=hc, hh=hh, wb=wb: e.matmul(
                        pbanks[oc][:, :], w2win[:, wb + hh*1024 + oc*128: wb + hh*1024 + oc*128 + 128],
                        xq2[:, hc*512:(hc+1)*512],
                        start=(hc == 0), stop=(hc == 31)),
                        wait=wv2, inc=inc)
        gmmv = (s_ + "gmm", P.val(s_ + "gmm"))
        for oc in range(8):
            P.rec("vector", lambda e, tt=tt, oc=oc: e.tensor_mul(
                htile[:, :], pbanks[oc][:, :], bcast[:, 2*TOK + tt*512: 2*TOK + (tt+1)*512]),
                wait=(gmmv, (s_ + "gs", P.val(s_ + "gs"))), inc=((s_ + "pse", 1),))
            P.rec("scalar", lambda e, oc=oc: e.activation(
                out=hsq[:, :], in_=htile[:, :], func=AF.Silu, scale=wsB2[:, 0:1]),
                wait=((s_ + "gw", P.val(s_ + "gw")), (s_ + "pse", P.val(s_ + "pse"))),
                inc=((s_ + "gs", 1),))
            P.rec("gpsimd", lambda e, oc=oc, tt=tt: e.dma_start(
                out=gdr[oc*128:(oc+1)*128, tt*512:(tt+1)*512], in_=hsq[:, :]),
                wait=((s_ + "gs", P.val(s_ + "gs")),),
                inc=((s_ + "gw", 16),))
    return (s_ + "gw", P.val(s_ + "gw"))

def emit_l3_ffn(P, A, x_dram, r_mean, r_rstd, consts, W, DR):
    onesc = A.sb("ones_col", [128, 1], F32)
    onesr = A.sb("ones_row", [1, 128], F32)
    bcast = A.sb("bcast2", [128, 3 * TOK], F32)
    xnscr = A.sb("xn_scr", [128, TOK], F32)
    acc = A.sb("colmax", [128, TOK], F32)
    rowA = A.sb("f3_rowA", [1, TOK], F32)
    rowB = A.sb("f3_rowB", [1, TOK], F32)
    scl = A.sb("f3_scl", [1, 16], F32)
    gsc = A.sb("f3_gsc", [128, 22], F32)
    usc = A.sb("f3_usc", [128, 22], F32)
    dsc = A.sb("f3_dsc", [128, 8], F32)
    scb = A.sb("f3_scb", [128, 4], F32)
    psc = A.ps("ps_bc", [128, 512], F32)
    wslab = A.sb("wslab", [128, 1024], F32)
    wslab2 = A.sb("wslab2", [128, 1024], F32)
    sbb = A.sb("wslabb", [128, 1024], BF16)
    sbb2 = A.sb("wslabb2", [128, 1024], BF16)
    rcol = A.sb("f3_rcol", [128, 1], F32)
    s_ = "f3_"
    xr = x_dram.rearrange("(kc p) t -> kc p t", p=128)
    P.rec("sync", lambda e: e.dma_start(out=scl[0:1, 14:16], in_=W["L3_ffn_clips"]),
          inc=(("f3w", 16),))

    def wq_rows(wdram, R, wq_dram, scol, tagp, prew=()):
        wr = wdram.rearrange("(n p) c -> n p c", p=128)
        wqr = wq_dram.rearrange("(n p) c -> n p c", p=128)
        C = wdram.shape[1]
        chunks = [(i, min(1024, C - i)) for i in range(0, C, 1024)]
        rcol2 = A.sb("f3_rcol2", [128, 1], F32)
        for n in range(R // 128):
            for ci, (c0, cw) in enumerate(chunks):
                j = n * len(chunks) + ci
                buf = wslab if j % 2 == 0 else wslab2
                lp = s_ + tagp + "l" + str(j % 2)
                wp = s_ + tagp + "w" + str(j % 2)
                wp0 = s_ + tagp + "w" + str(j % 2)
                P.rec("sync", lambda e, n=n, c0=c0, cw=cw, buf=buf: e.dma_start(
                    out=buf[:, 0:cw], in_=wr[n][:, c0:c0+cw]),
                    wait=([(s_ + tagp + "mx", P.val(s_ + tagp + "mx") - 1),
                           (wp0, P.val(wp0))] if j >= 2 else list(prew)),
                    inc=((lp, 16),))
                P.rec("vector", lambda e, buf=buf, cw=cw: e.reduce_max(
                    rcol2[:, 0:1], buf[:, 0:cw], axis=AX.X, apply_absolute_value=True),
                    wait=((lp, P.val(lp)),))
                if ci == 0:
                    P.rec("vector", lambda e: e.tensor_copy(rcol[:, 0:1], rcol2[:, 0:1]),
                          inc=((s_ + tagp + "mx", 1),))
                else:
                    P.rec("vector", lambda e: e.tensor_max(rcol[:, 0:1], rcol[:, 0:1],
                                                           rcol2[:, 0:1]),
                          inc=((s_ + tagp + "mx", 1),))
            P.rec("vector", lambda e, n=n, scol=scol: e.tensor_scalar_mul(
                scol[:, n:n+1], rcol[:, 0:1], 1.0 / 127.0))
            P.rec("vector", lambda e: e.reciprocal(rcol[:, 0:1], rcol[:, 0:1]))
            P.rec("vector", lambda e: e.tensor_scalar_mul(rcol[:, 0:1], rcol[:, 0:1], 127.0))
            for ci, (c0, cw) in enumerate(chunks):
                j = n * len(chunks) + ci
                buf = wslab if j % 2 == 0 else wslab2
                bq = sbb if j % 2 == 0 else sbb2
                lp = s_ + tagp + "lq" + str(j % 2)
                wp = s_ + tagp + "w" + str(j % 2)
                P.rec("sync", lambda e, n=n, c0=c0, cw=cw, buf=buf: e.dma_start(
                    out=buf[:, 0:cw], in_=wr[n][:, c0:c0+cw]),
                    wait=([(wp, P.val(wp)), (s_ + tagp + "mx", P.val(s_ + tagp + "mx"))]
                          if j >= 2 else [(s_ + tagp + "mx", P.val(s_ + tagp + "mx"))]),
                    inc=((lp, 16),))
                P.rec("vector", lambda e, buf=buf, cw=cw: e.tensor_scalar(
                    buf[:, 0:cw], buf[:, 0:cw], rcol[:, 0:1], RC, ALU.mult, ALU.add),
                    wait=((lp, P.val(lp)),))
                P.rec("vector", lambda e, buf=buf, cw=cw: e.tensor_scalar(
                    buf[:, 0:cw], buf[:, 0:cw], -RC, 127.0, ALU.add, ALU.min))
                P.rec("vector", lambda e, buf=buf, bq=bq, cw=cw: e.tensor_scalar(
                    bq[:, 0:cw], buf[:, 0:cw], -128.0, None, ALU.max),
                    wait=(((wp, P.val(wp)),) if j >= 2 else tuple(prew)),
                    inc=((s_ + tagp + "q", 1),))
                P.rec("gpsimd", lambda e, n=n, c0=c0, cw=cw, bq=bq: e.dma_start(
                    out=wqr[n][:, c0:c0+cw], in_=bq[:, 0:cw]),
                    wait=((s_ + tagp + "q", P.val(s_ + tagp + "q")),), inc=((wp, 16),))
        return [(s_ + tagp + "w0", P.val(s_ + tagp + "w0")),
                (s_ + tagp + "w1", P.val(s_ + tagp + "w1"))]
    gq_done = wq_rows(W["L3_ffn_w_gate"], 2816, DR["wgq"], gsc, "g")
    uq_done = wq_rows(W["L3_ffn_w_up"], 2816, DR["wuq"], usc, "u", prew=gq_done)
    dq_done = wq_rows(W["L3_ffn_w_down"], 1024, DR["wdq"], dsc, "d", prew=uq_done)

    emit_bcast_rows(P, A, [(r_mean, 0)], bcast, onesr, s_ + "m")
    mb = ("bcev", P.val("bcev"))
    P.rec("vector", lambda e: e.memset(acc[:, :], 0.0), inc=((s_ + "am", 1),))
    for kc in range(NKC):
        P.rec("sync", lambda e, kc=kc: e.dma_start(out=xnscr[:, :], in_=xr[kc]),
              wait=([mb] + ([(s_ + "am", P.val(s_ + "am"))] if kc else [])),
              inc=((s_ + "xl", 16),))
        P.rec("vector", lambda e, kc=kc: e.tensor_sub(
            xnscr[:, :], xnscr[:, :], bcast[:, 0:TOK]),
            wait=((s_ + "xl", P.val(s_ + "xl")),))
        P.rec("vector", lambda e: e.scalar_tensor_tensor(
            acc[:, :], xnscr[:, :], 1.0, acc[:, :], ALU.mult, ALU.max))
        P.rec("vector", lambda e: e.scalar_tensor_tensor(
            acc[:, :], xnscr[:, :], -1.0, acc[:, :], ALU.mult, ALU.max),
            inc=((s_ + "am", 1),))
    emit_absmax_tree(P, A, acc, TOK, s_ + "am")
    P.rec("vector", lambda e: e.tensor_mul(rowA[0:1, :], r_rstd[0:1, :], acc[0:1, :]),
          wait=((s_ + "am", P.val(s_ + "am")),))
    P.rec("vector", lambda e: e.reduce_max(scl[0:1, 0:1], rowA[0:1, :], axis=AX.X),
          inc=((s_ + "gm", 1),))
    cc_in = P.nc.dram_tensor("f3_cc_in", [1, 8], F32).ap()
    cc_out = P.nc.dram_tensor("f3_cc_out", [1, 8], F32).ap()
    P.rec("vector", lambda e: e.memset(scl[0:1, 1:8], 0.0), inc=((s_ + "gm", 1),))
    P.rec("sync", lambda e: e.dma_start(out=cc_in, in_=scl[0:1, 0:8]),
          wait=((s_ + "gm", P.val(s_ + "gm")),), inc=((s_ + "cc", 16),))
    P.rec("gpsimd", lambda e: e.collective_compute(
        "AllReduce", ALU.max, replica_groups=RG8,
        ins=[cc_in.tensor.ap().opt()], outs=[cc_out.tensor.ap().opt()]),
        wait=((s_ + "cc", P.val(s_ + "cc")),), inc=(("@f3cc", 1),))
    P.rec("sync", lambda e: e.dma_start(out=scl[0:1, 8:9], in_=cc_out[0:1, 0:1]),
          wait=(("@f3cc", P.val("@f3cc")),), inc=((s_ + "cc", 16),))
    P.rec("scalar", lambda e: e.activation(out=scl[0:1, 13:14], in_=scl[0:1, 14:15],
                                           func=AF.Sigmoid),
          wait=((s_ + "cc", P.val(s_ + "cc")), ("f3w", P.val("f3w"))), inc=((s_ + "sg", 1),))
    P.rec("vector", lambda e: e.tensor_mul(scl[0:1, 9:10], scl[0:1, 8:9], scl[0:1, 13:14]),
          wait=((s_ + "sg", 1),))
    P.rec("vector", lambda e: e.tensor_scalar_mul(scl[0:1, 9:10], scl[0:1, 9:10], 1.0 / 127.0))
    P.rec("vector", lambda e: e.reciprocal(scl[0:1, 10:11], scl[0:1, 9:10]),
          inc=((s_ + "sx", 1),))
    P.rec("vector", lambda e: e.tensor_scalar(rowB[0:1, :], r_rstd[0:1, :],
                                              scl[0:1, 10:11], None, ALU.mult),
          inc=((s_ + "r", 1),))
    P.rec("tensor", lambda e: e.matmul(psc[:, 0:1], onesr[0:1, :], scl[0:1, 9:10],
                                       start=True, stop=True),
          wait=((s_ + "sx", 1), ("bcev", P.val("bcev"))), inc=((s_ + "sxb", 1),))
    P.rec("vector", lambda e: e.tensor_copy(scb[:, 0:1], psc[:, 0:1]),
          wait=((s_ + "sxb", 1),), inc=(("bcev", 1),))
    emit_bcast_rows(P, A, [(rowB, TOK)], bcast, onesr, s_ + "c",
                    extra_wait=((s_ + "r", P.val(s_ + "r")),))
    cb = ("bcev", P.val("bcev"))
    xq = A.sb("xn_bf", [128, NKC * TOK], BF16)
    for kc in range(NKC):
        P.rec("sync", lambda e, kc=kc: e.dma_start(out=xnscr[:, :], in_=xr[kc]),
              wait=([cb] + ([(s_ + "xq", P.val(s_ + "xq"))] if kc else [])),
              inc=((s_ + "xl", 16),))
        P.rec("vector", lambda e, kc=kc: e.tensor_sub(
            xnscr[:, :], xnscr[:, :], bcast[:, 0:TOK]),
            wait=((s_ + "xl", P.val(s_ + "xl")),))
        P.rec("vector", lambda e: e.tensor_mul(xnscr[:, :], xnscr[:, :], bcast[:, TOK:2*TOK]))
        P.rec("vector", lambda e: e.tensor_scalar(
            xnscr[:, :], xnscr[:, :], RC, RC, ALU.add, ALU.subtract))
        P.rec("vector", lambda e, kc=kc: e.tensor_scalar(
            xq[:, kc*TOK:(kc+1)*TOK], xnscr[:, :], 127.0, -128.0, ALU.min, ALU.max),
            inc=((s_ + "xq", 1),))
    xqv = (s_ + "xq", P.val(s_ + "xq"))
    P.rec("vector", lambda e: e.tensor_scalar(gsc[:, :], gsc[:, :], scb[:, 0:1], None,
                                              ALU.mult), wait=(("bcev", P.val("bcev")),))
    P.rec("vector", lambda e: e.tensor_scalar(usc[:, :], usc[:, :], scb[:, 0:1], None,
                                              ALU.mult), inc=((s_ + "sc2", 1),))

    gwin = A.sb("w1win", [128, 2 * 8 * 512], BF16)
    ps_g = A.ps("ps_a", [128, 512], F32)
    ps_u = A.ps("ps_b", [128, 512], F32)
    htile = A.sb("htile", [128, 512], F32)
    htile2 = A.sb("htile2", [128, 512], F32)
    hdr = DR["h3_dram"]
    P.rec("vector", lambda e: e.memset(acc[:, :], 0.0),
          wait=((s_ + "am", P.val(s_ + "am")),), inc=((s_ + "hm", 1),))
    for hw_ in range(11):
        wb = 0 if hw_ % 2 == 0 else 8 * 512
        wv = gq_done + uq_done + [(s_ + "sc2", 1)]
        if hw_ >= 2:
            wv.append((s_ + "hmm", P.val(s_ + "hmm") - 8))
        for dc in range(NKC):
            P.rec("sync", lambda e, hw_=hw_, dc=dc, wb=wb: e.dma_start_transpose(
                out=gwin[:, wb + dc*512: wb + dc*512 + 256],
                in_=DR["wgq"][hw_*256:(hw_+1)*256, dc*128:(dc+1)*128]),
                wait=wv if dc == 0 else (), inc=((s_ + "wl" + str(hw_ % 2), 16),))
            P.rec("sync", lambda e, hw_=hw_, dc=dc, wb=wb: e.dma_start_transpose(
                out=gwin[:, wb + dc*512 + 256: wb + dc*512 + 512],
                in_=DR["wuq"][hw_*256:(hw_+1)*256, dc*128:(dc+1)*128]),
                inc=((s_ + "wl" + str(hw_ % 2), 16),))
        wlv = (s_ + "wl" + str(hw_ % 2), P.val(s_ + "wl" + str(hw_ % 2)))
        for tt in range(4):
            for oc2 in range(2):
                oc = hw_ * 2 + oc2
                i = hw_ * 8 + tt * 2 + oc2
                ht = htile if i % 2 == 0 else htile2
                for kc in range(NKC):
                    wv2 = []
                    if kc == 0:
                        wv2 = [wlv, xqv, (s_ + "hev", P.val(s_ + "hev"))]
                    P.rec("tensor", lambda e, oc2=oc2, tt=tt, kc=kc, wb=wb: e.matmul(
                        ps_g[:, :], gwin[:, wb + kc*512 + oc2*128: wb + kc*512 + oc2*128 + 128],
                        xq[:, kc*TOK + tt*512: kc*TOK + (tt+1)*512],
                        start=(kc == 0), stop=(kc == NKC - 1)), wait=wv2)
                for kc in range(NKC):
                    P.rec("tensor", lambda e, oc2=oc2, tt=tt, kc=kc, wb=wb: e.matmul(
                        ps_u[:, :], gwin[:, wb + kc*512 + 256 + oc2*128: wb + kc*512 + 256 + oc2*128 + 128],
                        xq[:, kc*TOK + tt*512: kc*TOK + (tt+1)*512],
                        start=(kc == 0), stop=(kc == NKC - 1)),
                        inc=((s_ + "hmm", 1),) if kc == NKC - 1 else ())
                hwp = s_ + "hw" + str(i % 2)
                P.rec("vector", lambda e, oc=oc, ht=ht: e.tensor_scalar(
                    ht[:, :], ps_g[:, :], gsc[:, oc:oc+1], None, ALU.mult),
                    wait=((s_ + "hmm", P.val(s_ + "hmm")), (hwp, P.val(hwp))),
                    inc=((s_ + "hts", 1),))
                P.rec("scalar", lambda e, ht=ht: e.activation(
                    out=ht[:, :], in_=ht[:, :], func=AF.Silu),
                    wait=((s_ + "hts", P.val(s_ + "hts")),),
                    inc=((s_ + "hsl", 1),))
                P.rec("vector", lambda e, oc=oc, ht=ht: e.scalar_tensor_tensor(
                    ht[:, :], ps_u[:, :], usc[:, oc:oc+1], ht[:, :], ALU.mult, ALU.mult),
                    wait=((s_ + "hsl", P.val(s_ + "hsl")),), inc=((s_ + "hev", 1),))
                P.rec("vector", lambda e, ht=ht: e.scalar_tensor_tensor(
                    acc[:, tt*512:(tt+1)*512], ht[:, :], 1.0, acc[:, tt*512:(tt+1)*512],
                    ALU.mult, ALU.max), wait=((s_ + "hm", 1),))
                P.rec("vector", lambda e, ht=ht: e.scalar_tensor_tensor(
                    acc[:, tt*512:(tt+1)*512], ht[:, :], -1.0, acc[:, tt*512:(tt+1)*512],
                    ALU.mult, ALU.max), inc=((s_ + "hcm", 1),))
                P.rec("gpsimd", lambda e, oc=oc, tt=tt, ht=ht: e.dma_start(
                    out=hdr[oc*128:(oc+1)*128, tt*512:(tt+1)*512], in_=ht[:, :]),
                    wait=((s_ + "hcm", P.val(s_ + "hcm")),),
                    inc=((s_ + "hw" + str(i % 2), 16),))
    hwv = [(s_ + "hw0", P.val(s_ + "hw0")), (s_ + "hw1", P.val(s_ + "hw1"))]
    P.rec("vector", lambda e: e.engine_nop(), wait=((s_ + "hcm", P.val(s_ + "hcm")),),
          inc=((s_ + "hm", 1),))
    emit_absmax_tree(P, A, acc, TOK, s_ + "hm")
    P.rec("vector", lambda e: e.reduce_max(scl[0:1, 0:1], acc[0:1, :], axis=AX.X),
          wait=((s_ + "hm", P.val(s_ + "hm")),))
    P.rec("vector", lambda e: e.memset(scl[0:1, 1:8], 0.0), inc=((s_ + "gm2", 1),))
    P.rec("sync", lambda e: e.dma_start(out=cc_in, in_=scl[0:1, 0:8]),
          wait=((s_ + "gm2", P.val(s_ + "gm2")),), inc=((s_ + "cc", 16),))
    P.rec("gpsimd", lambda e: e.collective_compute(
        "AllReduce", ALU.max, replica_groups=RG8,
        ins=[cc_in.tensor.ap().opt()], outs=[cc_out.tensor.ap().opt()]),
        wait=((s_ + "cc", P.val(s_ + "cc")),), inc=(("@f3cc", 1),))
    P.rec("sync", lambda e: e.dma_start(out=scl[0:1, 8:9], in_=cc_out[0:1, 0:1]),
          wait=(("@f3cc", P.val("@f3cc")),), inc=((s_ + "cc", 16),))
    P.rec("scalar", lambda e: e.activation(out=scl[0:1, 13:14], in_=scl[0:1, 15:16],
                                           func=AF.Sigmoid),
          wait=((s_ + "cc", P.val(s_ + "cc")),), inc=((s_ + "sg", 1),))
    P.rec("vector", lambda e: e.tensor_mul(scl[0:1, 11:12], scl[0:1, 8:9], scl[0:1, 13:14]),
          wait=((s_ + "sg", P.val(s_ + "sg")),))
    P.rec("vector", lambda e: e.tensor_scalar_mul(scl[0:1, 11:12], scl[0:1, 11:12], 1.0 / 127.0))
    P.rec("vector", lambda e: e.reciprocal(scl[0:1, 12:13], scl[0:1, 11:12]),
          inc=((s_ + "sh", 1),))
    P.rec("tensor", lambda e: e.matmul(psc[:, 0:1], onesr[0:1, :], scl[0:1, 11:12],
                                       start=True, stop=True),
          wait=((s_ + "sh", 1), ("bcev", P.val("bcev"))), inc=((s_ + "shb", 1),))
    P.rec("vector", lambda e: e.tensor_copy(scb[:, 2:3], psc[:, 0:1]),
          wait=((s_ + "shb", 1),), inc=(("bcev", 1),))
    P.rec("tensor", lambda e: e.matmul(psc[:, 0:1], onesr[0:1, :], scl[0:1, 12:13],
                                       start=True, stop=True),
          wait=(("bcev", P.val("bcev")),), inc=((s_ + "shb", 1),))
    P.rec("vector", lambda e: e.tensor_copy(scb[:, 3:4], psc[:, 0:1]),
          wait=((s_ + "shb", 2),), inc=(("bcev", 1),))
    P.rec("vector", lambda e: e.tensor_scalar(dsc[:, :], dsc[:, :], scb[:, 2:3], None,
                                              ALU.mult), wait=(("bcev", P.val("bcev")),),
          inc=((s_ + "sc3", 1),))

    xq2 = A.sb("xn_bf", [128, NKC * TOK], BF16)
    gdr = DR["g_dram"]
    hsq = A.sb("hsq", [128, 512], F32)
    pbanks = [A.ps(t, [128, 512], F32) for t in
              ("ptb0", "ptb1", "ptb2", "ptb3", "ps_a", "ps_b", "ps_bc", "ps_c")]
    dwin = A.sb("w1win", [128, 2 * 8 * 512], BF16)
    for tt in range(4):
        for hc in range(22):
            ht = htile if hc % 2 == 0 else htile2
            wv = hwv[:]
            if hc < 2:
                wv.append((s_ + "gmm", P.val(s_ + "gmm")))
                wv.append((s_ + "gs", P.val(s_ + "gs")))
            else:
                wv.append((s_ + "q2", P.val(s_ + "q2") - 1))
            hlp = s_ + "hl" + str(hc % 2)
            P.rec("sync", lambda e, tt=tt, hc=hc, ht=ht: e.dma_start(
                out=ht[:, :], in_=hdr[hc*128:(hc+1)*128, tt*512:(tt+1)*512]),
                wait=wv, inc=((hlp, 16),))
            P.rec("vector", lambda e, ht=ht: e.tensor_scalar(
                ht[:, :], ht[:, :], scb[:, 3:4], RC, ALU.mult, ALU.add),
                wait=((hlp, P.val(hlp)), ("bcev", P.val("bcev")) if (tt == 0 and hc == 0) else (s_ + "sh", 1)))
            P.rec("vector", lambda e, ht=ht: e.tensor_scalar(
                ht[:, :], ht[:, :], -RC, 127.0, ALU.add, ALU.min))
            P.rec("vector", lambda e, hc=hc, ht=ht: e.tensor_scalar(
                xq2[:, hc*512:(hc+1)*512], ht[:, :], -128.0, None, ALU.max),
                inc=((s_ + "q2", 1),))
        q2v = (s_ + "q2", P.val(s_ + "q2"))
        for hw_ in range(11):
            wb = 0 if hw_ % 2 == 0 else 8 * 512
            wv = dq_done[:]
            if hw_ >= 2:
                wv.append((s_ + "gwin", P.val(s_ + "gwin") - 1))
            for hh in range(2):
                hc = hw_ * 2 + hh
                P.rec("sync", lambda e, hc=hc, hh=hh, wb=wb: e.dma_start_transpose(
                    out=dwin[:, wb + hh*1024: wb + (hh+1)*1024],
                    in_=DR["wdq"][0:1024, hc*128:(hc+1)*128]),
                    wait=wv if hh == 0 else (), inc=((s_ + "w2l" + str(hw_ % 2), 16),))
            w2lv = (s_ + "w2l" + str(hw_ % 2), P.val(s_ + "w2l" + str(hw_ % 2)))
            for hh in range(2):
                hc = hw_ * 2 + hh
                for oc in range(8):
                    wv2 = []
                    if hh == 0 and oc == 0:
                        wv2 = [w2lv, q2v, (s_ + "pse", P.val(s_ + "pse"))]
                    elif hh == 0:
                        wv2 = [w2lv]
                    inc = []
                    if hc == 21:
                        inc.append((s_ + "gmm", 1))
                    if hh == 1 and oc == 7:
                        inc.append((s_ + "gwin", 1))
                    P.rec("tensor", lambda e, oc=oc, hc=hc, hh=hh, wb=wb: e.matmul(
                        pbanks[oc][:, :], dwin[:, wb + hh*1024 + oc*128: wb + hh*1024 + oc*128 + 128],
                        xq2[:, hc*512:(hc+1)*512],
                        start=(hc == 0), stop=(hc == 21)),
                        wait=wv2, inc=inc)
        gmmv = (s_ + "gmm", P.val(s_ + "gmm"))
        for oc in range(8):
            P.rec("vector", lambda e, tt=tt, oc=oc: e.tensor_scalar(
                htile[:, :], pbanks[oc][:, :], dsc[:, oc:oc+1], None, ALU.mult),
                wait=(gmmv, (s_ + "gs", P.val(s_ + "gs")), (s_ + "sc3", 1)),
                inc=((s_ + "pse", 1),))
            P.rec("scalar", lambda e: e.activation(out=hsq[:, :], in_=htile[:, :], func=AF.Silu),
                wait=((s_ + "gw", P.val(s_ + "gw")), (s_ + "pse", P.val(s_ + "pse"))),
                inc=((s_ + "gs", 1),))
            P.rec("gpsimd", lambda e, oc=oc, tt=tt: e.dma_start(
                out=gdr[oc*128:(oc+1)*128, tt*512:(tt+1)*512], in_=hsq[:, :]),
                wait=((s_ + "gs", P.val(s_ + "gs")),),
                inc=((s_ + "gw", 16),))
    return (s_ + "gw", P.val(s_ + "gw"))

def make_inputs_l1f(I, core, x_cur):
    o = make_consts()
    o["coresel"] = np.zeros((128, 8), np.float32)
    xs = x_cur[:, 1024*core:1024*(core+1), :].reshape(TOK, D)
    o["x_in"] = np.ascontiguousarray(xs)
    o["L1_ffn_w1"] = I["L1_ffn_w1"]
    cw = I["L1_ffn_conv_w"].reshape(32, 128, 3)
    o["L1_ffn_conv_w"] = np.ascontiguousarray(np.transpose(cw, (1, 2, 0)).reshape(128, 96))
    o["L1_ffn_conv_b"] = colsify(I["L1_ffn_conv_b"])
    o["L1_ffn_w2"] = I["L1_ffn_w2"]
    p = np.arange(128)
    o["prevrows"] = (((core - 1) * 128 + p) if core > 0 else np.full(128, 99999)).reshape(128, 1).astype(np.int32)
    o["nextrows"] = (((core + 1) * 128 + p) if core < 7 else np.full(128, 99999)).reshape(128, 1).astype(np.int32)
    return {k: np.ascontiguousarray(v) for k, v in o.items()}

def make_inputs_l3f(I, core, x_cur):
    o = make_consts()
    o["coresel"] = np.zeros((128, 8), np.float32)
    xs = x_cur[:, 1024*core:1024*(core+1), :].reshape(TOK, D)
    o["x_in"] = np.ascontiguousarray(xs)
    o["L3_ffn_clips"] = np.array([[I["L3_ffn_clip_x"][0], I["L3_ffn_clip_down"][0]]], np.float32)
    o["L3_ffn_w_gate"] = I["L3_ffn_w_gate"]
    o["L3_ffn_w_up"] = I["L3_ffn_w_up"]
    o["L3_ffn_w_down"] = I["L3_ffn_w_down"]
    return {k: np.ascontiguousarray(v) for k, v in o.items()}


def emit_l0_attn(P, A, xn, lnxn, consts, W, DR):
    s_ = "a0_"
    onesc = A.sb("ones_col", [128, 1], F32)
    onesr = A.sb("ones_row", [1, 128], F32)
    identb = A.sb("ident_bf", [128, 128], BF16)
    wq_sb = A.sb("a0_wqkv", [128, NKC * 3072], BF16)
    wo_sb = A.sb("a0_wout", [128, NKC * 1024], BF16)
    bqc = A.sb("a0_bq", [64, 48], F32)
    boc = A.sb("a0_bo", [128, 8], F32)
    qf = A.sb("a0_q", [64, 8192], BF16)
    kf = A.sb("a0_k", [64, 8192], BF16)
    vf = A.sb("a0_v", [64, 8192], BF16)
    vext = A.sb("a0_vext", [128, 64 * 65], BF16)
    masks = A.sb("a0_masks", [128, 4 * 512], BF16)
    ptile = A.sb("a0_pt", [128, 2 * 512], BF16)
    sums_sb = A.sb("a0_sums", [128, 512], F32)
    rb_sb = A.sb("a0_rb", [64, 512], F32)
    bcol = A.sb("a0_bcol", [128, 8], F32)
    osb = A.sb("a0_osb", [64, 512], BF16)
    safm = A.sb("a0_safm", [128, 8 * 512], BF16)
    sqs = A.sb("a0_sqs", [64, 512], F32)
    sstg0 = A.sb("a0_sstg", [128, 512], BF16)
    ps_s1 = A.ps("ps_a", [128, 512], F32)
    ps_s2 = A.ps("ps_b", [128, 512], F32)
    ps_o = A.ps("ps_c", [128, 512], F32)
    ps_w = A.ps("ps_bc", [128, 512], F32)
    ps_t = A.ps("ptb0", [128, 512], F32)
    pts = ptbanks(A)
    odr = DR["o0_dram"]
    sdr = DR["s_dram"]
    SC = float(1.0 / np.sqrt(2048.0))

    P.rec("sync", lambda e: e.dma_start(out=bqc[:, :], in_=W["L0_bq_cols"]), inc=((s_+"w", 16),))
    P.rec("sync", lambda e: e.dma_start(out=boc[:, :], in_=W["L0_bo_cols"]), inc=((s_+"w", 16),))
    P.rec("gpsimd", lambda e: e.dma_start(out=masks[:, :], in_=W["L0_masks"]), inc=((s_+"wg", 16),))
    wqr = W["L0_attn_W_qkv"].rearrange("(kc p) (j r c) -> kc p j r c", p=128, j=16, r=3)
    for kc in range(NKC):
        for role in range(3):
            P.rec("gpsimd", lambda e, kc=kc, role=role: e.dma_start(
                out=wq_sb[:, kc*3072 + role*1024: kc*3072 + (role+1)*1024]
                    .rearrange("p (j c) -> p j c", j=16),
                in_=wqr[kc][:, :, role, :]),
                inc=((s_+"wg", 16),))
        P.rec("gpsimd", lambda e, kc=kc: e.dma_start(
            out=wo_sb[:, kc*1024:(kc+1)*1024],
            in_=W["L0_attn_W_out"].rearrange("(kc p) o -> kc p o", p=128)[kc]),
            inc=((s_+"wg", 16),))
    wv0 = [(s_+"w", P.val(s_+"w")), (s_+"wg", P.val(s_+"wg")), lnxn]

    for inst in range(4):
        b, hl = inst % 2, inst // 2
        col0 = b * TLOC + hl * 512
        for role, dst in ((0, qf), (1, kf), (2, vf)):
            for jp in range(8):
                for kc in range(NKC):
                    def mm(e, role=role, jp=jp, kc=kc, col0=col0):
                        lhs = wq_sb[:, kc*3072 + role*1024 + jp*128:
                                    kc*3072 + role*1024 + (jp+1)*128]
                        return e.matmul(ps_w[:, :], lhs,
                                        xn[:, kc*TOK + col0: kc*TOK + col0 + 512],
                                        start=(kc == 0), stop=(kc == NKC - 1))
                    P.rec("tensor", mm,
                          wait=(wv0 + [(s_+"pe", P.val(s_+"pe"))]) if kc == 0 else (),
                          inc=((s_+"pm", 1),) if kc == NKC - 1 else ())
                for jj in range(2):
                    j = 2 * jp + jj
                    P.rec("scalar", lambda e, dst=dst, j=j, jj=jj, role=role: e.activation(
                        out=dst[0:64, j:8192:16], in_=ps_w[jj*64:(jj+1)*64, :],
                        func=AF.Identity, bias=bqc[:, j*3 + role: j*3 + role + 1]),
                        wait=((s_+"pm", P.val(s_+"pm")),),
                        inc=((s_+"pe", 1),))
        pev = (s_+"pe", P.val(s_+"pe"))
        P.rec("vector", lambda e: e.memset(vext[:, :], 1.0), wait=(pev,), inc=((s_+"vx", 1),))
        for kt in range(64):
            slot = 1 + kt % 3
            w = [pev, (s_+"vx", P.val(s_+"vx"))]
            if kt >= 3:
                w.append((s_+"vte", P.val(s_+"vte") - 2))
            P.rec("tensor", lambda e, kt=kt, slot=slot: e.transpose(
                pts[slot][:, 0:32].bitcast(BF16), vf[0:64, kt*128:(kt+1)*128],
                identb[0:64, 0:64]), wait=w, inc=((s_+"vt", 1),))
            P.rec("vector", lambda e, kt=kt, slot=slot: e.tensor_copy(
                vext[:, kt*65:kt*65+64], pts[slot][:, 0:32].bitcast(BF16)),
                wait=((s_+"vt", P.val(s_+"vt")),), inc=((s_+"vte", 1),))
        vxv = (s_+"vte", P.val(s_+"vte"))
        for src_t, colix in ((qf, 0), (kf, 1)):
            for tt2 in range(16):
                P.rec("scalar", lambda e, src_t=src_t, tt2=tt2: e.activation(
                    out=sqs[:, :], in_=src_t[0:64, tt2*512:(tt2+1)*512], func=AF.Square),
                    wait=(pev, (s_+"nsm", P.val(s_+"nsm"))), inc=((s_+"sq", 1),))
                P.rec("tensor", lambda e: e.matmul(
                    ps_t[0:1, :], onesc[0:64, :], sqs[:, :],
                    start=True, stop=True),
                    wait=((s_+"sq", P.val(s_+"sq")), (s_+"nse", P.val(s_+"nse"))),
                    inc=((s_+"nsm", 1),))
                if tt2 == 0:
                    P.rec("vector", lambda e, colix=colix: e.reduce_max(
                        bcol[0:1, colix:colix+1], ps_t[0:1, :], axis=AX.X),
                        wait=((s_+"nsm", P.val(s_+"nsm")),), inc=((s_+"nse", 1),))
                else:
                    P.rec("vector", lambda e: e.reduce_max(
                        bcol[0:1, 4:5], ps_t[0:1, :], axis=AX.X),
                        wait=((s_+"nsm", P.val(s_+"nsm")),), inc=((s_+"nse", 1),))
                    P.rec("vector", lambda e, colix=colix: e.tensor_max(
                        bcol[0:1, colix:colix+1], bcol[0:1, colix:colix+1], bcol[0:1, 4:5]))
            P.rec("vector", lambda e: e.nop(), inc=((s_+"bb", 1),))
        P.rec("vector", lambda e: e.tensor_mul(bcol[0:1, 2:3], bcol[0:1, 0:1], bcol[0:1, 1:2]),
              wait=((s_+"bb", P.val(s_+"bb")),), inc=((s_+"bm", 1),))
        P.rec("scalar", lambda e: e.activation(out=bcol[0:1, 2:3], in_=bcol[0:1, 2:3], func=AF.Sqrt),
              wait=((s_+"bm", P.val(s_+"bm")),), inc=((s_+"bs", 1),))
        P.rec("vector", lambda e: e.tensor_scalar_mul(bcol[0:1, 2:3], bcol[0:1, 2:3], -SC),
              wait=((s_+"bs", P.val(s_+"bs")),), inc=((s_+"bb", 1),))
        P.rec("tensor", lambda e: e.matmul(ps_t[:, 0:1], onesr[0:1, :], bcol[0:1, 2:3],
                                           start=True, stop=True),
              wait=((s_+"bb", P.val(s_+"bb")), (s_+"nse", P.val(s_+"nse"))), inc=((s_+"nsm", 1),))
        P.rec("vector", lambda e: e.tensor_copy(bcol[:, 3:4], ps_t[:, 0:1]),
              wait=((s_+"nsm", P.val(s_+"nsm")),), inc=((s_+"bc", 1),))
        bcv = (s_+"bc", P.val(s_+"bc"))

        for qg in range(16):
            nkt = 4 * qg + 4
            for kt in range(nkt):
                par = kt % 2
                pss = ps_s1 if par == 0 else ps_s2
                P.rec("tensor", lambda e, qg=qg, kt=kt, pss=pss: e.matmul(
                    pss[:, :], kf[0:64, kt*128:(kt+1)*128], qf[0:64, qg*512:(qg+1)*512],
                    start=True, stop=True),
                    wait=((s_+"px", P.val(s_+"px") - 1),),
                    inc=((s_+"sm", 1),))
                P.rec("scalar", lambda e, par=par, pss=pss: e.activation(
                    out=ptile[:, par*512:(par+1)*512], in_=pss[:, :],
                    func=AF.Exp, bias=bcol[:, 3:4], scale=SC),
                    wait=((s_+"sm", P.val(s_+"sm")), bcv,
                          (s_+"pv", P.val(s_+"pv") - 1)),
                    inc=((s_+"px", 1),))
                diag = kt - 4 * qg
                if diag >= 0:
                    P.rec("vector", lambda e, par=par, diag=diag: e.tensor_mul(
                        ptile[:, par*512:(par+1)*512], ptile[:, par*512:(par+1)*512],
                        masks[:, diag*512:(diag+1)*512]),
                        wait=((s_+"px", P.val(s_+"px")), (s_+"w", P.val(s_+"w"))),
                        inc=((s_+"mk", 1),))
                    pvw = [(s_+"mk", P.val(s_+"mk"))]
                else:
                    pvw = [(s_+"px", P.val(s_+"px"))]
                P.rec("tensor", lambda e, kt=kt, par=par, nkt=nkt: e.matmul(
                    ps_o[0:65, :], vext[:, kt*65:(kt+1)*65], ptile[:, par*512:(par+1)*512],
                    start=(kt == 0), stop=(kt == nkt - 1)),
                    wait=pvw, inc=((s_+"pv", 1),))
            P.rec("vector", lambda e: e.tensor_copy(sums_sb[64:65, :], ps_o[64:65, :]),
                  wait=((s_+"pv", P.val(s_+"pv")), (s_+"ow", P.val(s_+"ow"))),
                  inc=((s_+"su", 1),))
            P.rec("sync", lambda e: e.dma_start(out=sums_sb[0:1, :], in_=sums_sb[64:65, :]),
                  wait=((s_+"su", P.val(s_+"su")),), inc=((s_+"sd", 16),))
            P.rec("vector", lambda e: e.reciprocal(sums_sb[0:1, :], sums_sb[0:1, :]),
                  wait=((s_+"sd", P.val(s_+"sd")),), inc=((s_+"rc", 1),))
            P.rec("tensor", lambda e: e.matmul(ps_w[0:64, :], onesr[0:1, 0:64],
                                               sums_sb[0:1, :], start=True, stop=True),
                  wait=((s_+"rc", P.val(s_+"rc")), (s_+"rbe", P.val(s_+"rbe"))),
                  inc=((s_+"nsm", 1),))
            P.rec("vector", lambda e: e.tensor_copy(rb_sb[:, :], ps_w[0:64, :]),
                  wait=((s_+"nsm", P.val(s_+"nsm")),), inc=((s_+"rbe", 1),))
            P.rec("vector", lambda e: e.tensor_mul(osb[:, :], ps_o[0:64, :], rb_sb[:, :]),
                  wait=((s_+"rbe", P.val(s_+"rbe")),), inc=((s_+"oe", 1),))
            for i4 in range(4):
                slot = 1 + i4 % 3
                P.rec("tensor", lambda e, i4=i4, slot=slot: e.transpose(
                    pts[slot][:, 32:64].bitcast(BF16)[:, 0:64], osb[:, i4*128:(i4+1)*128],
                    identb[0:64, 0:64]),
                    wait=((s_+"oe", P.val(s_+"oe")), (s_+"oc", P.val(s_+"oc") - 2)),
                    inc=((s_+"ot", 1),))
                P.rec("vector", lambda e, slot=slot, i4=i4: e.tensor_copy(
                    sums_sb[:, i4*32:(i4+1)*32].bitcast(BF16), pts[slot][:, 32:64].bitcast(BF16)[:, 0:64]),
                    wait=((s_+"ot", P.val(s_+"ot")),), inc=((s_+"oc", 1),))
                P.rec("sync", lambda e, inst=inst, qg=qg, i4=i4: e.dma_start(
                    out=odr[inst*8192 + qg*512 + i4*128: inst*8192 + qg*512 + (i4+1)*128, :],
                    in_=sums_sb[:, i4*32:(i4+1)*32].bitcast(BF16)),
                    wait=((s_+"oc", P.val(s_+"oc")),), inc=((s_+"ow", 16),))
    owv = (s_+"ow", P.val(s_+"ow"))
    for inst in range(4):
        b, hl = inst % 2, inst // 2
        col0 = b * TLOC + hl * 512
        sav = odr[inst*8192:(inst+1)*8192, :].rearrange("(u f) c -> u (f c)", f=16)
        for kc in range(NKC):
            P.rec("sync", lambda e, kc=kc, sav=sav: e.dma_start_transpose(
                out=safm[:, kc*512:(kc+1)*512], in_=sav[:, kc*128:(kc+1)*128]),
                wait=([owv, (s_+"smm", P.val(s_+"smm"))] if kc == 0 else ()),
                inc=((s_+"sf", 16),))
        sfv = (s_+"sf", P.val(s_+"sf"))
        for oc in range(8):
            for kc in range(NKC):
                P.rec("tensor", lambda e, oc=oc, kc=kc: e.matmul(
                    ps_o[:, :], wo_sb[:, kc*1024 + oc*128: kc*1024 + oc*128 + 128],
                    safm[:, kc*512:(kc+1)*512],
                    start=(kc == 0), stop=(kc == NKC - 1)),
                    wait=([sfv, (s_+"se", P.val(s_+"se"))] if kc == 0 else ()),
                    inc=((s_+"smm", 1),) if kc == NKC - 1 else ())
            P.rec("scalar", lambda e, oc=oc: e.activation(
                out=sstg0[:, :], in_=ps_o[:, :], func=AF.Identity, bias=boc[:, oc:oc+1]),
                wait=((s_+"smm", P.val(s_+"smm")), (s_+"sw", P.val(s_+"sw"))),
                inc=((s_+"se", 1),))
            P.rec("sync", lambda e, oc=oc, col0=col0: e.dma_start(
                out=sdr[oc*128:(oc+1)*128, col0:col0+512],
                in_=sstg0[:, :]),
                wait=((s_+"se", P.val(s_+"se")),), inc=((s_+"sw", 16),))
    return (s_+"sw", P.val(s_+"sw"))

def make_inputs_l0a(I, core, x_cur):
    o = make_consts()
    o["coresel"] = np.zeros((128, 8), np.float32)
    xs = x_cur[:, 1024*core:1024*(core+1), :].reshape(TOK, D)
    o["x_in"] = np.ascontiguousarray(xs)
    o["L0_attn_W_qkv"] = I["L0_attn_W_qkv"]
    o["L0_attn_W_out"] = I["L0_attn_W_out"]
    bq = I["L0_attn_b_qkv"].reshape(16, 3, 64)
    o["L0_bq_cols"] = np.ascontiguousarray(np.transpose(bq, (2, 0, 1)).reshape(64, 48))
    o["L0_bo_cols"] = colsify(I["L0_attn_b_out"])
    q = np.arange(512)[None, :]
    k = np.arange(128)[:, None]
    m = np.concatenate([(q - k - 128*r >= 0).astype(np.float32) for r in range(4)], axis=1)
    o["L0_masks"] = np.ascontiguousarray(m)
    return {k2: np.ascontiguousarray(v) for k2, v in o.items()}

def mkap(base_ap, offset, pairs):
    c = base_ap.copy()
    c.offset = offset
    c.ap = bass_rust.VecI64Pair(pairs)
    return c

def emit_l3_attn(P, A, xn, lnxn, W, DR):
    s_ = "a3_"
    onesc = A.sb("ones_col", [128, 1], F32)
    onesr = A.sb("ones_row", [1, 128], F32)
    identb = A.sb("ident_bf", [128, 128], BF16)
    flipb = A.sb("a3_flip", [128, 128], BF16)
    wq3 = A.sb("a3_wqkv", [128, NKC * 3072], BF16)
    wo3 = A.sb("a3_wout", [128, NKC * 1024], BF16)
    bq3 = A.sb("a3_bq", [128, 24], F32)
    bo3 = A.sb("a3_bo", [128, 8], F32)
    w1r = A.sb("a3_w1r", [1, 32], F32)
    b1c = A.sb("a3_b1c", [32, 1], F32)
    w2r = A.sb("a3_w2r", [32, 16], F32)
    b2c = A.sb("a3_b2c", [16, 1], F32)
    relr = A.sb("a3_rel", [1, 1536], F32)
    hjt = A.sb("a3_hjt", [32, 512], F32)
    cvs = A.sb("a3_cvs", [16, 512], F32)
    ecs = A.sb("a3_ecs", [16, 512], BF16)
    cvmax = A.sb("a3_cvm", [16, 2], F32)
    A.push()
    xnR = A.sb("a3_xnR", [128, NKC * TOK], BF16)
    xsc = A.sb("a3_xsc", [128, 2 * 128], BF16)
    pstg = A.sb("a3_pstg", [128, 2 * 512], BF16)
    vtm = A.sb("a3_vtm", [128, 4 * 128], BF16)
    vstg = A.sb("a3_vstg", [128, 512], BF16)
    ps_s1 = A.ps("ps_a", [128, 512], F32)
    ps_s2 = A.ps("ps_b", [128, 512], F32)
    ps_o = A.ps("ps_c", [128, 512], F32)
    ps_w = A.ps("ps_bc", [128, 512], F32)
    ps_t = A.ps("ptb0", [128, 512], F32)
    pts = ptbanks(A)
    SC = 0.125
    qloc, kloc, vloc = DR["qloc3"], DR["kloc3"], DR["vloc3"]
    agk, agv = DR["agk3"], DR["agv3"]
    cvd, ecvd, sad = DR["cv3"], DR["ecv3"], DR["sa3"]
    sdr = DR["s_dram"]

    P.rec("gpsimd", lambda e: e.dma_start(out=flipb[:, :], in_=W["flip_f32"]), inc=((s_+"wg", 16),))
    for kc in range(NKC):
        P.rec("gpsimd", lambda e, kc=kc: e.dma_start(
            out=wq3[:, kc*3072:(kc+1)*3072],
            in_=W["L3_attn_w_qkv"].rearrange("(kc p) o -> kc p o", p=128)[kc]),
            inc=((s_+"wg", 16),))
        P.rec("gpsimd", lambda e, kc=kc: e.dma_start(
            out=wo3[:, kc*1024:(kc+1)*1024],
            in_=W["L3_attn_w_out"].rearrange("(kc p) o -> kc p o", p=128)[kc]),
            inc=((s_+"wg", 16),))
    for nm, dst in (("L3_bq_cols", bq3), ("L3_bo_cols", bo3), ("L3_w1rpe", w1r),
                    ("L3_b1rpe", b1c), ("L3_w2rpe", w2r), ("L3_b2rpe", b2c)):
        P.rec("sync", lambda e, nm=nm, dst=dst: e.dma_start(out=dst[:, :], in_=W[nm]),
              inc=((s_+"w", 16),))
    wv0 = [(s_+"w", P.val(s_+"w")), (s_+"wg", P.val(s_+"wg")), lnxn]

    for cc in range(6):
        P.rec("sync", lambda e, cc=cc: e.dma_start(
            out=relr[0:1, :], in_=W["L3_relrow"][0:1, cc*1536:(cc+1)*1536]),
            wait=[(s_+"w", P.val(s_+"w")), (s_+"rm", P.val(s_+"rm"))],
            inc=((s_+"rl", 16),))
        for tt in range(3):
            ci = cc * 3 + tt
            P.rec("tensor", lambda e, tt=tt: e.matmul(
                ps_w[0:32, :], w1r[0:1, :], relr[0:1, tt*512:(tt+1)*512],
                start=True, stop=True),
                wait=[(s_+"rl", P.val(s_+"rl")), (s_+"rpe", P.val(s_+"rpe"))],
                inc=((s_+"rm", 1),))
            P.rec("scalar", lambda e: e.activation(
                out=hjt[:, :], in_=ps_w[0:32, :], func=AF.Relu, bias=b1c[:, 0:1]),
                wait=((s_+"rm", P.val(s_+"rm")), (s_+"rm2", P.val(s_+"rm2"))),
                inc=((s_+"rpe", 1),))
            P.rec("tensor", lambda e: e.matmul(
                ps_t[0:16, :], w2r[:, :], hjt[:, :], start=True, stop=True),
                wait=[(s_+"rpe", P.val(s_+"rpe")), (s_+"cve", P.val(s_+"cve"))],
                inc=((s_+"rm2", 1),))
            P.rec("scalar", lambda e: e.activation(
                out=cvs[:, :], in_=ps_t[0:16, :], func=AF.Identity, bias=b2c[:, 0:1]),
                wait=((s_+"rm2", P.val(s_+"rm2")), (s_+"cvw", P.val(s_+"cvw")),
                      (s_+"cmx", P.val(s_+"cmx"))),
                inc=((s_+"cve", 1),))
            if ci == 0:
                P.rec("vector", lambda e: e.reduce_max(cvmax[:, 0:1], cvs[:, :], axis=AX.X),
                      wait=((s_+"cve", P.val(s_+"cve")),), inc=((s_+"cmx", 1),))
            else:
                P.rec("vector", lambda e: e.reduce_max(cvmax[:, 1:2], cvs[:, :], axis=AX.X),
                      wait=((s_+"cve", P.val(s_+"cve")),), inc=((s_+"cmx", 1),))
                P.rec("vector", lambda e: e.tensor_max(cvmax[:, 0:1], cvmax[:, 0:1], cvmax[:, 1:2]))
            P.rec("sync", lambda e, ci=ci: e.dma_start(
                out=cvd[:, ci*512:(ci+1)*512], in_=cvs[:, :]),
                wait=((s_+"cve", P.val(s_+"cve")),), inc=((s_+"cvw", 16),))
    P.rec("vector", lambda e: e.tensor_scalar_mul(cvmax[:, 0:1], cvmax[:, 0:1], -1.0),
          inc=((s_+"cm", 1),))
    cmv = (s_+"cm", P.val(s_+"cm"))
    for ci in range(18):
        P.rec("sync", lambda e, ci=ci: e.dma_start(
            out=cvs[:, :], in_=cvd[:, ci*512:(ci+1)*512]),
            wait=[(s_+"cvw", P.val(s_+"cvw")), (s_+"exe", P.val(s_+"exe")),
                  (s_+"cve", P.val(s_+"cve")), (s_+"cmx", P.val(s_+"cmx"))],
            inc=((s_+"cvr", 16),))
        P.rec("scalar", lambda e: e.activation(
            out=ecs[:, :], in_=cvs[:, :], func=AF.Exp, bias=cvmax[:, 0:1]),
            wait=((s_+"cvr", P.val(s_+"cvr")), cmv, (s_+"ecw", P.val(s_+"ecw"))),
            inc=((s_+"exe", 1),))
        P.rec("sync", lambda e, ci=ci: e.dma_start(
            out=ecvd[:, ci*512:(ci+1)*512], in_=ecs[:, :]),
            wait=((s_+"exe", P.val(s_+"exe")),), inc=((s_+"ecw", 16),))
    ecvv = (s_+"ecw", P.val(s_+"ecw"))

    for kc in range(NKC):
        for b2 in range(2):
            for i in range(8):
                n = (kc * 2 + b2) * 8 + i
                s1 = 1 + (2 * n) % 3
                s2 = 1 + (2 * n + 1) % 3
                par = n % 2
                w = list(wv0) if n == 0 else []
                if n >= 2:
                    w.append((s_+"xte", P.val(s_+"xte") - 1))
                P.rec("tensor", lambda e, kc=kc, b2=b2, i=i, s1=s1: e.transpose(
                    pts[s1][:, 0:64].bitcast(BF16),
                    xn[:, kc*TOK + b2*1024 + i*128: kc*TOK + b2*1024 + (i+1)*128],
                    identb[:, :]), wait=w, inc=((s_+"xt", 1),))
                P.rec("vector", lambda e, par=par, s1=s1: e.tensor_copy(
                    xsc[:, par*128:(par+1)*128], pts[s1][:, 0:64].bitcast(BF16)),
                    wait=((s_+"xt", P.val(s_+"xt")), (s_+"xt2", P.val(s_+"xt2"))),
                    inc=((s_+"xte", 1),))
                P.rec("tensor", lambda e, par=par, s2=s2: e.transpose(
                    pts[s2][:, 0:64].bitcast(BF16), xsc[:, par*128:(par+1)*128],
                    flipb[:, :]),
                    wait=((s_+"xte", P.val(s_+"xte")), (s_+"xre", P.val(s_+"xre") - 1)),
                    inc=((s_+"xt2", 1),))
                P.rec("vector", lambda e, kc=kc, b2=b2, i=i, s2=s2: e.tensor_copy(
                    xnR[:, kc*TOK + b2*1024 + (7-i)*128: kc*TOK + b2*1024 + (8-i)*128],
                    pts[s2][:, 0:64].bitcast(BF16)),
                    wait=((s_+"xt2", P.val(s_+"xt2")),), inc=((s_+"xre", 1),))
    xrv = (s_+"xre", P.val(s_+"xre"))

    for b in range(2):
        for hp in range(8):
            for role, rsrc, outd in ((0, xn, qloc), (1, xnR, kloc), (2, xnR, vloc)):
                for tt in range(2):
                    for kc in range(NKC):
                        P.rec("tensor", lambda e, b=b, hp=hp, role=role, rsrc=rsrc, tt=tt, kc=kc: e.matmul(
                            ps_w[:, :],
                            wq3[:, kc*3072 + role*1024 + hp*128: kc*3072 + role*1024 + (hp+1)*128],
                            rsrc[:, kc*TOK + b*1024 + tt*512: kc*TOK + b*1024 + (tt+1)*512],
                            start=(kc == 0), stop=(kc == NKC - 1)),
                            wait=(list(wv0) + [xrv, (s_+"pe", P.val(s_+"pe"))]) if kc == 0 else (),
                            inc=((s_+"pm", 1),) if kc == NKC - 1 else ())
                    if role < 2:
                        par = P.val(s_+"pm") % 2
                        P.rec("scalar", lambda e, role=role, hp=hp, par=par: e.activation(
                            out=pstg[:, par*512:(par+1)*512], in_=ps_w[:, :],
                            func=AF.Identity, bias=bq3[:, role*8 + hp: role*8 + hp + 1]),
                            wait=((s_+"pm", P.val(s_+"pm")),
                                  (s_+"qk" + str(par), P.val(s_+"qk" + str(par)))),
                            inc=((s_+"pe", 1),))
                        P.rec("sync", lambda e, outd=outd, hp=hp, b=b, tt=tt, par=par: e.dma_start(
                            out=outd[hp*128:(hp+1)*128, b*1024 + tt*512: b*1024 + (tt+1)*512],
                            in_=pstg[:, par*512:(par+1)*512]),
                            wait=((s_+"pe", P.val(s_+"pe")),),
                            inc=((s_+"qk" + str(par), 16),))
                    else:
                        P.rec("scalar", lambda e, hp=hp: e.activation(
                            out=vstg[:, :], in_=ps_w[:, :],
                            func=AF.Identity, bias=bq3[:, 16 + hp: 16 + hp + 1]),
                            wait=((s_+"pm", P.val(s_+"pm")), (s_+"vt", P.val(s_+"vt"))),
                            inc=((s_+"pe", 1),))
                        for sv in range(4):
                            sl = 1 + P.val(s_+"vt") % 3
                            P.rec("tensor", lambda e, sv=sv, sl=sl: e.transpose(
                                pts[sl][:, 0:64].bitcast(BF16), vstg[:, sv*128:(sv+1)*128],
                                identb[:, :]),
                                wait=[(s_+"pe", P.val(s_+"pe")),
                                      (s_+"vte", P.val(s_+"vte") - 2)],
                                inc=((s_+"vt", 1),))
                            P.rec("vector", lambda e, sv=sv, sl=sl: e.tensor_copy(
                                vtm[:, sv*128:(sv+1)*128], pts[sl][:, 0:64].bitcast(BF16)),
                                wait=((s_+"vt", P.val(s_+"vt")),
                                      (s_+"vwd", P.val(s_+"vwd"))),
                                inc=((s_+"vte", 1),))
                            P.rec("sync", lambda e, b=b, tt=tt, sv=sv, hp=hp: e.dma_start(
                                out=vloc[b*1024 + tt*512 + sv*128: b*1024 + tt*512 + (sv+1)*128,
                                         hp*128:(hp+1)*128],
                                in_=vtm[:, sv*128:(sv+1)*128]),
                                wait=((s_+"vte", P.val(s_+"vte")),), inc=((s_+"vwd", 16),))
    P.rec("gpsimd", lambda e: e.collective_compute(
        "AllGather", ALU.bypass, replica_groups=RG8,
        ins=[kloc.opt()], outs=[agk.opt()]),
        wait=(("a3_qk0", P.val("a3_qk0")), ("a3_qk1", P.val("a3_qk1"))),
        inc=(("@a3k", 1),))
    P.rec("gpsimd", lambda e: e.collective_compute(
        "AllGather", ALU.bypass, replica_groups=RG8,
        ins=[vloc.opt()], outs=[agv.opt()]),
        wait=(("a3_vwd", P.val("a3_vwd")),), inc=(("@a3v", 1),))
    agkv = ("@a3k", P.val("@a3k"))
    agvv = ("@a3v", P.val("@a3v"))
    A.pop()
    P.gate((("a3_qk0", P.val("a3_qk0")), ("a3_qk1", P.val("a3_qk1")),
            ("a3_vwd", P.val("a3_vwd")), ("a3_pe", P.val("a3_pe")),
            ("a3_vte", P.val("a3_vte")), ("a3_xre", P.val("a3_xre"))))
    EH = A.sb("a3_EH", [128, 9088], BF16)
    qf = A.sb("a3_q", [64, 1024], BF16)
    kf = A.sb("a3_k", [64, 8192], BF16)
    vext = A.sb("a3_vext", [128, 4 * 65], BF16)
    ptile = A.sb("a3_pt", [128, 2 * 512], BF16)
    sums_sb = A.sb("a3_sums", [128, 512], F32)
    rb_sb = A.sb("a3_rb", [64, 512], F32)
    bcol = A.sb("a3_bcol", [128, 8], F32)
    osb = A.sb("a3_osb", [64, 512], BF16)
    safm = A.sb("a3_safm", [128, 8 * 512], BF16)
    sstg0 = A.sb("a3_sstg", [128, 512], BF16)
    sqs = A.sb("a3_sqs", [64, 512], F32)

    P.rec("vector", lambda e: e.memset(vext[:, :], 1.0), inc=((s_+"vxz", 1),))
    for h in range(16):
        P.rec("sync", lambda e, h=h: e.dma_start(
            out=EH[:, :], in_=mkap(ecvd, h * 9216, [[1, 128], [1, 9088]])),
            wait=[ecvv, (s_+"mk", P.val(s_+"mk"))], inc=((s_+"ehl", 16),))
        ehv = (s_+"ehl", P.val(s_+"ehl"))
        for b in range(2):
            P.rec("sync", lambda e, h=h, b=b: e.dma_start(
                out=qf[:, :], in_=qloc[h*64:(h+1)*64, b*1024:(b+1)*1024]),
                wait=[("a3_qk0", P.val("a3_qk0")), ("a3_qk1", P.val("a3_qk1")),
                      (s_+"sm", P.val(s_+"sm")), (s_+"sq", P.val(s_+"sq"))],
                inc=((s_+"ql", 16),))
            for j in range(8):
                P.rec("sync", lambda e, h=h, b=b, j=j: e.dma_start(
                    out=kf[:, j*1024:(j+1)*1024],
                    in_=agk[(7-j)*1024 + h*64: (7-j)*1024 + (h+1)*64, b*1024:(b+1)*1024]),
                    wait=([agkv, (s_+"sm", P.val(s_+"sm")), (s_+"sq", P.val(s_+"sq"))] if j == 0 else ()),
                    inc=((s_+"kl", 16),))
            qkv_ld = [(s_+"ql", P.val(s_+"ql")), (s_+"kl", P.val(s_+"kl"))]
            for src_t, nch, colix in ((qf, 2, 0), (kf, 16, 1)):
                for tt2 in range(nch):
                    P.rec("scalar", lambda e, src_t=src_t, tt2=tt2: e.activation(
                        out=sqs[:, :], in_=src_t[0:64, tt2*512:(tt2+1)*512], func=AF.Square),
                        wait=qkv_ld + [(s_+"nsm", P.val(s_+"nsm"))], inc=((s_+"sq", 1),))
                    P.rec("tensor", lambda e: e.matmul(
                        ps_t[0:1, :], onesc[0:64, :], sqs[:, :], start=True, stop=True),
                        wait=((s_+"sq", P.val(s_+"sq")), (s_+"nse", P.val(s_+"nse"))),
                        inc=((s_+"nsm", 1),))
                    if tt2 == 0:
                        P.rec("vector", lambda e, colix=colix: e.reduce_max(
                            bcol[0:1, colix:colix+1], ps_t[0:1, :], axis=AX.X),
                            wait=((s_+"nsm", P.val(s_+"nsm")),), inc=((s_+"nse", 1),))
                    else:
                        P.rec("vector", lambda e: e.reduce_max(
                            bcol[0:1, 4:5], ps_t[0:1, :], axis=AX.X),
                            wait=((s_+"nsm", P.val(s_+"nsm")),), inc=((s_+"nse", 1),))
                        P.rec("vector", lambda e, colix=colix: e.tensor_max(
                            bcol[0:1, colix:colix+1], bcol[0:1, colix:colix+1], bcol[0:1, 4:5]))
            P.rec("vector", lambda e: e.tensor_mul(bcol[0:1, 2:3], bcol[0:1, 0:1], bcol[0:1, 1:2]),
                  inc=((s_+"bm", 1),))
            P.rec("scalar", lambda e: e.activation(out=bcol[0:1, 2:3], in_=bcol[0:1, 2:3], func=AF.Sqrt),
                  wait=((s_+"bm", P.val(s_+"bm")),), inc=((s_+"bs", 1),))
            P.rec("vector", lambda e: e.tensor_scalar_mul(bcol[0:1, 2:3], bcol[0:1, 2:3], -SC),
                  wait=((s_+"bs", P.val(s_+"bs")),), inc=((s_+"bb", 1),))
            P.rec("tensor", lambda e: e.matmul(ps_t[:, 0:1], onesr[0:1, :], bcol[0:1, 2:3],
                                               start=True, stop=True),
                  wait=((s_+"bb", P.val(s_+"bb")), (s_+"nse", P.val(s_+"nse"))),
                  inc=((s_+"nsm", 1),))
            P.rec("vector", lambda e: e.tensor_copy(bcol[:, 3:4], ps_t[:, 0:1]),
                  wait=((s_+"nsm", P.val(s_+"nsm")),), inc=((s_+"bc", 1),))
            bcv = (s_+"bc", P.val(s_+"bc"))
            for qg in range(2):
                for kt in range(64):
                    par = kt % 2
                    pss = ps_s1 if par == 0 else ps_s2
                    P.rec("tensor", lambda e, qg=qg, kt=kt, pss=pss: e.matmul(
                        pss[:, :], kf[0:64, kt*128:(kt+1)*128], qf[0:64, qg*512:(qg+1)*512],
                        start=True, stop=True),
                        wait=((s_+"px", P.val(s_+"px") - 1),),
                        inc=((s_+"sm", 1),))
                    P.rec("scalar", lambda e, par=par, pss=pss: e.activation(
                        out=ptile[:, par*512:(par+1)*512], in_=pss[:, :],
                        func=AF.Exp, bias=bcol[:, 3:4], scale=SC),
                        wait=((s_+"sm", P.val(s_+"sm")), bcv,
                              (s_+"pv", P.val(s_+"pv") - 1)),
                        inc=((s_+"px", 1),))
                    P.rec("vector", lambda e, par=par, qg=qg, kt=kt: e.tensor_mul(
                        ptile[:, par*512:(par+1)*512], ptile[:, par*512:(par+1)*512],
                        EH[:, qg*512 + kt*128: qg*512 + kt*128 + 512]),
                        wait=((s_+"px", P.val(s_+"px")), ehv),
                        inc=((s_+"mk", 1),))
                    vsl = kt % 4
                    pmax = 8191 - kt * 128
                    jsrc = pmax // 1024
                    r0 = jsrc * 2048 + b * 1024 + (1023 - (pmax % 1024))
                    P.rec("gpsimd", lambda e, vsl=vsl, r0=r0, h=h: e.dma_start(
                        out=vext[:, vsl*65:vsl*65+64],
                        in_=mkap(agv, r0 * 1024 + h * 64, [[1024, 128], [1, 64]])),
                        wait=[agvv, (s_+"pv", P.val(s_+"pv") - 3), (s_+"vxz", 1)],
                        inc=((s_+"vxl" + str(vsl), 16),))
                    P.rec("tensor", lambda e, kt=kt, par=par, vsl=vsl: e.matmul(
                        ps_o[0:65, :], vext[:, vsl*65:(vsl+1)*65], ptile[:, par*512:(par+1)*512],
                        start=(kt == 0), stop=(kt == 63)),
                        wait=[(s_+"mk", P.val(s_+"mk")),
                              (s_+"vxl" + str(vsl), P.val(s_+"vxl" + str(vsl)))] +
                             ([(s_+"vxz", 1)] if kt == 0 else []),
                        inc=((s_+"pv", 1),))
                P.rec("vector", lambda e: e.tensor_copy(sums_sb[64:65, :], ps_o[64:65, :]),
                      wait=((s_+"pv", P.val(s_+"pv")), (s_+"ow", P.val(s_+"ow"))),
                      inc=((s_+"su", 1),))
                P.rec("sync", lambda e: e.dma_start(out=sums_sb[0:1, :], in_=sums_sb[64:65, :]),
                      wait=((s_+"su", P.val(s_+"su")),), inc=((s_+"sd", 16),))
                P.rec("vector", lambda e: e.reciprocal(sums_sb[0:1, :], sums_sb[0:1, :]),
                      wait=((s_+"sd", P.val(s_+"sd")),), inc=((s_+"rc", 1),))
                P.rec("tensor", lambda e: e.matmul(ps_w[0:64, :], onesr[0:1, 0:64],
                                                   sums_sb[0:1, :], start=True, stop=True),
                      wait=((s_+"rc", P.val(s_+"rc")), (s_+"rbe", P.val(s_+"rbe"))),
                      inc=((s_+"nsm", 1),))
                P.rec("vector", lambda e: e.tensor_copy(rb_sb[:, :], ps_w[0:64, :]),
                      wait=((s_+"nsm", P.val(s_+"nsm")),), inc=((s_+"rbe", 1),))
                P.rec("vector", lambda e: e.tensor_mul(osb[:, :], ps_o[0:64, :], rb_sb[:, :]),
                      wait=((s_+"rbe", P.val(s_+"rbe")),), inc=((s_+"oe", 1),))
                P.rec("sync", lambda e, h=h, b=b, qg=qg: e.dma_start(
                    out=sad[h*64:(h+1)*64, b*1024 + qg*512: b*1024 + (qg+1)*512],
                    in_=osb[:, :]),
                    wait=((s_+"oe", P.val(s_+"oe")),), inc=((s_+"ow", 16),))
    owv = (s_+"ow", P.val(s_+"ow"))
    for tt in range(4):
        for kc in range(NKC):
            P.rec("sync", lambda e, tt=tt, kc=kc: e.dma_start(
                out=safm[:, kc*512:(kc+1)*512],
                in_=sad[kc*128:(kc+1)*128, tt*512:(tt+1)*512]),
                wait=([owv, (s_+"smm", P.val(s_+"smm"))] if kc == 0 else ()),
                inc=((s_+"sf", 16),))
        sfv = (s_+"sf", P.val(s_+"sf"))
        for oc in range(8):
            for kc in range(NKC):
                P.rec("tensor", lambda e, oc=oc, kc=kc: e.matmul(
                    ps_o[:, :], wo3[:, kc*1024 + oc*128: kc*1024 + oc*128 + 128],
                    safm[:, kc*512:(kc+1)*512],
                    start=(kc == 0), stop=(kc == NKC - 1)),
                    wait=([sfv, (s_+"se", P.val(s_+"se"))] if kc == 0 else ()),
                    inc=((s_+"smm", 1),) if kc == NKC - 1 else ())
            P.rec("scalar", lambda e, oc=oc: e.activation(
                out=sstg0[:, :], in_=ps_o[:, :], func=AF.Identity, bias=bo3[:, oc:oc+1]),
                wait=((s_+"smm", P.val(s_+"smm")), (s_+"sw", P.val(s_+"sw"))),
                inc=((s_+"se", 1),))
            P.rec("sync", lambda e, oc=oc, tt=tt: e.dma_start(
                out=sdr[oc*128:(oc+1)*128, tt*512:(tt+1)*512],
                in_=sstg0[:, :]),
                wait=((s_+"se", P.val(s_+"se")),), inc=((s_+"sw", 16),))
    return (s_+"sw", P.val(s_+"sw"))

def make_inputs_l3a(I, core, x_cur):
    o = make_consts()
    o["coresel"] = np.zeros((128, 8), np.float32)
    xs = x_cur[:, 1024*core:1024*(core+1), :].reshape(TOK, D)
    o["x_in"] = np.ascontiguousarray(xs)
    o["L3_attn_w_qkv"] = I["L3_attn_w_qkv"]
    o["L3_attn_w_out"] = I["L3_attn_w_out"]
    bq = I["L3_attn_b_qkv"]
    o["L3_bq_cols"] = np.stack([bq[role*1024 + hp*128: role*1024 + (hp+1)*128]
                                for role in range(3) for hp in range(8)], axis=1)
    o["L3_bo_cols"] = colsify(I["L3_attn_b_out"])
    d = np.arange(-8191, 8192, dtype=np.float32)
    relg = np.sign(d) * np.log1p(np.abs(d))
    rl = np.zeros(9216, np.float32)
    seg = relg[core*1024: core*1024 + 9216]
    rl[:len(seg)] = seg
    o["L3_relrow"] = rl[None, :]
    o["L3_w1rpe"] = I["L3_attn_w1_rpe"]
    o["L3_b1rpe"] = I["L3_attn_b1_rpe"].reshape(32, 1)
    o["L3_w2rpe"] = I["L3_attn_w2_rpe"]
    o["L3_b2rpe"] = I["L3_attn_b2_rpe"].reshape(16, 1)
    o["flip_f32"] = np.eye(128, dtype=np.float32)[::-1]
    return {k2: np.ascontiguousarray(v) for k2, v in o.items()}

def emit_l0_ffn_route(P, A, xn, lnxn, W, DR, x_dram, r_mean, r_rstd):
    s_ = "f0_"
    onesc = A.sb("ones_col", [128, 1], F32)
    onesr = A.sb("ones_row", [1, 128], F32)
    identb = A.sb("ident_bf", [128, 128], BF16)
    identf = A.sb("identf_sb", [128, 128], F32)
    gdr = DR["g_dram"]
    pts = ptbanks(A)
    ps_a = A.ps("ps_a", [128, 512], F32)
    ps_b = A.ps("ps_b", [128, 512], F32)
    ps_c = A.ps("ps_c", [128, 512], F32)
    ps_w = A.ps("ps_bc", [128, 512], F32)

    A.push()
    rw_sb = A.sb("f0_rw", [128, 64], F32)
    xfc = A.sb("f0_xfc", [128, 8 * 128], F32)
    rowB = A.sb("rowA", [1, TOK], F32)
    scol8 = A.sb("f0_scol8", [8, 1], F32)
    row8 = A.sb("f0_row8", [1, 8], F32)
    Sb = A.sb("f0_Sb", [128, 8], F32)
    acol = A.sb("f0_acol", [128, 1], F32)
    bcol2 = A.sb("f0_bcol2", [128, 1], F32)
    mtmp8 = A.sb("f0_mtmp8", [128, 8], F32)
    tri8 = A.sb("f0_tri", [8, 8], F32)
    step8 = A.sb("f0_step", [8, 1], F32)
    mrow = A.sb("f0_mrow", [128, 16], F32)
    lgs = A.sb("f0_lgs", [8, 128], F32)
    lgT = A.sb("f0_lgT", [128, 8], F32)
    eqb = A.sb("f0_eqb", [128, 8], F32)
    scn = A.sb("f0_scn", [128, 8], F32)
    iso = A.sb("f0_iso", [128, 8], F32)
    f1b = A.sb("f0_f1", [128, 8], F32)
    f2b = A.sb("f0_f2", [128, 8], F32)
    mtmp = A.sb("f0_mtmp", [128, 8], F32)
    zzc = A.sb("f0_zz", [128, 1], F32)
    t2c = A.sb("f0_t2c", [128, 8], F32)
    wcols = A.sb("f0_wcols", [128, 32], F32)
    oh = A.sb("f0_oh", [8, 4096], F32)
    pc = A.sb("f0_pc", [8, 4096], F32)
    ccol = A.sb("f0_ccol", [8, 8], F32)
    bcols = A.sb("f0_bcols", [8, 4], F32)
    srow = A.sb("f0_srow", [1, 4096], F32)
    scol = A.sb("f0_scol", [128, 2], F32)
    scoli = A.sb("f0_scoli", [128, 1], I32)
    zbuf = A.sb("f0_zbuf", [128, 256], F32)
    xstg = A.sb("f0_xstg", [128, 2 * 128], BF16)

    P.rec("sync", lambda e: e.dma_start(out=rw_sb[:, :], in_=W["L0_rw_cols"]), inc=((s_+"wg2", 16),))
    P.rec("sync", lambda e: e.dma_start(out=tri8[:, :], in_=W["tri8"]), inc=((s_+"w", 16),))
    P.rec("sync", lambda e: e.dma_start(out=step8[:, :], in_=W["step8"]), inc=((s_+"w", 16),))
    P.rec("sync", lambda e: e.dma_start(out=mrow[:, :], in_=W["moerowc"]), inc=((s_+"w", 16),))
    P.rec("vector", lambda e: e.memset(zzc[:, :], 0.0), inc=((s_+"zz", 1),))
    wv0 = [(s_+"w", P.val(s_+"w")), (s_+"wg2", P.val(s_+"wg2")), (s_+"zz", 1), lnxn]
    P.rec("vector", lambda e: e.tensor_mul(rowB[0:1, :], r_mean[0:1, :], r_rstd[0:1, :]),
          wait=(lnxn,), inc=((s_+"rmv", 1),))
    P.rec("sync", lambda e: e.dma_start(out=DR["rstd_d"], in_=r_rstd[0:1, :]),
          wait=((s_+"rmv", 1), lnxn), inc=((s_+"mrw", 16),))
    P.rec("sync", lambda e: e.dma_start(out=DR["mrs_d"], in_=rowB[0:1, :]),
          inc=((s_+"mrw", 16),))
    mrwv = (s_+"mrw", P.val(s_+"mrw"))
    for kc in range(NKC):
        P.rec("tensor", lambda e, kc=kc: e.matmul(
            ps_w[0:8, 0:1], rw_sb[:, kc*8:(kc+1)*8], onesc[:, 0:1],
            start=(kc == 0), stop=(kc == NKC - 1)),
            wait=(wv0 if kc == 0 else ()),
            inc=((s_+"sem1", 1),) if kc == NKC - 1 else ())
    P.rec("vector", lambda e: e.tensor_copy(scol8[:, :], ps_w[0:8, 0:1]),
          wait=((s_+"sem1", P.val(s_+"sem1")),), inc=((s_+"sev", 1),))
    P.rec("tensor", lambda e: e.transpose(ps_w[0:1, 0:8], scol8[:, :], identf[0:8, 0:8]),
          wait=((s_+"sev", P.val(s_+"sev")),), inc=((s_+"sem1", 1),))
    P.rec("vector", lambda e: e.tensor_copy(row8[0:1, :], ps_w[0:1, 0:8]),
          wait=((s_+"sem1", P.val(s_+"sem1")),), inc=((s_+"sev", 1),))
    P.rec("tensor", lambda e: e.matmul(ps_w[0:128, 0:8], onesr[0:1, :], row8[0:1, :],
                                       start=True, stop=True),
          wait=((s_+"sev", P.val(s_+"sev")),), inc=((s_+"sem1", 1),))
    P.rec("vector", lambda e: e.tensor_copy(Sb[:, :], ps_w[0:128, 0:8]),
          wait=((s_+"sem1", P.val(s_+"sem1")),), inc=((s_+"sbv", 1),))
    sbvv = (s_+"sbv", P.val(s_+"sbv"))
    xr0 = x_dram.rearrange("(kc p) t -> kc p t", p=128)

    for cc in range(16):
        for kc in range(NKC):
            P.rec("sync", lambda e, cc=cc, kc=kc: e.dma_start(
                out=xfc[:, kc*128:(kc+1)*128], in_=xr0[kc][:, cc*128:(cc+1)*128]),
                wait=([(s_+"lgm", P.val(s_+"lgm"))] if kc == 0 else ()),
                inc=((s_+"xfl", 16),))
        for kc in range(NKC):
            P.rec("tensor", lambda e, cc=cc, kc=kc: e.matmul(
                ps_w[0:8, :128], rw_sb[:, kc*8:(kc+1)*8],
                xfc[:, kc*128:(kc+1)*128],
                start=(kc == 0), stop=(kc == NKC - 1)),
                wait=(wv0 + [(s_+"lge", P.val(s_+"lge")), (s_+"sbv", P.val(s_+"sbv")),
                             (s_+"xfl", P.val(s_+"xfl"))]) if kc == 0 else (),
                inc=((s_+"lgm", 1),) if kc == NKC - 1 else ())
        P.rec("vector", lambda e: e.tensor_copy(lgs[:, :], ps_w[0:8, :128]),
              wait=((s_+"lgm", P.val(s_+"lgm")),), inc=((s_+"lge", 1),))
        P.rec("tensor", lambda e: e.transpose(ps_a[0:128, 0:8], lgs[:, :], identf[0:8, 0:8]),
              wait=((s_+"lge", P.val(s_+"lge")), (s_+"lgt2", P.val(s_+"lgt2"))),
              inc=((s_+"lgt", 1),))
        P.rec("vector", lambda e: e.tensor_copy(lgT[:, :], ps_a[0:128, 0:8]),
              wait=((s_+"lgt", P.val(s_+"lgt")),), inc=((s_+"lgt2", 1),))
        P.rec("sync", lambda e, cc=cc: e.dma_start(
            out=acol[:, 0:1], in_=mkap(DR["rstd_d"], cc*128, [[1, 128], [1, 1]])),
            wait=[mrwv, (s_+"crv", P.val(s_+"crv"))], inc=((s_+"mrl", 16),))
        P.rec("sync", lambda e, cc=cc: e.dma_start(
            out=bcol2[:, 0:1], in_=mkap(DR["mrs_d"], cc*128, [[1, 128], [1, 1]])),
            inc=((s_+"mrl", 16),))
        P.rec("vector", lambda e: e.tensor_scalar(lgT[:, :], lgT[:, :], acol[:, 0:1], None, ALU.mult),
              wait=((s_+"mrl", P.val(s_+"mrl")), sbvv))
        P.rec("vector", lambda e: e.tensor_scalar(mtmp8[:, :], Sb[:, :], bcol2[:, 0:1], None, ALU.mult))
        P.rec("vector", lambda e: e.tensor_sub(lgT[:, :], lgT[:, :], mtmp8[:, :]),
              inc=((s_+"crv", 1),))
        P.rec("vector", lambda e: e.reduce_max(t2c[:, 0:1], lgT[:, :], axis=AX.X))
        P.rec("vector", lambda e: e.tensor_scalar(eqb[:, :], lgT[:, :], t2c[:, 0:1], None, ALU.is_equal))
        P.rec("vector", lambda e: e.tensor_tensor_scan(scn[:, :], eqb[:, :], eqb[:, :], zzc[:, 0:1], ALU.add, ALU.bypass))
        P.rec("vector", lambda e: e.tensor_scalar(iso[:, :], scn[:, :], 1.0, None, ALU.is_equal))
        P.rec("vector", lambda e: e.tensor_mul(f1b[:, :], eqb[:, :], iso[:, :]))
        P.rec("vector", lambda e: e.tensor_scalar_mul(mtmp[:, :], f1b[:, :], -1e30))
        P.rec("vector", lambda e: e.tensor_add(mtmp[:, :], mtmp[:, :], lgT[:, :]))
        P.rec("vector", lambda e: e.reduce_max(t2c[:, 1:2], mtmp[:, :], axis=AX.X))
        P.rec("vector", lambda e: e.tensor_scalar(eqb[:, :], mtmp[:, :], t2c[:, 1:2], None, ALU.is_equal))
        P.rec("vector", lambda e: e.tensor_tensor_scan(scn[:, :], eqb[:, :], eqb[:, :], zzc[:, 0:1], ALU.add, ALU.bypass))
        P.rec("vector", lambda e: e.tensor_scalar(iso[:, :], scn[:, :], 1.0, None, ALU.is_equal))
        P.rec("vector", lambda e: e.tensor_mul(f2b[:, :], eqb[:, :], iso[:, :]))
        P.rec("vector", lambda e: e.tensor_sub(t2c[:, 2:3], t2c[:, 0:1], t2c[:, 1:2]))
        P.rec("vector", lambda e: e.tensor_sub(t2c[:, 5:6], t2c[:, 1:2], t2c[:, 0:1]),
              inc=((s_+"tv", 1),))
        P.rec("scalar", lambda e: e.activation(out=t2c[:, 3:4], in_=t2c[:, 2:3], func=AF.Sigmoid),
              wait=((s_+"tv", P.val(s_+"tv")),))
        P.rec("scalar", lambda e: e.activation(out=t2c[:, 6:7], in_=t2c[:, 5:6], func=AF.Sigmoid),
              inc=((s_+"sg", 1),))
        P.rec("vector", lambda e, cc=cc: e.tensor_copy(wcols[:, 2*cc:2*cc+1], t2c[:, 3:4]),
              wait=((s_+"sg", P.val(s_+"sg")),))
        P.rec("vector", lambda e, cc=cc: e.tensor_copy(wcols[:, 2*cc+1:2*cc+2], t2c[:, 6:7]),
              inc=((s_+"wcl", 1),))
        for j, fb in ((0, f1b), (1, f2b)):
            P.rec("tensor", lambda e, fb=fb: e.transpose(ps_b[0:8, 0:128], fb[:, :], identf[:, :]),
                  wait=((s_+"ohe", P.val(s_+"ohe")), (s_+"wcl", P.val(s_+"wcl"))),
                  inc=((s_+"oht", 1),))
            P.rec("vector", lambda e, cc=cc, j=j: e.tensor_copy(
                oh[:, cc*256 + j: cc*256 + 256: 2], ps_b[0:8, 0:128]),
                wait=((s_+"oht", P.val(s_+"oht")),), inc=((s_+"ohe", 1),))
    P.rec("vector", lambda e: e.tensor_tensor_scan(pc[:, :], oh[:, :], oh[:, :], zzc[0:8, 0:1], ALU.add, ALU.bypass),
          wait=((s_+"ohe", P.val(s_+"ohe")),), inc=((s_+"pcv", 1),))
    P.rec("vector", lambda e: e.tensor_copy(ccol[:, 0:1], pc[:, 4095:4096]),
          inc=((s_+"lcv", 1),))
    P.rec("sync", lambda e: e.dma_start(out=DR["cnt_in"], in_=ccol[:, 0:1]),
          wait=((s_+"lcv", P.val(s_+"lcv")),), inc=((s_+"ci", 16),))
    P.rec("gpsimd", lambda e: e.collective_compute(
        "AllGather", ALU.bypass, replica_groups=RG8,
        ins=[DR["cnt_in"].opt()], outs=[DR["cnt_out"].opt()]),
        wait=((s_+"ci", P.val(s_+"ci")),), inc=(("@f0c", 1),))
    P.rec("sync", lambda e: e.dma_start(
        out=ccol[:, :], in_=DR["cnt_out"].rearrange("(c e) o -> c (e o)", e=8)),
        wait=(("@f0c", P.val("@f0c")),), inc=((s_+"cl", 16),))
    P.rec("tensor", lambda e: e.matmul(ps_w[0:8, 0:1], ccol[0:8, :], onesc[0:8, 0:1],
                                       start=True, stop=True),
          wait=((s_+"cl", P.val(s_+"cl")), (s_+"lge", P.val(s_+"lge"))), inc=((s_+"bm", 1),))
    P.rec("vector", lambda e: e.tensor_copy(bcols[:, 0:1], ps_w[0:8, 0:1]),
          wait=((s_+"bm", P.val(s_+"bm")),), inc=((s_+"be", 1),))
    P.rec("tensor", lambda e: e.matmul(ps_w[0:8, 0:1], tri8[:, :], bcols[0:8, 0:1],
                                       start=True, stop=True),
          wait=((s_+"be", P.val(s_+"be")),), inc=((s_+"bm", 1),))
    P.rec("vector", lambda e: e.tensor_copy(bcols[:, 1:2], ps_w[0:8, 0:1]),
          wait=((s_+"bm", P.val(s_+"bm")),), inc=((s_+"be", 1),))
    P.rec("tensor", lambda e: e.matmul(ps_w[0:8, 0:1], ccol[0:8, :], step8[0:8, 0:1],
                                       start=True, stop=True),
          wait=((s_+"be", P.val(s_+"be")),), inc=((s_+"bm", 1),))
    P.rec("vector", lambda e: e.tensor_copy(bcols[:, 2:3], ps_w[0:8, 0:1]),
          wait=((s_+"bm", P.val(s_+"bm")),))
    P.rec("vector", lambda e: e.tensor_add(bcols[:, 3:4], bcols[:, 1:2], bcols[:, 2:3]))
    P.rec("vector", lambda e: e.tensor_scalar(bcols[:, 3:4], bcols[:, 3:4], -1.0, None, ALU.add),
          inc=((s_+"bsv", 1),))
    P.rec("vector", lambda e: e.tensor_scalar(pc[:, :], pc[:, :], bcols[0:8, 3:4], None, ALU.add),
          wait=((s_+"bsv", P.val(s_+"bsv")), (s_+"pcv", P.val(s_+"pcv"))))
    P.rec("vector", lambda e: e.tensor_mul(pc[:, :], pc[:, :], oh[:, :]),
          inc=((s_+"pm2", 1),))
    for tt in range(8):
        P.rec("tensor", lambda e, tt=tt: e.matmul(
            ps_w[0:1, :], onesc[0:8, 0:1], pc[:, tt*512:(tt+1)*512],
            start=True, stop=True),
            wait=((s_+"pm2", P.val(s_+"pm2")), (s_+"sre", P.val(s_+"sre"))),
            inc=((s_+"srm", 1),))
        P.rec("vector", lambda e, tt=tt: e.tensor_copy(srow[0:1, tt*512:(tt+1)*512], ps_w[0:1, :]),
              wait=((s_+"srm", P.val(s_+"srm")),), inc=((s_+"sre", 1),))
    P.rec("sync", lambda e: e.dma_start(out=DR["srd"], in_=srow[0:1, :]),
          wait=((s_+"sre", P.val(s_+"sre")),), inc=((s_+"srw", 16),))
    srwv = (s_+"srw", P.val(s_+"srw"))
    paycol = A.sb("f0_pay", [128, 2], F32)
    P.rec("vector", lambda e: e.memset(zbuf[:, :], 0.0), inc=((s_+"zb", 1),))
    for hh in range(2):
        P.rec("sync", lambda e, hh=hh: e.dma_start(
            out=DR["rvtw0"].rearrange("(h p c) o -> h p (c o)", h=2, p=128)[hh], in_=zbuf[:, :]),
            wait=((s_+"zb", 1),) if hh == 0 else (), inc=((s_+"zw", 16),))
    zwv = (s_+"zw", P.val(s_+"zw"))
    for cc in range(16):
        for j in range(2):
            P.rec("sync", lambda e, cc=cc, j=j: e.dma_start(
                out=scol[:, 0:1], in_=mkap(DR["srd"], cc*256 + j, [[2, 128], [1, 1]]),
                allow_slow_non_contiguous=True),
                wait=[srwv, (s_+"sci", P.val(s_+"sci"))], inc=((s_+"scl", 16),))
            P.rec("vector", lambda e: e.tensor_copy(scoli[:, :], scol[:, 0:1]),
                  wait=((s_+"scl", P.val(s_+"scl")), (s_+"sct0", P.val(s_+"sct0"))),
                  inc=((s_+"sci", 1),))
            P.rec("vector", lambda e, cc=cc: e.tensor_copy(paycol[:, 0:1], mrow[:, cc:cc+1]),
                  wait=((s_+"w", P.val(s_+"w")),))
            P.rec("vector", lambda e, cc=cc, j=j: e.tensor_copy(
                paycol[:, 1:2], wcols[:, 2*cc+j:2*cc+j+1]),
                inc=((s_+"pyv", 1),))
            P.rec("gpsimd", lambda e: e.indirect_dma_start(
                out=DR["rvtw0"], out_offset=IndirectOffsetOnAxis(scoli[:, :], 0),
                in_=paycol[:, :], in_offset=None,
                ),
                wait=[zwv, (s_+"sci", P.val(s_+"sci")), (s_+"pyv", P.val(s_+"pyv"))],
                inc=((s_+"sct0", 16),))
    P.rec("gpsimd", lambda e: e.collective_compute(
        "AllReduce", ALU.add, replica_groups=RG8,
        ins=[DR["rvtw0"].opt()], outs=[DR["rvtwA"].opt()]),
        wait=((s_+"sct0", P.val(s_+"sct0")),),
        inc=(("@f0t", 1),))
    rvv = [("@f0t", P.val("@f0t"))]
    for kc in range(NKC):
        for i in range(16):
            n = kc * 16 + i
            sl = 1 + n % 3
            par = n % 2
            P.rec("tensor", lambda e, kc=kc, i=i, sl=sl: e.transpose(
                pts[sl][:, 0:64].bitcast(BF16), xn[:, kc*TOK + i*128: kc*TOK + (i+1)*128],
                identb[:, :]),
                wait=([lnxn] if n == 0 else []) + [(s_+"xtc", P.val(s_+"xtc") - 2)],
                inc=((s_+"xtt", 1),))
            P.rec("vector", lambda e, par=par, sl=sl: e.tensor_copy(
                xstg[:, par*128:(par+1)*128], pts[sl][:, 0:64].bitcast(BF16)),
                wait=((s_+"xtt", P.val(s_+"xtt")),
                      (s_+"xtw" + str(par), P.val(s_+"xtw" + str(par)))),
                inc=((s_+"xtc", 1),))
            P.rec("sync", lambda e, kc=kc, i=i, par=par: e.dma_start(
                out=DR["xtm0"][i*128:(i+1)*128, kc*128:(kc+1)*128],
                in_=xstg[:, par*128:(par+1)*128]),
                wait=((s_+"xtc", P.val(s_+"xtc")),), inc=((s_+"xtw" + str(par), 16),))
    P.rec("gpsimd", lambda e: e.collective_compute(
        "AllGather", ALU.bypass, replica_groups=RG8,
        ins=[DR["xtm0"].opt()], outs=[DR["agxn"].opt()]),
        wait=((s_+"xtw0", P.val(s_+"xtw0")), (s_+"xtw1", P.val(s_+"xtw1"))),
        inc=(("@f0x", 1),))
    agxv = ("@f0x", P.val("@f0x"))
    A.pop()
    return dict(rvv=rvv, agxv=agxv,
                gate=tuple(rvv) + (agxv, (s_+"sre", P.val(s_+"sre")),
                                   (s_+"ohe", P.val(s_+"ohe")),
                                   (s_+"scl", P.val(s_+"scl")),
                                   (s_+"wcl", P.val(s_+"wcl"))))


def emit_l0_ffn_main(P, A, W, DR, ctx):
    s_ = "f0_"
    onesc = A.sb("ones_col", [128, 1], F32)
    identb = A.sb("ident_bf", [128, 128], BF16)
    identf = A.sb("identf_sb", [128, 128], F32)
    gdr = DR["g_dram"]
    pts = ptbanks(A)
    ps_a = A.ps("ps_a", [128, 512], F32)
    ps_b = A.ps("ps_b", [128, 512], F32)
    ps_c = A.ps("ps_c", [128, 512], F32)
    ps_w = A.ps("ps_bc", [128, 512], F32)
    rvv = ctx["rvv"]
    agxv = ctx["agxv"]
    P.gate(ctx["gate"])

    A.push()
    w1sb = A.sb("f0_w1", [128, NKC * 4096], BF16)
    w2sb = A.sb("f0_w2", [128, 32 * 512], BF16)
    widx = A.sb("f0_widx", [128, 1], I32)
    rvall = A.sb("f0_rvall", [128, 64], F32)
    rvt = A.sb("f0_rvt", [128, 1], F32)
    tic = A.sb("f0_ti", [128, 1], I32)
    rvw4 = A.sb("f0_rvw4", [128, 4], F32)
    xg = A.sb("f0_xg", [128, 1024], BF16)
    xgf = A.sb("f0_xgf", [128, NKC * 512], BF16)
    hfm = A.sb("f0_hfm", [128, 32 * 512], BF16)
    yfm = A.sb("f0_yfm", [128, 8 * 512], BF16)
    ytm = A.sb("f0_ytm", [128, 1024], BF16)
    for kc in range(NKC):
        P.rec("gpsimd", lambda e, kc=kc: e.dma_start(
            out=w1sb[:, kc*4096:(kc+1)*4096],
            in_=W["L0_w1_my"].rearrange("(kc p) o -> kc p o", p=128)[kc]),
            inc=((s_+"wl", 16),))
    P.rec("sync", lambda e: e.dma_start(out=widx[:, :], in_=W["moewin2"]),
          inc=((s_+"wxl", 16),))
    wv2 = [(s_+"wl", P.val(s_+"wl")), (s_+"wxl", P.val(s_+"wxl"))]
    P.rec("gpsimd", lambda e: e.indirect_dma_start(
        out=rvall[:, :], out_offset=None, in_=DR["rvtwA"],
        in_offset=IndirectOffsetOnAxis(widx[:, 0:1], 0),
        ),
        wait=[(s_+"wxl", P.val(s_+"wxl"))] + rvv, inc=((s_+"rvl", 16),))
    P.rec("sync", lambda e: e.dma_start(out=DR["rvall_d"], in_=rvall[:, :]),
          wait=((s_+"rvl", P.val(s_+"rvl")),), inc=((s_+"rvd", 16),))
    rvdv = (s_+"rvd", P.val(s_+"rvd"))
    for grp in range(8):
        for sub in range(4):
            q = grp * 4 + sub
            P.rec("sync", lambda e, q=q: e.dma_start(
                out=rvt[:, 0:1],
                in_=mkap(DR["rvall_d"], (4*q)*64, [[2, 128], [1, 1]]),
                allow_slow_non_contiguous=True),
                wait=[rvdv, (s_+"tiv", P.val(s_+"tiv"))], inc=((s_+"rtl", 16),))
            P.rec("vector", lambda e: e.tensor_copy(tic[:, :], rvt[:, 0:1]),
                  wait=((s_+"rtl", P.val(s_+"rtl")), (s_+"xgl", P.val(s_+"xgl"))),
                  inc=((s_+"tiv", 1),))
            P.rec("sync", lambda e, q=q, sub=sub: e.dma_start(
                out=rvw4[:, sub:sub+1],
                in_=mkap(DR["rvall_d"], (4*q)*64 + 1, [[2, 128], [1, 1]]),
                allow_slow_non_contiguous=True),
                wait=[rvdv, (s_+"ymu", P.val(s_+"ymu"))], inc=((s_+"rwl", 16),))
            P.rec("gpsimd", lambda e: e.indirect_dma_start(
                out=xg[:, :], out_offset=None, in_=DR["agxn"],
                in_offset=IndirectOffsetOnAxis(tic[:, :], 0),
                ),
                wait=[agxv, (s_+"tiv", P.val(s_+"tiv")), (s_+"xgt", P.val(s_+"xgt"))],
                inc=((s_+"xgl", 16),))
            for kc in range(NKC):
                n = P.val(s_+"xgt")
                sl = 1 + n % 3
                P.rec("tensor", lambda e, kc=kc, sl=sl: e.transpose(
                    pts[sl][:, 0:64].bitcast(BF16), xg[:, kc*128:(kc+1)*128],
                    identb[:, :]),
                    wait=[(s_+"xgl", P.val(s_+"xgl")), (s_+"xge", P.val(s_+"xge") - 2)] +
                         ([(s_+"h1m", P.val(s_+"h1m"))] if kc == 0 and sub == 0 else []),
                    inc=((s_+"xgt", 1),))
                P.rec("vector", lambda e, kc=kc, sub=sub, sl=sl: e.tensor_copy(
                    xgf[:, kc*512 + sub*128: kc*512 + (sub+1)*128],
                    pts[sl][:, 0:64].bitcast(BF16)),
                    wait=((s_+"xgt", P.val(s_+"xgt")),), inc=((s_+"xge", 1),))
        xgv = (s_+"xge", P.val(s_+"xge"))
        for hc in range(32):
            psx = ps_a if hc % 2 == 0 else ps_b
            for kc in range(NKC):
                P.rec("tensor", lambda e, hc=hc, kc=kc, psx=psx: e.matmul(
                    psx[:, :], w1sb[:, kc*4096 + hc*128: kc*4096 + (hc+1)*128],
                    xgf[:, kc*512:(kc+1)*512],
                    start=(kc == 0), stop=(kc == NKC - 1)),
                    wait=([xgv, (s_+"wl", P.val(s_+"wl")), (s_+"h1e", P.val(s_+"h1e") - 1)] +
                          ([(s_+"y2e", P.val(s_+"y2e"))] if hc <= 1 else [])
                          if kc == 0 else ()),
                    inc=((s_+"h1m", 1),) if kc == NKC - 1 else ())
            P.rec("scalar", lambda e, hc=hc, psx=psx: e.activation(
                out=hfm[:, hc*512:(hc+1)*512], in_=psx[:, :], func=AF.Gelu),
                wait=((s_+"h1m", P.val(s_+"h1m")), (s_+"y2m", P.val(s_+"y2m"))),
                inc=((s_+"h1e", 1),))
        h1v = (s_+"h1e", P.val(s_+"h1e"))
        for ocg in range(2):
            for hc in range(32):
                P.rec("gpsimd", lambda e, hc=hc, ocg=ocg: e.dma_start(
                    out=w2sb[:, hc*512:(hc+1)*512],
                    in_=W["L0_w2_my"].rearrange("(hc p) o -> hc p o", p=128)[hc]
                        [:, ocg*512:(ocg+1)*512]),
                    wait=([(s_+"y2m", P.val(s_+"y2m"))] if hc == 0 else ()),
                    inc=((s_+"w2l", 16),))
            w2lv = (s_+"w2l", P.val(s_+"w2l"))
            for oc_in in range(4):
                oc = ocg * 4 + oc_in
                psx = ps_a if oc % 2 == 0 else ps_b
                for hc in range(32):
                    P.rec("tensor", lambda e, oc_in=oc_in, hc=hc, psx=psx: e.matmul(
                        psx[:, :], w2sb[:, hc*512 + oc_in*128: hc*512 + (oc_in+1)*128],
                        hfm[:, hc*512:(hc+1)*512],
                        start=(hc == 0), stop=(hc == 31)),
                        wait=([h1v, w2lv, (s_+"y2e", P.val(s_+"y2e") - 1)] if hc == 0 else ()),
                        inc=((s_+"y2m", 1),) if hc == 31 else ())
                P.rec("vector", lambda e, oc=oc, psx=psx: e.tensor_copy(
                    yfm[:, oc*512:(oc+1)*512], psx[:, :]),
                    wait=((s_+"y2m", P.val(s_+"y2m")), (s_+"ytt", P.val(s_+"ytt"))),
                    inc=((s_+"y2e", 1),))
        y2v = (s_+"y2e", P.val(s_+"y2e"))
        for sub in range(4):
            for oc in range(8):
                n = P.val(s_+"ytt")
                sl = 1 + n % 3
                P.rec("tensor", lambda e, oc=oc, sub=sub, sl=sl: e.transpose(
                    pts[sl][:, 0:64].bitcast(BF16),
                    yfm[:, oc*512 + sub*128: oc*512 + (sub+1)*128], identb[:, :]),
                    wait=[y2v, (s_+"yte", P.val(s_+"yte") - 2)],
                    inc=((s_+"ytt", 1),))
                P.rec("vector", lambda e, oc=oc, sl=sl: e.tensor_copy(
                    ytm[:, oc*128:(oc+1)*128], pts[sl][:, 0:64].bitcast(BF16)),
                    wait=((s_+"ytt", P.val(s_+"ytt")), (s_+"ymw", P.val(s_+"ymw"))),
                    inc=((s_+"yte", 1),))
            P.rec("vector", lambda e, sub=sub: e.tensor_scalar(
                ytm[:, :], ytm[:, :], rvw4[:, sub:sub+1], None, ALU.mult),
                wait=((s_+"yte", P.val(s_+"yte")), (s_+"rwl", P.val(s_+"rwl"))),
                inc=((s_+"ymu", 1),))
            P.rec("sync", lambda e, grp=grp, sub=sub: e.dma_start(
                out=DR["ysl"][(grp*4+sub)*128:(grp*4+sub+1)*128, :], in_=ytm[:, :]),
                wait=((s_+"ymu", P.val(s_+"ymu")),), inc=((s_+"ymw", 16),))
    P.rec("gpsimd", lambda e: e.collective_compute(
        "AllGather", ALU.bypass, replica_groups=RG8,
        ins=[DR["ysl"].opt()], outs=[DR["agy"].opt()]),
        wait=((s_+"ymw", P.val(s_+"ymw")),), inc=(("@f0y", 1),))
    agyv = ("@f0y", P.val("@f0y"))
    A.pop()
    P.gate((agyv, (s_+"yte", P.val(s_+"yte")), (s_+"ymu", P.val(s_+"ymu"))))

    A.push()
    scol2 = A.sb("f0_scol2", [128, 1], F32)
    sci2 = A.sb("f0_sci2", [128, 1], I32)
    yj = A.sb("f0_yj", [128, 1024], BF16)
    gacc = A.sb("f0_gacc", [128, 1024], F32)
    gtm = A.sb("f0_gtm", [128, 1024], BF16)
    gstg = A.sb("f0_gstg", [128, 2 * 128], BF16)
    for cc in range(16):
        for j in range(2):
            P.rec("sync", lambda e, cc=cc, j=j: e.dma_start(
                out=scol2[:, 0:1], in_=mkap(DR["srd"], cc*256 + j, [[2, 128], [1, 1]]),
                allow_slow_non_contiguous=True),
                wait=[(s_+"si2", P.val(s_+"si2"))], inc=((s_+"sl2", 16),))
            P.rec("vector", lambda e: e.tensor_copy(sci2[:, :], scol2[:, 0:1]),
                  wait=((s_+"sl2", P.val(s_+"sl2")), (s_+"yjl", P.val(s_+"yjl"))),
                  inc=((s_+"si2", 1),))
            P.rec("gpsimd", lambda e: e.indirect_dma_start(
                out=yj[:, :], out_offset=None, in_=DR["agy"],
                in_offset=IndirectOffsetOnAxis(sci2[:, :], 0),
                ),
                wait=[agyv, (s_+"si2", P.val(s_+"si2")), (s_+"gup", P.val(s_+"gup"))],
                inc=((s_+"yjl", 16),))
            if j == 0:
                P.rec("vector", lambda e: e.tensor_copy(gacc[:, :], yj[:, :]),
                      wait=((s_+"yjl", P.val(s_+"yjl")),), inc=((s_+"gup", 1),))
            else:
                P.rec("vector", lambda e: e.tensor_add(gacc[:, :], gacc[:, :], yj[:, :]),
                      wait=((s_+"yjl", P.val(s_+"yjl")),), inc=((s_+"gup", 1),))
        P.rec("scalar", lambda e: e.activation(out=gtm[:, :], in_=gacc[:, :], func=AF.Silu),
              wait=((s_+"gup", P.val(s_+"gup")), (s_+"gtt", P.val(s_+"gtt"))),
              inc=((s_+"gse", 1),))
        for kc in range(NKC):
            n = P.val(s_+"gtt")
            sl = 1 + n % 3
            par = n % 2
            P.rec("tensor", lambda e, kc=kc, sl=sl: e.transpose(
                pts[sl][:, 0:64].bitcast(BF16), gtm[:, kc*128:(kc+1)*128], identb[:, :]),
                wait=[(s_+"gse", P.val(s_+"gse")), (s_+"gte", P.val(s_+"gte") - 2)],
                inc=((s_+"gtt", 1),))
            P.rec("vector", lambda e, par=par, sl=sl: e.tensor_copy(
                gstg[:, par*128:(par+1)*128], pts[sl][:, 0:64].bitcast(BF16)),
                wait=((s_+"gtt", P.val(s_+"gtt")),
                      (s_+"gw" + str(par), P.val(s_+"gw" + str(par)))),
                inc=((s_+"gte", 1),))
            P.rec("sync", lambda e, kc=kc, cc=cc, par=par: e.dma_start(
                out=gdr[kc*128:(kc+1)*128, cc*128:(cc+1)*128],
                in_=gstg[:, par*128:(par+1)*128]),
                wait=((s_+"gte", P.val(s_+"gte")),), inc=((s_+"gw" + str(par), 16),))
    A.pop()
    return [(s_+"gw0", P.val(s_+"gw0")), (s_+"gw1", P.val(s_+"gw1"))]

def make_inputs_l0f(I, core, x_cur):
    o = make_consts()
    o["coresel"] = np.zeros((128, 8), np.float32)
    xs = x_cur[:, 1024*core:1024*(core+1), :].reshape(TOK, D)
    o["x_in"] = np.ascontiguousarray(xs)
    rw = I["L0_ffn_router_w"]
    o["L0_rw_cols"] = np.stack([rw[e, kc*128:(kc+1)*128]
                                for kc in range(8) for e in range(8)], axis=1)
    o["L0_w1_my"] = I["L0_ffn_w1"][core]
    o["L0_w2_my"] = I["L0_ffn_w2"][core]
    o["tri8"] = np.triu(np.ones((8, 8), np.float32), 1)
    o["step8"] = (np.arange(8) < core).astype(np.float32).reshape(8, 1)
    p = np.arange(128)[:, None]
    o["moewin2"] = (core*4096 + 32*p).astype(np.int32)
    o["moerowc"] = (core*2048 + np.arange(16)[None, :]*128 + p).astype(np.float32)
    return {k2: np.ascontiguousarray(v) for k2, v in o.items()}

def emit_l1_attn(P, A, xn, lnxn, W, DR, x_dram):
    s_ = "a1_"
    onesc = A.sb("ones_col", [128, 1], F32)
    onesr = A.sb("ones_row", [1, 128], F32)
    identb = A.sb("ident_bf", [128, 128], BF16)
    identf = A.sb("identf_sb", [128, 128], F32)
    epsc = A.sb("epscol_sb", [128, 4], F32)
    pts = ptbanks(A)
    ps_a = A.ps("ps_a", [128, 512], F32)
    ps_b = A.ps("ps_b", [128, 512], F32)
    ps_c = A.ps("ps_c", [128, 512], F32)
    ps_w = A.ps("ps_bc", [128, 512], F32)
    sdr = DR["s_dram"]

    A.push()
    win_sb = A.sb("a1_win", [128, NKC * 2048], BF16)
    cwx = A.sb("a1_cwx", [128, 32], F32)
    cwz = A.sb("a1_cwz", [128, 32], F32)
    cbx = A.sb("a1_cbx", [128, 8], F32)
    cbz = A.sb("a1_cbz", [128, 8], F32)
    bstg = A.sb("a1_bstg", [128, 2], F32)
    hidx = A.sb("a1_hidx", [128, 48], I32)
    xh_fm = A.sb("a1_xhfm", [128, 48], F32)
    hrows = A.sb("a1_hrows", [1, 48], F32)
    hrow2 = A.sb("a1_hrow2", [1, 48], F32)
    hsq = A.sb("a1_hsq", [128, 48], F32)
    mcols = A.sb("a1_mcols", [128, 6], F32)
    rcols = A.sb("a1_rcols", [128, 6], F32)
    mS = A.sb("a1_mS", [128, 6], F32)
    swin = A.sb("a1_swin", [128, 16], F32)
    xzh = A.sb("a1_xzh", [128, 16 * 6], F32)
    xzslab = A.sb("a1_xzs", [128, 2048], F32)
    ext = A.sb("a1_ext", [128, 1027], F32)
    cacc = A.sb("a1_cacc", [128, 1024], F32)
    ctmp = A.sb("a1_ctmp", [128, 1024], F32)
    slb = A.sb("a1_slb", [128, 1024], BF16)
    tstg = A.sb("a1_tstg", [128, 2 * 128], BF16)
    for kc in range(NKC):
        P.rec("gpsimd", lambda e, kc=kc: e.dma_start(
            out=win_sb[:, kc*2048:(kc+1)*2048],
            in_=W["L1_Win"].rearrange("(kc p) o -> kc p o", p=128)[kc]),
            inc=((s_+"wg", 16),))
    for nm, dst in (("L1_cwx", cwx), ("L1_cwz", cwz), ("L1_cbx", cbx), ("L1_cbz", cbz),
                    ("l1_hidx", hidx)):
        P.rec("sync", lambda e, nm=nm, dst=dst: e.dma_start(out=dst[:, :], in_=W[nm]),
              inc=((s_+"w", 16),))
    wv0 = [(s_+"w", P.val(s_+"w")), (s_+"wg", P.val(s_+"wg")), lnxn]
    xr = x_dram.rearrange("(kc p) t -> kc p t", p=128)
    for kc in range(NKC):
        for ci, tcol in enumerate((0, 1, 1023, 1024, 1025, 2047)):
            P.rec("sync", lambda e, kc=kc, ci=ci, tcol=tcol: e.dma_start(
                out=DR["bnd_loc"][kc*128:(kc+1)*128, ci:ci+1], in_=xr[kc][:, tcol:tcol+1],
                allow_slow_non_contiguous=True),
                inc=((s_+"bp", 16),))
    P.rec("vector", lambda e: e.memset(bstg[:, :], 0.0), inc=((s_+"bz", 1),))
    for ci in range(3):
        P.rec("sync", lambda e, ci=ci: e.dma_start(
            out=DR["bnd_loc"][1024:1152, ci*2:(ci+1)*2], in_=bstg[:, :],
            allow_slow_non_contiguous=True),
            wait=((s_+"bz", 1),) if ci == 0 else (), inc=((s_+"bp", 16),))
    P.rec("sync", lambda e: e.dma_start(
        out=DR["bnd_loc"][1024:1152, 6:8], in_=bstg[:, :],
        allow_slow_non_contiguous=True), inc=((s_+"bp", 16),))
    for kc in range(NKC):
        P.rec("sync", lambda e, kc=kc: e.dma_start(
            out=DR["bnd_loc"][kc*128:(kc+1)*128, 6:8], in_=bstg[:, :],
            allow_slow_non_contiguous=True), inc=((s_+"bp", 16),))
    P.rec("gpsimd", lambda e: e.collective_compute(
        "AllGather", ALU.bypass, replica_groups=RG8,
        ins=[DR["bnd_loc"].opt()], outs=[DR["bnd_all"].opt()]),
        wait=((s_+"bp", P.val(s_+"bp")),), inc=(("@l1b", 1),))
    HCOL = (2, 0, 1, 5, 3, 4)
    for kc in range(NKC):
        for hv in range(6):
            c = HCOL[hv]
            P.rec("gpsimd", lambda e, kc=kc, hv=hv, c=c: e.indirect_dma_start(
                out=xh_fm[:, kc*6+hv:kc*6+hv+1], out_offset=None,
                in_=mkap(DR["bnd_all"], 0, [[1, 73728], [1, 1]]),
                in_offset=IndirectOffsetOnAxis(hidx[:, kc*6+hv:kc*6+hv+1], 0)),
                wait=[("@l1b", P.val("@l1b")), (s_+"w", P.val(s_+"w"))],
                inc=((s_+"hg", 16),))
    hgv = (s_+"hg", P.val(s_+"hg"))
    P.rec("scalar", lambda e: e.activation(out=hsq[:, :], in_=xh_fm[:, :], func=AF.Square),
          wait=[hgv], inc=((s_+"hsq", 1),))
    P.rec("tensor", lambda e: e.matmul(ps_w[0:1, 0:48], onesc[:, :], xh_fm[:, :],
                                       start=True, stop=True),
          wait=[hgv, (s_+"hse", P.val(s_+"hse"))], inc=((s_+"hsm", 1),))
    P.rec("vector", lambda e: e.tensor_copy(hrows[0:1, :], ps_w[0:1, 0:48]),
          wait=((s_+"hsm", P.val(s_+"hsm")),), inc=((s_+"hse", 1),))
    P.rec("tensor", lambda e: e.matmul(ps_w[0:1, 0:48], onesc[:, :], hsq[:, :],
                                       start=True, stop=True),
          wait=((s_+"hsq", 1), (s_+"hse", P.val(s_+"hse"))), inc=((s_+"hsm", 1),))
    P.rec("vector", lambda e: e.tensor_copy(hrow2[0:1, :], ps_w[0:1, 0:48]),
          wait=((s_+"hsm", P.val(s_+"hsm")),), inc=((s_+"hse", 1),))
    for half in (24, 12, 6):
        P.rec("vector", lambda e, half=half: e.tensor_add(
            hrows[0:1, 0:half], hrows[0:1, 0:half], hrows[0:1, half:2*half]))
        P.rec("vector", lambda e, half=half: e.tensor_add(
            hrow2[0:1, 0:half], hrow2[0:1, 0:half], hrow2[0:1, half:2*half]))
    P.rec("vector", lambda e: e.tensor_scalar_mul(hrows[0:1, 0:6], hrows[0:1, 0:6], 1.0 / D))
    P.rec("vector", lambda e: e.tensor_scalar_mul(hrow2[0:1, 0:6], hrow2[0:1, 0:6], 1.0 / D))
    P.rec("vector", lambda e: e.tensor_mul(hrow2[0:1, 6:12], hrows[0:1, 0:6], hrows[0:1, 0:6]))
    P.rec("vector", lambda e: e.tensor_sub(hrow2[0:1, 0:6], hrow2[0:1, 0:6], hrow2[0:1, 6:12]),
          inc=((s_+"hv1", 1),))
    P.rec("scalar", lambda e: e.activation(out=hrow2[0:1, 0:6], in_=hrow2[0:1, 0:6],
                                           func=AF.Sqrt, bias=epsc[0:1, 0:1]),
          wait=((s_+"hv1", 1),), inc=((s_+"hv2", 1),))
    P.rec("vector", lambda e: e.reciprocal(hrow2[0:1, 0:6], hrow2[0:1, 0:6]),
          wait=((s_+"hv2", 1),), inc=((s_+"hv3", 1),))
    P.rec("tensor", lambda e: e.matmul(ps_w[0:128, 0:6], onesr[0:1, :], hrows[0:1, 0:6],
                                       start=True, stop=True),
          wait=((s_+"hv3", 1), (s_+"hse", P.val(s_+"hse"))), inc=((s_+"hsm", 1),))
    P.rec("vector", lambda e: e.tensor_copy(mcols[:, :], ps_w[0:128, 0:6]),
          wait=((s_+"hsm", P.val(s_+"hsm")),), inc=((s_+"hse", 1),))
    P.rec("tensor", lambda e: e.matmul(ps_w[0:128, 0:6], onesr[0:1, :], hrow2[0:1, 0:6],
                                       start=True, stop=True),
          wait=((s_+"hse", P.val(s_+"hse")),), inc=((s_+"hsm", 1),))
    P.rec("vector", lambda e: e.tensor_copy(rcols[:, :], ps_w[0:128, 0:6]),
          wait=((s_+"hsm", P.val(s_+"hsm")),), inc=((s_+"hcv", 1),))
    onesb = A.sb("ones_col_bf", [128, 1], BF16)
    for oc2 in range(16):
        for kc in range(NKC):
            P.rec("tensor", lambda e, oc2=oc2, kc=kc: e.matmul(
                ps_c[:, 0:1], win_sb[:, kc*2048 + oc2*128: kc*2048 + (oc2+1)*128],
                onesb[:, 0:1], start=(kc == 0), stop=(kc == NKC - 1)),
                wait=(wv0 + [(s_+"swe", P.val(s_+"swe"))]) if kc == 0 else (),
                inc=((s_+"swm", 1),) if kc == NKC - 1 else ())
        P.rec("vector", lambda e, oc2=oc2: e.tensor_copy(swin[:, oc2:oc2+1], ps_c[:, 0:1]),
              wait=((s_+"swm", P.val(s_+"swm")),), inc=((s_+"swe", 1),))
    xhbf = A.sb("a1_xhbf", [128, 48], BF16)
    P.rec("vector", lambda e: e.tensor_copy(xhbf[:, :], xh_fm[:, :]),
          wait=[hgv], inc=((s_+"hbf", 1),))
    for oc2 in range(16):
        for kc in range(NKC):
            P.rec("tensor", lambda e, oc2=oc2, kc=kc: e.matmul(
                ps_c[0:128, 0:6], win_sb[:, kc*2048 + oc2*128: kc*2048 + (oc2+1)*128],
                xhbf[:, kc*6:(kc+1)*6], start=(kc == 0), stop=(kc == NKC - 1)),
                wait=([(s_+"hbf", 1), (s_+"swe", P.val(s_+"swe")),
                       (s_+"zhe", P.val(s_+"zhe"))] if kc == 0 else ()),
                inc=((s_+"zhm", 1),) if kc == NKC - 1 else ())
        P.rec("vector", lambda e, oc2=oc2: e.tensor_copy(xzh[:, oc2*6:(oc2+1)*6], ps_c[0:128, 0:6]),
              wait=((s_+"zhm", P.val(s_+"zhm")),), inc=((s_+"zhe", 1),))
        P.rec("vector", lambda e, oc2=oc2: e.tensor_scalar(
            mS[:, :], mcols[:, :], swin[:, oc2:oc2+1], None, ALU.mult),
            wait=((s_+"hcv", 1),))
        P.rec("vector", lambda e, oc2=oc2: e.tensor_sub(
            xzh[:, oc2*6:(oc2+1)*6], xzh[:, oc2*6:(oc2+1)*6], mS[:, :]))
        P.rec("vector", lambda e, oc2=oc2: e.tensor_mul(
            xzh[:, oc2*6:(oc2+1)*6], xzh[:, oc2*6:(oc2+1)*6], rcols[:, :]),
            inc=((s_+"zhc", 1),))
    zhcv = (s_+"zhc", P.val(s_+"zhc"))
    for oc2 in range(16):
        for tt in range(4):
            psx = ps_a if tt % 2 == 0 else ps_b
            for kc in range(NKC):
                P.rec("tensor", lambda e, oc2=oc2, tt=tt, kc=kc, psx=psx: e.matmul(
                    psx[:, :], win_sb[:, kc*2048 + oc2*128: kc*2048 + (oc2+1)*128],
                    xn[:, kc*TOK + tt*512: kc*TOK + (tt+1)*512],
                    start=(kc == 0), stop=(kc == NKC - 1)),
                    wait=(wv0 + [(s_+"ze", P.val(s_+"ze") - 1)]) if kc == 0 else (),
                    inc=((s_+"zm", 1),) if kc == NKC - 1 else ())
            P.rec("vector", lambda e, tt=tt, psx=psx: e.tensor_copy(
                xzslab[:, tt*512:(tt+1)*512], psx[:, :]),
                wait=((s_+"zm", P.val(s_+"zm")), (s_+"cvu", P.val(s_+"cvu"))),
                inc=((s_+"ze", 1),))
        zev = (s_+"ze", P.val(s_+"ze"))
        isx = oc2 < 8
        o8 = oc2 % 8
        cw = cwx if isx else cwz
        cb = cbx if isx else cbz
        for b in range(2):
            P.rec("vector", lambda e, b=b: e.tensor_copy(ext[:, 1:1025], xzslab[:, b*1024:(b+1)*1024]),
                  wait=[zev])
            P.rec("vector", lambda e, oc2=oc2, b=b: e.tensor_copy(
                ext[:, 0:1], xzh[:, oc2*6 + b*3: oc2*6 + b*3 + 1]), wait=[zhcv])
            P.rec("vector", lambda e, oc2=oc2, b=b: e.tensor_copy(
                ext[:, 1025:1027], xzh[:, oc2*6 + b*3 + 1: oc2*6 + b*3 + 3]),
                inc=((s_+"exv", 1),))
            P.rec("vector", lambda e, o8=o8, cw=cw: e.tensor_scalar(
                cacc[:, :], ext[:, 0:1024], cw[:, o8*4:o8*4+1], None, ALU.mult),
                wait=((s_+"exv", P.val(s_+"exv")), (s_+"slv", P.val(s_+"slv"))))
            for j in range(1, 4):
                P.rec("vector", lambda e, o8=o8, j=j, cw=cw: e.tensor_scalar(
                    ctmp[:, :], ext[:, j:j+1024], cw[:, o8*4+j:o8*4+j+1], None, ALU.mult))
                P.rec("vector", lambda e: e.tensor_add(cacc[:, :], cacc[:, :], ctmp[:, :]))
            P.rec("vector", lambda e, o8=o8, cb=cb: e.tensor_scalar(
                cacc[:, :], cacc[:, :], cb[:, o8:o8+1], None, ALU.add),
                inc=((s_+"cvu", 1),))
            P.rec("scalar", lambda e: e.activation(out=slb[:, :], in_=cacc[:, :], func=AF.Silu),
                  wait=((s_+"cvu", P.val(s_+"cvu")), (s_+"slu", P.val(s_+"slu"))),
                  inc=((s_+"slv", 1),))
            if isx:
                for ti in range(8):
                    n = P.val(s_+"xtt")
                    sl = 1 + n % 3
                    par = n % 2
                    P.rec("tensor", lambda e, ti=ti, sl=sl: e.transpose(
                        pts[sl][:, 0:64].bitcast(BF16), slb[:, ti*128:(ti+1)*128],
                        identb[:, :]),
                        wait=[(s_+"slv", P.val(s_+"slv")), (s_+"xtc", P.val(s_+"xtc") - 2)],
                        inc=((s_+"xtt", 1),))
                    P.rec("vector", lambda e, par=par, sl=sl: e.tensor_copy(
                        tstg[:, par*128:(par+1)*128], pts[sl][:, 0:64].bitcast(BF16)),
                        wait=((s_+"xtt", P.val(s_+"xtt")),
                              (s_+"xw" + str(par), P.val(s_+"xw" + str(par)))),
                        inc=((s_+"xtc", 1),))
                    P.rec("sync", lambda e, oc2=oc2, b=b, ti=ti, par=par: e.dma_start(
                        out=DR["xs_tm"][b*1024 + ti*128: b*1024 + (ti+1)*128,
                                        oc2*128:(oc2+1)*128],
                        in_=tstg[:, par*128:(par+1)*128]),
                        wait=((s_+"xtc", P.val(s_+"xtc")),),
                        inc=((s_+"xw" + str(par), 16),))
            else:
                P.rec("sync", lambda e, o8=o8, b=b: e.dma_start(
                    out=DR["zs_fm"][o8*128:(o8+1)*128, b*1024:(b+1)*1024],
                    in_=slb[:, :]),
                    wait=((s_+"slv", P.val(s_+"slv")),), inc=((s_+"zw", 16),))
            P.rec("scalar", lambda e: None,
                  wait=((s_+"zw", P.val(s_+"zw")), (s_+"xtt", P.val(s_+"xtt"))),
                  inc=((s_+"slu", 1),))
    P.rec("gpsimd", lambda e: e.collective_compute(
        "AllGather", ALU.bypass, replica_groups=RG8,
        ins=[DR["xs_tm"].opt()], outs=[DR["agxs"].opt()]),
        wait=((s_+"xw0", P.val(s_+"xw0")), (s_+"xw1", P.val(s_+"xw1"))),
        inc=(("@l1x", 1),))
    agxv = ("@l1x", P.val("@l1x"))
    A.pop()
    P.gate((agxv, (s_+"zw", P.val(s_+"zw")), (s_+"zhc", P.val(s_+"zhc")),
            (s_+"slu", P.val(s_+"slu"))))
    return agxv

def emit_l1_attn_core(P, A, W, DR, agxv):
    s_ = "a1_"
    onesc = A.sb("ones_col", [128, 1], F32)
    onesr = A.sb("ones_row", [1, 128], F32)
    identb = A.sb("ident_bf", [128, 128], BF16)
    identf = A.sb("identf_sb", [128, 128], F32)
    epsc = A.sb("epscol_sb", [128, 4], F32)
    pts = ptbanks(A)
    ps_a = A.ps("ps_a", [128, 512], F32)
    ps_b = A.ps("ps_b", [128, 512], F32)
    ps_c = A.ps("ps_c", [128, 512], F32)
    ps_w = A.ps("ps_bc", [128, 512], F32)
    sdr = DR["s_dram"]
    A.push()
    wqk = A.sb("a1_wqk", [128, NKC * 512], BF16)
    wvg = A.sb("a1_wvg", [128, NKC * 768], BF16)
    wbg = A.sb("a1_wbg", [128, NKC * 2], BF16)
    wsoT = A.sb("a1_wsoT", [128, NKC * 384], BF16)
    wo1 = A.sb("a1_wo1", [128, NKC * 1024], BF16)
    bias2 = A.sb("a1_bias2", [128, 2], F32)
    alogc = A.sb("a1_alog", [128, 1], F32)
    naexp = A.sb("a1_naexp", [128, 1], F32)
    gnwrow = A.sb("a1_gnwr", [1, 384], F32)
    gnwb = A.sb("a1_gnwb", [128, 384], F32)
    negtril = A.sb("a1_negtril", [128, 128], F32)
    triuinc = A.sb("a1_triuinc", [128, 128], F32)
    gidx = A.sb("a1_gidx", [128, 64], I32)
    yidx = A.sb("a1_yidx", [128, 64], I32)
    for nm, dst, eng in (("L1_Wqk_my", wqk, "g"), ("L1_Wvg_my", wvg, "g"),
                         ("L1_Wbgk_my", wbg, "g"), ("L1_WsoT_my", wsoT, "g"),
                         ("L1_Wout1", wo1, "g")):
        C = dst.shape[1] // NKC
        for kc in range(NKC):
            P.rec("gpsimd", lambda e, nm=nm, dst=dst, kc=kc, C=C: e.dma_start(
                out=dst[:, kc*C:(kc+1)*C],
                in_=W[nm].rearrange("(kc p) o -> kc p o", p=128)[kc]),
                inc=((s_+"w2g", 16),))
    for nm, dst in (("L1_bias2", bias2), ("L1_alog", alogc), ("L1_gnw", gnwrow),
                    ("L1_negtril", negtril), ("L1_triuinc", triuinc),
                    ("l1_gidx", gidx), ("l1_yidx", yidx)):
        P.rec("sync", lambda e, nm=nm, dst=dst: e.dma_start(out=dst[:, :], in_=W[nm]),
              inc=((s_+"w2", 16),))
    wv1 = [(s_+"w2", P.val(s_+"w2")), (s_+"w2g", P.val(s_+"w2g"))]
    P.rec("scalar", lambda e: e.activation(out=naexp[:, :], in_=alogc[:, :], func=AF.Exp),
          wait=wv1, inc=((s_+"nae", 1),))
    P.rec("vector", lambda e: e.tensor_scalar_mul(naexp[:, :], naexp[:, :], -1.0),
          wait=((s_+"nae", 1),), inc=((s_+"nae2", 1),))
    P.rec("tensor", lambda e: e.matmul(ps_w[0:128, 0:384], onesr[0:1, :], gnwrow[0:1, :],
                                       start=True, stop=True),
          wait=wv1, inc=((s_+"gnm", 1),))
    P.rec("vector", lambda e: e.tensor_copy(gnwb[:, :], ps_w[0:128, 0:384]),
          wait=((s_+"gnm", 1),), inc=((s_+"gnv", 1),))
    wcomb = A.sb("a1_wcomb", [128, 3 * 1024], BF16)
    for ds in range(3):
        for oc in range(2):
            psx = ps_a if oc % 2 == 0 else ps_b
            for kc in range(NKC):
                P.rec("tensor", lambda e, ds=ds, oc=oc, kc=kc, psx=psx: e.matmul(
                    psx[:, :], wsoT[:, kc*384 + ds*128: kc*384 + (ds+1)*128],
                    wo1[:, kc*1024 + oc*512: kc*1024 + (oc+1)*512],
                    start=(kc == 0), stop=(kc == NKC - 1)),
                    wait=(wv1 + [(s_+"wce", P.val(s_+"wce") - 1)]) if kc == 0 else (),
                    inc=((s_+"wcm", 1),) if kc == NKC - 1 else ())
            P.rec("vector", lambda e, ds=ds, oc=oc, psx=psx: e.tensor_copy(
                wcomb[:, ds*1024 + oc*512: ds*1024 + (oc+1)*512], psx[:, :]),
                wait=((s_+"wcm", P.val(s_+"wcm")),), inc=((s_+"wce", 1),))
    wcv = (s_+"wce", P.val(s_+"wce"))
    q_fm1 = A.sb("a1_qfm1", [128, 8192], BF16)
    q_fm2 = A.sb("a1_qfm2", [128, 8192], BF16)
    k_fm1 = A.sb("a1_kfm1", [128, 8192], BF16)
    k_fm2 = A.sb("a1_kfm2", [128, 8192], BF16)
    S1 = A.sb("a1_S1", [128, 384], F32)
    xsg = A.sb("a1_xsg", [128, 1024], BF16)
    xsgf = A.sb("a1_xsgf", [128, NKC * 128], BF16)
    qk_tm = A.sb("a1_qktm", [128, 512], F32)
    qkn = A.sb("a1_qkn", [128, 512], BF16)
    sqs = A.sb("a1_sqs", [128, 192], F32)
    ncl = A.sb("a1_ncl", [128, 4], F32)
    bgk2 = A.sb("a1_bgk2", [128, 2], F32)
    spc = A.sb("a1_spc", [128, 4], F32)
    vgstg = A.sb("a1_vgstg", [128, 768], BF16)
    S2p = A.sb("a1_S2p", [128, 384], F32)
    P.rec("vector", lambda e: e.memset(S1[:, :], 0.0))
    P.rec("vector", lambda e: e.memset(S2p[:, :], 0.0))
    P.rec("vector", lambda e: e.memset(qkn[:, :], 0.0), inc=((s_+"s0", 1),))
    for cq in range(64):
        P.rec("gpsimd", lambda e, cq=cq: e.indirect_dma_start(
            out=xsg[:, :], out_offset=None, in_=DR["agxs"],
            in_offset=IndirectOffsetOnAxis(gidx[:, cq:cq+1], 0)),
            wait=[agxv] + wv1 + [(s_+"xgt", P.val(s_+"xgt"))],
            inc=((s_+"xgl", 16),))
        for kc in range(NKC):
            n = P.val(s_+"xgt")
            sl = 1 + n % 3
            P.rec("tensor", lambda e, kc=kc, sl=sl: e.transpose(
                pts[sl][:, 0:64].bitcast(BF16), xsg[:, kc*128:(kc+1)*128], identb[:, :]),
                wait=[(s_+"xgl", P.val(s_+"xgl")), (s_+"xge", P.val(s_+"xge") - 2)] +
                     ([(s_+"pjm", P.val(s_+"pjm"))] if kc == 0 else []),
                inc=((s_+"xgt", 1),))
            P.rec("vector", lambda e, kc=kc, sl=sl: e.tensor_copy(
                xsgf[:, kc*128:(kc+1)*128], pts[sl][:, 0:64].bitcast(BF16)),
                wait=((s_+"xgt", P.val(s_+"xgt")),), inc=((s_+"xge", 1),))
        xgv = (s_+"xge", P.val(s_+"xge"))
        for kc in range(NKC):
            P.rec("tensor", lambda e, kc=kc: e.matmul(
                ps_a[:, :], xsgf[:, kc*128:(kc+1)*128], wqk[:, kc*512:(kc+1)*512],
                start=(kc == 0), stop=(kc == NKC - 1)),
                wait=([xgv, (s_+"qke", P.val(s_+"qke"))] if kc == 0 else ()),
                inc=((s_+"pjm", 1),))
        P.rec("vector", lambda e: e.tensor_copy(qk_tm[:, :], ps_a[:, :]),
              wait=((s_+"pjm", P.val(s_+"pjm")),), inc=((s_+"qke", 1),))
        for base, dst0 in ((0, 0), (256, 256)):
            P.rec("scalar", lambda e, base=base: e.activation(
                out=sqs[:, 0:192], in_=qk_tm[:, base:base+192], func=AF.Square,
                accum_out=ncl[:, 0:1]),
                wait=((s_+"qke", P.val(s_+"qke")), (s_+"l2e", P.val(s_+"l2e"))),
                inc=((s_+"l2s", 1),))
            P.rec("scalar", lambda e: e.activation(
                out=ncl[:, 1:2], in_=ncl[:, 0:1], func=AF.Sqrt, bias=epsc[:, 2:3]),
                inc=((s_+"l2r", 1),))
            P.rec("vector", lambda e: e.reciprocal(ncl[:, 1:2], ncl[:, 1:2]),
                  wait=((s_+"l2r", P.val(s_+"l2r")),))
            P.rec("vector", lambda e, base=base: e.tensor_scalar(
                qkn[:, base:base+192], qk_tm[:, base:base+192], ncl[:, 1:2], None, ALU.mult),
                inc=((s_+"l2e", 1),))
        l2v = (s_+"l2e", P.val(s_+"l2e"))
        for t, dst in ((0, q_fm1), (1, q_fm2), (2, k_fm1), (3, k_fm2)):
            n = P.val(s_+"qtt")
            sl = 1 + n % 3
            P.rec("tensor", lambda e, t=t, sl=sl: e.transpose(
                pts[sl][:, 0:64].bitcast(BF16), qkn[:, t*128:(t+1)*128], identb[:, :]),
                wait=[l2v, (s_+"qte", P.val(s_+"qte") - 2)],
                inc=((s_+"qtt", 1),))
            P.rec("vector", lambda e, cq=cq, dst=dst, sl=sl: e.tensor_copy(
                dst[:, cq*128:(cq+1)*128], pts[sl][:, 0:64].bitcast(BF16)),
                wait=((s_+"qtt", P.val(s_+"qtt")),), inc=((s_+"qte", 1),))
        for kc in range(NKC):
            P.rec("tensor", lambda e, kc=kc: e.matmul(
                ps_b[:, 0:2], xsgf[:, kc*128:(kc+1)*128], wbg[:, kc*2:(kc+1)*2],
                start=(kc == 0), stop=(kc == NKC - 1)),
                wait=([(s_+"bge", P.val(s_+"bge"))] if kc == 0 else ()),
                inc=((s_+"pjm", 1),))
        P.rec("scalar", lambda e: e.activation(
            out=bgk2[:, 0:1], in_=ps_b[:, 0:1], func=AF.Sigmoid, bias=bias2[:, 0:1]),
            wait=((s_+"pjm", P.val(s_+"pjm")), (s_+"bgw", P.val(s_+"bgw"))))
        P.rec("scalar", lambda e: e.activation(
            out=spc[:, 0:1], in_=ps_b[:, 1:2], func=AF.Identity, bias=bias2[:, 1:2]),
            inc=((s_+"bge", 1),))
        P.rec("vector", lambda e: e.tensor_scalar_mul(spc[:, 1:2], spc[:, 0:1], -1.0),
              wait=((s_+"bge", P.val(s_+"bge")),))
        P.rec("vector", lambda e: e.tensor_max(spc[:, 1:2], spc[:, 1:2], spc[:, 0:1]),
              inc=((s_+"spv", 1),))
        P.rec("scalar", lambda e: e.activation(out=spc[:, 2:3], in_=spc[:, 1:2],
                                               func=AF.Exp, scale=-1.0),
              wait=((s_+"spv", P.val(s_+"spv")),), inc=((s_+"spe", 1),))
        P.rec("vector", lambda e: e.tensor_scalar(spc[:, 2:3], spc[:, 2:3], 1.0, None, ALU.add),
              wait=((s_+"spe", P.val(s_+"spe")),), inc=((s_+"spv", 1),))
        P.rec("scalar", lambda e: e.activation(out=spc[:, 2:3], in_=spc[:, 2:3], func=AF.Ln),
              wait=((s_+"spv", P.val(s_+"spv")),), inc=((s_+"spe", 1),))
        P.rec("vector", lambda e: e.tensor_scalar(spc[:, 3:4], spc[:, 0:1], 0.0, None, ALU.max),
              wait=((s_+"spe", P.val(s_+"spe")),))
        P.rec("vector", lambda e: e.tensor_add(spc[:, 3:4], spc[:, 3:4], spc[:, 2:3]))
        P.rec("vector", lambda e: e.tensor_scalar(
            bgk2[:, 1:2], spc[:, 3:4], naexp[:, 0:1], None, ALU.mult),
            wait=((s_+"nae2", 1),), inc=((s_+"bgv", 1),))
        P.rec("sync", lambda e, cq=cq: e.dma_start(
            out=DR["bg_d"][cq*128:(cq+1)*128, :], in_=bgk2[:, :]),
            wait=((s_+"bgv", P.val(s_+"bgv")),), inc=((s_+"bgw", 16),))
        for kc in range(NKC):
            P.rec("tensor", lambda e, kc=kc: e.matmul(
                ps_c[:, 0:512], xsgf[:, kc*128:(kc+1)*128], wvg[:, kc*768:kc*768+512],
                start=(kc == 0), stop=(kc == NKC - 1)),
                wait=([(s_+"vge", P.val(s_+"vge"))] if kc == 0 else ()),
                inc=((s_+"pjm", 1),))
        P.rec("vector", lambda e: e.tensor_copy(vgstg[:, 0:512], ps_c[:, 0:512]),
              wait=((s_+"pjm", P.val(s_+"pjm")), (s_+"vgw", P.val(s_+"vgw"))),
              inc=((s_+"vge", 1),))
        for kc in range(NKC):
            P.rec("tensor", lambda e, kc=kc: e.matmul(
                ps_c[:, 0:256], xsgf[:, kc*128:(kc+1)*128], wvg[:, kc*768+512:(kc+1)*768],
                start=(kc == 0), stop=(kc == NKC - 1)),
                wait=([(s_+"vge", P.val(s_+"vge"))] if kc == 0 else ()),
                inc=((s_+"pjm", 1),))
        P.rec("vector", lambda e: e.tensor_copy(vgstg[:, 512:768], ps_c[:, 0:256]),
              wait=((s_+"pjm", P.val(s_+"pjm")),), inc=((s_+"vge", 1),))
        P.rec("sync", lambda e, cq=cq: e.dma_start(
            out=DR["vb_tm"][cq*128:(cq+1)*128, :], in_=vgstg[:, 0:384]),
            wait=((s_+"vge", P.val(s_+"vge")),), inc=((s_+"vgw", 16),))
        P.rec("sync", lambda e, cq=cq: e.dma_start(
            out=DR["gb_tm"][cq*128:(cq+1)*128, :], in_=vgstg[:, 384:768]),
            inc=((s_+"vgw", 16),))
    P.gate(((s_+"vgw", P.val(s_+"vgw")), (s_+"bgw", P.val(s_+"bgw")),
            (s_+"qte", P.val(s_+"qte"))))
    return dict()

def emit_l1_delta(P, A, W, DR):
    s_ = "a1_"
    onesc = A.sb("ones_col", [128, 1], F32)
    onesr = A.sb("ones_row", [1, 128], F32)
    identb = A.sb("ident_bf", [128, 128], BF16)
    identf = A.sb("identf_sb", [128, 128], F32)
    epsc = A.sb("epscol_sb", [128, 4], F32)
    pts = ptbanks(A)
    ps_a = A.ps("ps_a", [128, 512], F32)
    ps_b = A.ps("ps_b", [128, 512], F32)
    ps_c = A.ps("ps_c", [128, 512], F32)
    ps_w = A.ps("ps_bc", [128, 512], F32)
    q_fm1 = A.sb("a1_qfm1", [128, 8192], BF16)
    q_fm2 = A.sb("a1_qfm2", [128, 8192], BF16)
    k_fm1 = A.sb("a1_kfm1", [128, 8192], BF16)
    k_fm2 = A.sb("a1_kfm2", [128, 8192], BF16)
    S1 = A.sb("a1_S1", [128, 384], F32)
    S2 = A.sb("a1_S2p", [128, 384], F32)
    cumtri = A.sb("a1_cumtrif", [128, 128], F32)
    negtril = A.sb("a1_negtril", [128, 128], F32)
    negtriu = A.sb("a1_negtriu", [128, 128], F32)
    triuinc = A.sb("a1_triuinc", [128, 128], F32)
    yidx = A.sb("a1_yidx", [128, 64], I32)
    gnwb = A.sb("a1_gnwb", [128, 384], F32)
    wcomb = A.sb("a1_wcomb", [128, 3 * 1024], BF16)
    bgc = A.sb("a1_bgc", [128, 2], F32)
    vc_tm = A.sb("a1_vctm", [128, 384], BF16)
    gc_tm = A.sb("a1_gctm", [128, 384], BF16)
    bcum = A.sb("a1_bcum", [128, 1], F32)
    bcrow = A.sb("a1_bcrow", [1, 128], F32)
    Earg = A.sb("a1_Earg", [128, 128], F32)
    Etmp = A.sb("a1_Etmp", [128, 128], F32)
    E_f = A.sb("a1_Ef", [128, 128], F32)
    ET_f = A.sb("a1_ETf", [128, 128], F32)
    BCB = A.sb("a1_BCB", [128, 128], F32)
    bcr2 = A.sb("a1_bcr2", [1, 128], F32)
    Nf = A.sb("a1_Nf", [128, 128], F32)
    A_sb = A.sb("a1_A1", [128, 128], F32)
    B_sb = A.sb("a1_B1", [128, 128], F32)
    A2_sb = A.sb("a1_A2", [128, 128], F32)
    B2_sb = A.sb("a1_B2", [128, 128], F32)
    R_sb = A.sb("a1_R1", [128, 128], F32)
    Rt_bf = A.sb("a1_Rt", [128, 128], F32)
    ktm = A.sb("a1_ktm", [128, 256], BF16)
    bv_tm = A.sb("a1_bvtm", [128, 384], F32)
    bw_tm = A.sb("a1_bwtm", [128, 256], F32)
    kp_bf = A.sb("a1_kpbf", [128, 256], BF16)
    S1b = A.sb("a1_S1b", [128, 384], BF16)
    S2b = A.sb("a1_S2b", [128, 384], BF16)
    P_tm = A.sb("a1_Ptm", [128, 384], F32)
    W_tm = A.sb("a1_Wtm", [128, 256], F32)
    Wf = A.sb("a1_Wf", [128, 256], F32)
    U_bf = A.sb("a1_Ubf", [128, 384], BF16)
    O_tm = A.sb("a1_Otm", [128, 384], F32)
    AqkT = A.sb("a1_AqkT", [128, 128], BF16)
    osq = A.sb("a1_osq", [128, 384], F32)
    ncl = A.sb("a1_ncl", [128, 4], F32)
    btcol = A.sb("a1_btcol", [128, 4], F32)
    gs = A.sb("a1_gs", [128, 384], F32)
    o3_bf = A.sb("a1_o3", [128, 384], BF16)
    ofm = A.sb("a1_ofm", [128, 3 * 128], BF16)
    zb2 = A.sb("a1_zb2", [128, 1024], F32)
    ystg = zb2

    def mm(fn, extra=()):
        P.rec("tensor", fn,
              wait=[(s_+"de", P.val(s_+"de")), (s_+"ds", P.val(s_+"ds"))] + list(extra),
              inc=((s_+"dm", 1),))
    def ev(fn, extra=()):
        P.rec("vector", fn, wait=[(s_+"dm", P.val(s_+"dm")),
                                  (s_+"ds", P.val(s_+"ds"))] + list(extra),
              inc=((s_+"de", 1),))
    def sc(fn, extra=()):
        P.rec("scalar", fn, wait=[(s_+"dm", P.val(s_+"dm")),
                                  (s_+"de", P.val(s_+"de"))] + list(extra),
              inc=((s_+"ds", 1),))

    for nm, dst in (("L1_cumtri", cumtri), ("L1_negtril", negtril),
                    ("L1_negtriu", negtriu), ("L1_triuinc", triuinc)):
        P.rec("sync", lambda e, nm=nm, dst=dst: e.dma_start(out=dst[:, :], in_=W[nm]),
              inc=((s_+"w3", 16),))
    w3v = (s_+"w3", P.val(s_+"w3"))
    P.rec("vector", lambda e: e.memset(zb2[:, :], 0.0), inc=((s_+"zy", 1),))
    for n in range(128):
        P.rec("sync", lambda e, n=n: e.dma_start(
            out=DR["ypart"].rearrange("(n p) c -> n p c", p=128)[n], in_=zb2[:, :]),
            wait=((s_+"zy", 1),) if n == 0 else (), inc=((s_+"yz", 16),))
    yzv = (s_+"yz", P.val(s_+"yz"))
    P.rec("vector", lambda e: e.memset(S1b[:, :], 0.0))
    P.rec("vector", lambda e: e.memset(S2b[:, :], 0.0))
    P.rec("vector", lambda e: e.nop(), wait=((s_+"s0", 1),), inc=((s_+"de", 1),))

    for cq in range(64):
        ld = []
        P.rec("sync", lambda e, cq=cq: e.dma_start(
            out=bgc[:, :], in_=DR["bg_d"][cq*128:(cq+1)*128, :]),
            wait=[(s_+"dcon", P.val(s_+"dcon"))], inc=((s_+"dl", 16),))
        P.rec("sync", lambda e, cq=cq: e.dma_start(
            out=vc_tm[:, :], in_=DR["vb_tm"][cq*128:(cq+1)*128, :]),
            inc=((s_+"dl", 16),))
        P.rec("sync", lambda e, cq=cq: e.dma_start(
            out=gc_tm[:, :], in_=DR["gb_tm"][cq*128:(cq+1)*128, :]),
            inc=((s_+"dl", 16),))
        dlv = (s_+"dl", P.val(s_+"dl"))
        mm(lambda e: e.matmul(ps_w[0:128, 0:1], cumtri[:, :], bgc[:, 1:2],
                              start=True, stop=True), extra=[dlv, w3v])
        ev(lambda e: e.tensor_copy(bcum[:, :], ps_w[0:128, 0:1]))
        mm(lambda e: e.transpose(ps_w[0:1, 0:128], bcum[:, :], identf[:, :]))
        ev(lambda e: e.tensor_copy(bcrow[0:1, :], ps_w[0:1, 0:128]))
        mm(lambda e: e.matmul(ps_w[0:128, 0:128], onesr[0:1, :], bcrow[0:1, :],
                              start=True, stop=True))
        ev(lambda e: e.tensor_scalar(Earg[:, :], ps_w[0:128, 0:128], bcum[:, 0:1],
                                     None, ALU.subtract))
        ev(lambda e: e.tensor_scalar(Etmp[:, :], Earg[:, :], 0.0, None, ALU.max))
        sc(lambda e: e.activation(out=E_f[:, :], in_=Etmp[:, :], func=AF.Exp, scale=-1.0))
        ev(lambda e: e.tensor_scalar(Etmp[:, :], Earg[:, :], 0.0, None, ALU.min))
        sc(lambda e: e.activation(out=ET_f[:, :], in_=Etmp[:, :], func=AF.Exp))
        mm(lambda e, cq=cq: e.matmul(ps_a[:, 0:128], k_fm1[:, cq*128:(cq+1)*128],
                                     k_fm1[:, cq*128:(cq+1)*128], start=True, stop=False))
        mm(lambda e, cq=cq: e.matmul(ps_a[:, 0:128], k_fm2[:, cq*128:(cq+1)*128],
                                     k_fm2[:, cq*128:(cq+1)*128], start=False, stop=True))
        mm(lambda e, cq=cq: e.matmul(ps_b[:, 0:128], k_fm1[:, cq*128:(cq+1)*128],
                                     q_fm1[:, cq*128:(cq+1)*128], start=True, stop=False))
        mm(lambda e, cq=cq: e.matmul(ps_b[:, 0:128], k_fm2[:, cq*128:(cq+1)*128],
                                     q_fm2[:, cq*128:(cq+1)*128], start=False, stop=True))
        ev(lambda e: e.tensor_mul(Nf[:, :], E_f[:, :], ps_a[:, 0:128]))
        ev(lambda e: e.tensor_scalar(Nf[:, :], Nf[:, :], bgc[:, 0:1], None, ALU.mult))
        ev(lambda e: e.tensor_mul(Nf[:, :], Nf[:, :], negtril[:, :]))
        ev(lambda e: e.tensor_copy(A_sb[:, :], Nf[:, :]))
        mm(lambda e: e.transpose(ps_w[0:1, 0:128], bgc[:, 0:1], identf[:, :]))
        ev(lambda e: e.tensor_copy(bcr2[0:1, :], ps_w[0:1, 0:128]))
        mm(lambda e: e.matmul(ps_w[0:128, 0:128], onesr[0:1, :], bcr2[0:1, :],
                              start=True, stop=True))
        ev(lambda e: e.tensor_mul(Nf[:, :], ET_f[:, :], ps_a[:, 0:128]))
        ev(lambda e: e.tensor_mul(Nf[:, :], Nf[:, :], ps_w[0:128, 0:128]))
        ev(lambda e: e.tensor_mul(Nf[:, :], Nf[:, :], negtriu[:, :]))
        ev(lambda e: e.tensor_copy(B_sb[:, :], Nf[:, :]))
        ev(lambda e: e.tensor_add(R_sb[:, :], B_sb[:, :], identf[:, :]))
        ev(lambda e: e.tensor_mul(Nf[:, :], ET_f[:, :], ps_b[:, 0:128]))
        ev(lambda e: e.tensor_mul(Nf[:, :], Nf[:, :], triuinc[:, :]))
        ev(lambda e: e.tensor_copy(AqkT[:, :], Nf[:, :]))
        Ac, Bc, An, Bn = A_sb, B_sb, A2_sb, B2_sb
        for it in range(7):
            mm(lambda e, Ac=Ac, Bc=Bc: e.matmul(ps_c[:, 0:128], Bc[:, :], Ac[:, :],
                                                start=True, stop=True))
            ev(lambda e, An=An: e.tensor_copy(An[:, :], ps_c[:, 0:128]))
            mm(lambda e, Ac=Ac, Bc=Bc: e.matmul(ps_c[:, 0:128], Ac[:, :], Bc[:, :],
                                                start=True, stop=True))
            ev(lambda e, Bn=Bn: e.tensor_copy(Bn[:, :], ps_c[:, 0:128]))
            mm(lambda e, An=An: e.matmul(ps_c[:, 0:128], An[:, :], R_sb[:, :],
                                         start=True, stop=True))
            ev(lambda e: e.tensor_copy(Rt_bf[:, :], ps_c[:, 0:128]))
            ev(lambda e: e.tensor_add(R_sb[:, :], R_sb[:, :], Rt_bf[:, :]))
            Ac, An = An, Ac
            Bc, Bn = Bn, Bc
        ev(lambda e: e.tensor_scalar(bv_tm[:, :], vc_tm[:, :], bgc[:, 0:1], None, ALU.mult),
           extra=[dlv])
        mm(lambda e: e.matmul(ps_a[:, 0:384], R_sb[:, :], bv_tm[:, :],
                              start=True, stop=True))
        ev(lambda e: e.tensor_copy(P_tm[:, :], ps_a[:, 0:384]))
        sc(lambda e: e.activation(out=btcol[:, 0:1], in_=bcum[:, :], func=AF.Exp))
        ev(lambda e: e.tensor_mul(btcol[:, 1:2], btcol[:, 0:1], bgc[:, 0:1]))
        mm(lambda e, cq=cq: e.transpose(pts[1][:, 0:64].bitcast(BF16),
                                        k_fm1[:, cq*128:(cq+1)*128], identb[:, :]))
        ev(lambda e: e.tensor_copy(ktm[:, 0:128], pts[1][:, 0:64].bitcast(BF16)))
        mm(lambda e, cq=cq: e.transpose(pts[2][:, 0:64].bitcast(BF16),
                                        k_fm2[:, cq*128:(cq+1)*128], identb[:, :]))
        ev(lambda e: e.tensor_copy(ktm[:, 128:256], pts[2][:, 0:64].bitcast(BF16)))
        ev(lambda e: e.tensor_scalar(bw_tm[:, :], ktm[:, :], btcol[:, 1:2], None, ALU.mult))
        mm(lambda e: e.matmul(ps_b[:, 0:256], R_sb[:, :], bw_tm[:, :],
                              start=True, stop=True))
        ev(lambda e: e.tensor_copy(W_tm[:, :], ps_b[:, 0:256]))
        mm(lambda e: e.transpose(pts[1][:, 0:128], W_tm[:, 0:128], identf[:, :]))
        ev(lambda e: e.tensor_copy(Wf[:, 0:128], pts[1][:, 0:128]))
        mm(lambda e: e.transpose(pts[2][:, 0:128], W_tm[:, 128:256], identf[:, :]))
        ev(lambda e: e.tensor_copy(Wf[:, 128:256], pts[2][:, 0:128]))
        mm(lambda e: e.matmul(ps_c[:, 0:384], Wf[:, 0:128], S1[:, :],
                              start=True, stop=False))
        mm(lambda e: e.matmul(ps_c[:, 0:384], Wf[:, 128:256], S2[:, :],
                              start=False, stop=True))
        ev(lambda e: e.tensor_sub(U_bf[:, :], P_tm[:, :], ps_c[:, 0:384]))
        mm(lambda e, cq=cq: e.matmul(ps_a[:, 0:384], q_fm1[:, cq*128:(cq+1)*128], S1b[:, :],
                                     start=True, stop=False))
        mm(lambda e, cq=cq: e.matmul(ps_a[:, 0:384], q_fm2[:, cq*128:(cq+1)*128], S2b[:, :],
                                     start=False, stop=True))
        ev(lambda e: e.tensor_scalar(O_tm[:, :], ps_a[:, 0:384], btcol[:, 0:1], None, ALU.mult))
        mm(lambda e: e.matmul(ps_b[:, 0:384], AqkT[:, :], U_bf[:, :],
                              start=True, stop=True))
        ev(lambda e: e.tensor_add(O_tm[:, :], O_tm[:, :], ps_b[:, 0:384]))
        sc(lambda e: e.activation(out=osq[:, :], in_=O_tm[:, :], func=AF.Square,
                                  accum_out=ncl[:, 2:3]))
        sc(lambda e: e.activation(out=ncl[:, 3:4], in_=ncl[:, 2:3], func=AF.Sqrt,
                                  bias=epsc[:, 0:1], scale=1.0 / 384.0))
        ev(lambda e: e.reciprocal(ncl[:, 3:4], ncl[:, 3:4]))
        ev(lambda e: e.tensor_scalar(O_tm[:, :], O_tm[:, :], ncl[:, 3:4], None, ALU.mult))
        ev(lambda e: e.tensor_mul(O_tm[:, :], O_tm[:, :], gnwb[:, :]),
           extra=[(s_+"gnv", 1)])
        sc(lambda e: e.activation(out=gs[:, :], in_=gc_tm[:, :], func=AF.Silu))
        ev(lambda e: e.tensor_mul(O_tm[:, :], O_tm[:, :], gs[:, :]))
        ev(lambda e: e.tensor_copy(o3_bf[:, :], O_tm[:, :]))
        mm(lambda e: e.matmul(ps_w[0:128, 0:1], onesr[0:1, :], bcrow[0:1, 127:128],
                              start=True, stop=True))
        ev(lambda e: e.tensor_copy(btcol[:, 2:3], ps_w[0:128, 0:1]))
        sc(lambda e: e.activation(out=btcol[:, 3:4], in_=bcum[:, :], func=AF.Exp,
                                  bias=btcol[:, 2:3], scale=-1.0))
        sc(lambda e: e.activation(out=btcol[:, 2:3], in_=btcol[:, 2:3], func=AF.Exp))
        ev(lambda e: e.tensor_scalar(kp_bf[:, :], ktm[:, :], btcol[:, 3:4], None, ALU.mult))
        mm(lambda e: e.matmul(ps_c[:, 0:384], kp_bf[:, 0:128], U_bf[:, :],
                              start=True, stop=True))
        mm(lambda e: e.matmul(ps_w[:, 0:384], kp_bf[:, 128:256], U_bf[:, :],
                              start=True, stop=True))
        ev(lambda e: e.tensor_scalar(S1[:, :], S1[:, :], btcol[:, 2:3], None, ALU.mult))
        ev(lambda e: e.tensor_add(S1[:, :], S1[:, :], ps_c[:, 0:384]))
        ev(lambda e: e.tensor_scalar(S2[:, :], S2[:, :], btcol[:, 2:3], None, ALU.mult))
        ev(lambda e: e.tensor_add(S2[:, :], S2[:, :], ps_w[:, 0:384]))
        ev(lambda e: e.tensor_copy(S1b[:, :], S1[:, :]))
        ev(lambda e: e.tensor_copy(S2b[:, :], S2[:, :]))
        for t in range(3):
            mm(lambda e, t=t: e.transpose(pts[1 + t % 3][:, 0:64].bitcast(BF16),
                                          o3_bf[:, t*128:(t+1)*128], identb[:, :]))
            ev(lambda e, t=t: e.tensor_copy(ofm[:, t*128:(t+1)*128],
                                            pts[1 + t % 3][:, 0:64].bitcast(BF16)))
        for half in range(2):
            for t in range(3):
                mm(lambda e, half=half, t=t: e.matmul(
                    ps_a[:, 0:512], ofm[:, t*128:(t+1)*128],
                    wcomb[:, t*1024 + half*512: t*1024 + (half+1)*512],
                    start=(t == 0), stop=(t == 2)), extra=[(s_+"wce", P.val(s_+"wce"))])
            ev(lambda e, half=half: e.tensor_copy(ystg[:, half*512:(half+1)*512],
                                                  ps_a[:, 0:512]),
               extra=[(s_+"ysc", P.val(s_+"ysc")), yzv])
        P.rec("gpsimd", lambda e, cq=cq: e.indirect_dma_start(
            out=DR["ypart"], out_offset=IndirectOffsetOnAxis(yidx[:, cq:cq+1], 0),
            in_=ystg[:, :], in_offset=None),
            wait=[yzv, (s_+"de", P.val(s_+"de")), (s_+"w2", P.val(s_+"w2"))],
            inc=((s_+"ysc", 16),))
        P.rec("vector", lambda e: e.nop(), wait=((s_+"ysc", P.val(s_+"ysc")),),
              inc=((s_+"dcon", 1),))
    P.rec("gpsimd", lambda e: e.collective_compute(
        "AllReduce", ALU.add, replica_groups=RG8,
        ins=[DR["ypart"].opt()], outs=[DR["yall"].opt()]),
        wait=((s_+"ysc", P.val(s_+"ysc")),), inc=(("@l1y", 1),))
    return ("@l1y", P.val("@l1y"))

def emit_l1_fin(P, A, W, DR, yv):
    s_ = "a1_"
    identb = A.sb("ident_bf", [128, 128], BF16)
    pts = ptbanks(A)
    ps_a = A.ps("ps_a", [128, 512], F32)
    sdr = DR["s_dram"]
    A.push()
    wo2 = A.sb("a1_wo2", [128, NKC * 1024], BF16)
    sidx = A.sb("a1_sidx", [128, 16], I32)
    ytok = A.sb("a1_ytok", [128, 1024], F32)
    zsg = A.sb("a1_zsg", [128, NKC * 128], BF16)
    stile = A.sb("a1_stile", [128, 512], BF16)
    sstg = A.sb("a1_sstg2", [128, 2 * 128], BF16)
    for kc in range(NKC):
        P.rec("gpsimd", lambda e, kc=kc: e.dma_start(
            out=wo2[:, kc*1024:(kc+1)*1024],
            in_=W["L1_Wout2"].rearrange("(kc p) o -> kc p o", p=128)[kc]),
            inc=((s_+"w4g", 16),))
    P.rec("sync", lambda e: e.dma_start(out=sidx[:, :], in_=W["l1_sidx"]),
          inc=((s_+"w4", 16),))
    wv4 = [(s_+"w4", P.val(s_+"w4")), (s_+"w4g", P.val(s_+"w4g"))]
    for tt in range(16):
        P.rec("gpsimd", lambda e, tt=tt: e.indirect_dma_start(
            out=ytok[:, :], out_offset=None, in_=DR["yall"],
            in_offset=IndirectOffsetOnAxis(sidx[:, tt:tt+1], 0)),
            wait=[yv] + wv4 + [(s_+"fe", P.val(s_+"fe"))],
            inc=((s_+"fyl", 16),))
        for kc in range(NKC):
            P.rec("sync", lambda e, tt=tt, kc=kc: e.dma_start(
                out=zsg[:, kc*128:(kc+1)*128],
                in_=DR["zs_fm"][kc*128:(kc+1)*128, tt*128:(tt+1)*128]),
                wait=([(s_+"fm2", P.val(s_+"fm2"))] if kc == 0 else ()),
                inc=((s_+"fzl", 16),))
        for half in range(2):
            for kc in range(NKC):
                P.rec("tensor", lambda e, half=half, kc=kc: e.matmul(
                    ps_a[:, :], zsg[:, kc*128:(kc+1)*128],
                    wo2[:, kc*1024 + half*512: kc*1024 + (half+1)*512],
                    start=(kc == 0), stop=(kc == NKC - 1)),
                    wait=([(s_+"fzl", P.val(s_+"fzl")), (s_+"fyl", P.val(s_+"fyl")),
                           (s_+"fe", P.val(s_+"fe"))] if kc == 0 else ()),
                    inc=((s_+"fm2", 1),))
            P.rec("vector", lambda e, half=half: e.tensor_add(
                stile[:, :], ps_a[:, :],
                ytok[:, half*512:(half+1)*512]),
                wait=((s_+"fm2", P.val(s_+"fm2")), (s_+"fte", P.val(s_+"fte"))),
                inc=((s_+"fe", 1),))
            for st in range(4):
                n = P.val(s_+"ftt")
                sl = 1 + n % 3
                par = n % 2
                P.rec("tensor", lambda e, st=st, sl=sl: e.transpose(
                    pts[sl][:, 0:64].bitcast(BF16), stile[:, st*128:(st+1)*128],
                    identb[:, :]),
                    wait=[(s_+"fe", P.val(s_+"fe")), (s_+"fte", P.val(s_+"fte") - 2)],
                    inc=((s_+"ftt", 1),))
                P.rec("vector", lambda e, par=par, sl=sl: e.tensor_copy(
                    sstg[:, par*128:(par+1)*128], pts[sl][:, 0:64].bitcast(BF16)),
                    wait=((s_+"ftt", P.val(s_+"ftt")),
                          (s_+"fw" + str(par), P.val(s_+"fw" + str(par)))),
                    inc=((s_+"fte", 1),))
                P.rec("sync", lambda e, half=half, st=st, tt=tt, par=par: e.dma_start(
                    out=sdr[half*512 + st*128: half*512 + (st+1)*128, tt*128:(tt+1)*128],
                    in_=sstg[:, par*128:(par+1)*128]),
                    wait=((s_+"fte", P.val(s_+"fte")),),
                    inc=((s_+"fw" + str(par), 16),))
    A.pop()
    return [(s_+"fw0", P.val(s_+"fw0")), (s_+"fw1", P.val(s_+"fw1"))]

def make_inputs_l1a(I, core, x_cur):
    o = make_consts()
    o["coresel"] = np.zeros((128, 8), np.float32)
    xs = x_cur[:, 1024*core:1024*(core+1), :].reshape(TOK, D)
    o["x_in"] = np.ascontiguousarray(xs)
    b, h = core // 4, core % 4
    o["L1_Win"] = I["L1_attn_W_in"]
    cwx = I["L1_attn_conv_x_w"][:, 0, :]
    cwz = I["L1_attn_conv_z_w"][:, 0, :]
    o["L1_cwx"] = np.concatenate([cwx[o8*128:(o8+1)*128] for o8 in range(8)], axis=1)
    o["L1_cwz"] = np.concatenate([cwz[o8*128:(o8+1)*128] for o8 in range(8)], axis=1)
    o["L1_cbx"] = colsify(I["L1_attn_conv_x_b"])
    o["L1_cbz"] = colsify(I["L1_attn_conv_z_b"])
    p = np.arange(128)
    hid = np.zeros((128, 48), np.int32)
    ZR = 1024 + (p % 128)
    HCOL_ = (2, 0, 1, 5, 3, 4)
    for kc in range(8):
        for hv in range(6):
            if hv in (0, 3):
                r = core - 1
            else:
                r = core + 1
            if r < 0 or r > 7:
                hid[:, kc*6+hv] = ZR*8 + HCOL_[hv]
            else:
                hid[:, kc*6+hv] = (r*1152 + kc*128 + p)*8 + HCOL_[hv]
    o["l1_hidx"] = hid
    wq = I["L1_attn_W_q"][:, h*192:(h+1)*192]
    wk = I["L1_attn_W_k"][:, h*192:(h+1)*192]
    z64 = np.zeros((1024, 64), np.float32)
    o["L1_Wqk_my"] = np.concatenate([wq, z64, wk, z64], axis=1)
    o["L1_Wvg_my"] = np.concatenate([I["L1_attn_W_v"][:, h*384:(h+1)*384],
                                     I["L1_attn_W_g"][:, h*384:(h+1)*384]], axis=1)
    o["L1_Wbgk_my"] = np.stack([I["L1_attn_W_b"][:, h], I["L1_attn_W_gk"][:, h]], axis=1)
    o["L1_bias2"] = np.tile(np.array([[I["L1_attn_b_b"][h], I["L1_attn_dt_bias"][h]]],
                                     np.float32), (128, 1))
    o["L1_alog"] = np.full((128, 1), I["L1_attn_A_log"][h], np.float32)
    o["L1_gnw"] = I["L1_attn_g_norm_w"].reshape(1, 384)
    o["L1_WsoT_my"] = np.ascontiguousarray(I["L1_attn_W_so"][h*384:(h+1)*384, :].T)
    o["L1_Wout1"] = I["L1_attn_W_out"][0:1024]
    o["L1_Wout2"] = I["L1_attn_W_out"][1024:2048]
    ii = np.arange(128)[:, None]; jj = np.arange(128)[None, :]
    o["L1_cumtri"] = (ii <= jj).astype(np.float32)
    o["L1_negtril"] = -(jj < ii).astype(np.float32)
    o["L1_negtriu"] = -(jj > ii).astype(np.float32)
    o["L1_triuinc"] = (jj >= ii).astype(np.float32)
    t = np.arange(64*128).reshape(1, -1)
    tg = (np.arange(64)[None, :]*128 + p[:, None])
    o["l1_gidx"] = ((tg // 1024)*2048 + b*1024 + (tg % 1024)).astype(np.int32)
    o["l1_yidx"] = (b*8192 + tg).astype(np.int32)
    lt = (np.arange(16)[None, :]*128 + p[:, None])
    o["l1_sidx"] = ((lt // 1024)*8192 + core*1024 + (lt % 1024)).astype(np.int32)
    return {k2: np.ascontiguousarray(v) for k2, v in o.items()}

FULL_LAYERS = ("L0", "L1", 2, "L3")

LAYER_STAGERS = {
    "L0": ("l0a", "l0f"),
    "L1": ("l1a", "l1f"),
    2: ("l2",),
    "L3": ("l3a", "l3f"),
}

def make_inputs_layer(I, core, x_cur, L):
    fns = {"l0a": make_inputs_l0a, "l0f": make_inputs_l0f, "l1a": make_inputs_l1a,
           "l1f": make_inputs_l1f, "l2": make_inputs_l2, "l3a": make_inputs_l3a,
           "l3f": make_inputs_l3f}
    o = {}
    for nm in LAYER_STAGERS[L]:
        sub = fns[nm](I, core, x_cur)
        for k, v in sub.items():
            if k not in o:
                o[k] = v
    if L == 2:
        o["coresel"] = make_inputs_l2(I, core, x_cur)["coresel"]
    return o

def make_inputs_full(I, core):
    x_cur = I["L0_attn_inputs"].astype(np.float32)
    o = {}
    for fn in (make_inputs_l2, make_inputs_l0a, make_inputs_l0f, make_inputs_l1a,
               make_inputs_l3a, make_inputs_l1f, make_inputs_l3f):
        sub = fn(I, core, x_cur)
        for k, v in sub.items():
            if k not in o or k == "coresel" and fn is make_inputs_l2:
                o[k] = v
    o["coresel"] = make_inputs_l2(I, core, x_cur)["coresel"]
    return o


_NC = None

def kernel(**inputs):
    global _NC
    if _NC is None:
        _NC = build_program(layers=FULL_LAYERS)
    ims = [make_inputs_full(inputs, c) for c in range(8)]
    res = run_bass_kernel_spmd(_NC, ims, core_ids=list(range(8)))
    outs = []
    for c in range(8):
        xo = np.asarray(res.results[c]["x_out"], dtype=np.float32)
        outs.append(xo.reshape(2, 1024, 1024))
    return np.ascontiguousarray(np.concatenate(outs, axis=1))
